# version 03
# baseline (speedup 1.0000x reference)
import math
import contextlib
import types
import numpy as np
import ml_dtypes
import concourse.bass as bass
import concourse.mybir as mybir
from concourse.bass_utils import run_bass_kernel_spmd
from concourse.aot_env import allow_ici_all_to_all

F32 = mybir.dt.float32
BF16 = mybir.dt.bfloat16
AF = mybir.ActivationFunctionType
ALU = mybir.AluOpType
AX = mybir.AxisListType

NCORES = 8
D = 2048
H, DK, DV = 16, 128, 128
HPC = H // NCORES
BINS = 64
INTER = 5632
CH = 128
WIN = 512
NCK = WIN // CH
NHR = HPC * 2
QS = 1.0 / math.sqrt(DK)
CLAMP = 85.0
MAGIC = float(3 * 2**22)
EPS = 1e-6
NORM_EPS = 1e-5
LN_EPS = 1e-5
LOGBINS = math.log(BINS)

bfdt = ml_dtypes.bfloat16


class _FakeInstr:
    def then_inc(self, *a, **k):
        return self


_FAKE = _FakeInstr()


class _EngProxy:

    def __init__(self, ctx, name):
        object.__setattr__(self, "_ctx", ctx)
        object.__setattr__(self, "_name", name)

    def __getattr__(self, attr):
        ctx = self._ctx
        if ctx.active == self._name:
            return getattr(ctx.real, attr)

        def _noop(*a, **k):
            return _FAKE
        return _noop


class Ctx:
    CH_NAMES = ("t", "v", "s", "g", "cc")
    NDMA = 72

    def __init__(self, nc, sems, dpool):
        self.nc = nc
        self.sems = sems
        self.dpool = dpool
        self.dslot = {}
        self.dcnt = {}
        self.active = None
        self.real = None
        self.T = _EngProxy(self, "tensor")
        self.V = _EngProxy(self, "vector")
        self.S = _EngProxy(self, "scalar")
        self.G = _EngProxy(self, "gpsimd")
        self.Y = _EngProxy(self, "sync")
        self.reset()

    def reset(self):
        self.cnt = {k: 0 for k in self.CH_NAMES}
        self.dcnt = {k: 0 for k in self.dcnt}

    def mark(self, ch, instr, n=1):
        instr.then_inc(self.sems[ch], n)
        self.cnt[ch] += n
        return self.cnt[ch]

    def dma(self, eng_proxy, name, out, in_, **kw):
        if name not in self.dslot:
            assert len(self.dslot) < self.NDMA, "dma sem pool exhausted"
            self.dslot[name] = len(self.dslot)
            self.dcnt[name] = 0
        ins = eng_proxy.dma_start(out=out, in_=in_, **kw)
        ins.then_inc(self.dpool[self.dslot[name]], 16)
        self.dcnt[name] += 16
        return self.dcnt[name]

    def waitd(self, eng_proxy, name, val=None):
        if name not in self.dslot:
            return
        v = self.dcnt[name] if val is None else val
        if v > 0:
            eng_proxy.wait_ge(self.dpool[self.dslot[name]], v)

    def wait(self, eng_proxy, ch, val=None):
        v = self.cnt[ch] if val is None else val
        if v > 0:
            eng_proxy.wait_ge(self.sems[ch], v)

    def _op(self, proxy, ch, name, *a, dep=True, **k):
        if dep:
            self.wait(proxy, ch)
        ins = getattr(proxy, name)(*a, **k)
        self.mark(ch, ins)
        return self.cnt[ch]

    def v(self, name, *a, **k):
        return self._op(self.V, "v", name, *a, **k)

    def s(self, name, *a, **k):
        return self._op(self.S, "s", name, *a, **k)

    def g(self, name, *a, **k):
        return self._op(self.G, "g", name, *a, **k)


def build_program(T=16384, debug_names=()):
    TPC = T // NCORES
    NTT = TPC // 128
    NWIN = T // WIN
    NBLK = max(1, TPC // 512)
    AGROW = D + 1

    nc = bass.Bass("TRN2", target_bir_lowering=False, debug=False)
    B = types.SimpleNamespace(T=T, TPC=TPC, NTT=NTT, NWIN=NWIN, NBLK=NBLK,
                              AGROW=AGROW, nc=nc)

    def din(name, shape, dt=F32):
        setattr(B, name, nc.dram_tensor(name, list(shape), dt, kind="ExternalInput").ap())

    din("x_sl", (TPC, D))
    din("wq_s", (D, HPC * DK))
    din("wk_s", (D, HPC * DK))
    din("wv_s", (D, HPC * DV))
    din("cq_s", (HPC * DK, 4))
    din("ck_s", (HPC * DK, 4))
    din("cv_s", (HPC * DV, 4))
    din("alog_s", (1, HPC))
    din("dtb_s", (128, HPC))
    din("proxy", (DK, DV))
    din("unc", (D, BINS))
    din("he_s", (1, HPC * 4))
    din("mlp1", (128, 128))
    din("mlpb1", (32, 1))
    din("mlp2", (128, 128))
    din("mlpb2", (32, 1))
    din("gw5", (32, 5))
    din("gb5", (1, 5))
    din("g1w", (D, 128))
    din("g2w", (128, D))
    din("g2b", (128, D // 128))
    din("onorm", (DV, 1))
    din("wo", (H * DV, D))
    din("gateT", (D, 2 * INTER))
    din("downT", (INTER, D))
    din("gnorm", (1, D))
    din("dnorm", (1, INTER))
    din("identc", (128, 128), BF16)
    din("identf", (128, 128), F32)
    din("trilS", (128, 128), BF16)
    din("trilI", (128, 128), BF16)
    din("rep32", (32, 128), F32)
    din("ones1", (1, 128), BF16)
    din("ones1f", (1, 128), F32)
    din("onescf", (128, 1), F32)
    din("trilIf", (128, 128), F32)
    din("bd8m", (128, 128), mybir.dt.uint8)
    din("onescb", (128, 1), BF16)
    din("onesff", (128, 128), F32)
    din("sel96f", (128, 128), F32)

    B.y_sl = nc.dram_tensor("y_sl", [TPC, D], F32, kind="ExternalOutput").ap()
    B.dbg = {}
    for name, shape, dt in debug_names:
        B.dbg[name] = nc.dram_tensor(name, list(shape), dt, kind="ExternalOutput").ap()

    B.xn_tm_d = nc.dram_tensor("xn_tm_d", [TPC, D], F32).ap()
    B.h_d = nc.dram_tensor("h_d", [512, INTER], F32).ap()
    B.ag_in = nc.dram_tensor("ag_in", [AGROW, TPC], BF16)
    B.ag_out = nc.dram_tensor("ag_out", [NCORES * AGROW, TPC], BF16)
    B.a2a_in = nc.dram_tensor("a2a_in", [NCORES * HPC * DV, TPC], BF16)
    B.a2a_out = nc.dram_tensor("a2a_out", [NCORES * HPC * DV, TPC], BF16)
    B.tern_dn_d = nc.dram_tensor("tern_dn_d", [INTER, D], BF16).ap()
    B.tern_gt_d = nc.dram_tensor("tern_gt_d", [D, 2 * INTER], BF16).ap()
    B.gffn_d = nc.dram_tensor("gffn_d", [TPC, D], BF16).ap()
    B.sgate_d = nc.dram_tensor("sgate_d", [D, TPC], BF16).ap()

    es = contextlib.ExitStack()

    def sbuf(name, shape, dt=F32):
        setattr(B, name, es.enter_context(nc.sbuf_tensor(name, list(shape), dt)))

    def psum(name, shape, dt=F32):
        setattr(B, name, es.enter_context(nc.psum_tensor(name, list(shape), dt)))

    with es:
        sbuf("ident_t", (128, 128), BF16)
        sbuf("identf_t", (128, 128), F32)
        sbuf("trilS_t", (128, 128), BF16)
        sbuf("trilI_t", (128, 128), BF16)
        sbuf("rep32_t", (32, 128), F32)
        sbuf("ones1_t", (1, 128), BF16)
        sbuf("ones1f_t", (1, 128), F32)
        sbuf("onescf_t", (128, 1), F32)
        sbuf("trilIf_t", (128, 128), F32)
        sbuf("bd8m_t", (128, 128), mybir.dt.uint8)
        sbuf("onesff_t", (128, 128), F32)
        sbuf("sel96f_t", (128, 128), F32)
        sbuf("onescb_t", (128, 1), BF16)
        sbuf("hebc_t", (128, HPC * 4), F32)
        sbuf("wq_t", (128, 16 * HPC * DK), BF16)
        sbuf("wk_t", (128, 16 * HPC * DK), BF16)
        sbuf("wv_t", (128, 16 * HPC * DV), BF16)
        sbuf("convw", (128, 3 * HPC * 4), F32)
        sbuf("proxy_t", (128, DV), BF16)
        sbuf("unc_t", (128, 16 * BINS), BF16)
        sbuf("he_t", (1, HPC * 4), F32)
        sbuf("mlp1_t", (128, 128), BF16)
        sbuf("mlp2_t", (128, 128), BF16)
        sbuf("gaug_t", (128, 128), BF16)
        sbuf("gw5_t", (32, 5), F32)
        sbuf("baug_t", (1, 16), F32)
        sbuf("baugc_t", (128, 1), F32)
        sbuf("b1rep_t", (128, 1), F32)
        sbuf("b2rep_t", (128, 1), F32)
        sbuf("gcol_t", (128, HPC), F32)
        sbuf("ngcol_t", (128, HPC), F32)
        sbuf("gnb_t", (128, D), F32)
        sbuf("dnrow_t", (1, INTER), F32)
        sbuf("g2b_t", (128, 16), F32)
        sbuf("onorm_t", (128, 1), F32)
        sbuf("swcol_t", (128, 4), F32)
        sbuf("stat_t", (128, 4 * NTT), F32)
        sbuf("wpart_t", (128, 160), F32)
        sbuf("scl_t", (1, 16), F32)
        sbuf("mlpb_t", (32, 2), F32)
        sbuf("dtb_t", (128, HPC), F32)
        sbuf("alog_t", (1, HPC), F32)
        sbuf("arA", (128, 20 * 1024), BF16)
        sbuf("arB", (128, 11904), F32)
        sbuf("arC", (128, 14336), BF16)
        sbuf("arD", (128, 4 * 1024), F32)
        sbuf("Sf_t", (128, NHR * DV), F32)
        sbuf("Sb_t", (128, NHR * DV), BF16)
        sbuf("carry_t", (128, 9 * HPC), F32)
        sbuf("lamc_t", (128, NHR), F32)
        for i in range(5):
            psum(f"pb{i}", (128, 512), F32)
        psum("pT1a", (128, 1024), BF16)
        psum("pT1b", (128, 1024), BF16)
        psum("pT2", (128, 512), F32)

        sems = {}
        for chn in Ctx.CH_NAMES:
            sems[chn] = es.enter_context(nc.semaphore("sem_" + chn))
        dpool = [es.enter_context(nc.semaphore(f"dsem{i}")) for i in range(Ctx.NDMA)]
        block = es.enter_context(nc.Block())
        ctx = Ctx(nc, sems, dpool)

        for starter, eng_name in ((block.sync, "sync"), (block.tensor, "tensor"),
                                  (block.vector, "vector"), (block.scalar, "scalar"),
                                  (block.gpsimd, "gpsimd")):
            def _run(engine, _n=eng_name):
                ctx.active = _n
                ctx.real = engine
                ctx.reset()
                _emit(ctx, B)
                ctx.active = None
                ctx.real = None
            starter(_run)

    return nc


_PH1 = 9
_PH2 = True
_PH3 = True
_PH4 = True
_PH0HEB = True
_PH0REP = True
_PH0GAUG = True


def _emit(ctx, B):
    _ph0_init(ctx, B)
    if _PH1 >= 1:
        _ph1_ln(ctx, B)
    _ag_issue(ctx, B)
    if _PH2:
        _ph2_sgate(ctx, B)
    _wmean(ctx, B)
    if _PH3:
        _wtern(ctx, B)
        _ph3_ffn(ctx, B)
    if _PH4:
        _ph4_scan(ctx, B)
        _ph5_a2a(ctx, B)
        _ph6_epilogue(ctx, B)
    else:
        _ph_stub_out(ctx, B)


def _ag_issue(ctx, B):
    G = ctx.G
    for fs in range(4):
        ctx.waitd(G, f"fm{fs}")
    ctx.waitd(G, "agent")
    B.c_ag = ctx.mark("cc", G.collective_compute(
        "AllGather", ALU.bypass, replica_groups=[list(range(NCORES))],
        ins=[B.ag_in.ap().opt()], outs=[B.ag_out.ap().opt()]))


def _ph2_sgate(ctx, B):
    T, V, S, G, Y = ctx.T, ctx.V, ctx.S, ctx.G, ctx.Y
    g1t = B.arC[:, 8192:10240]
    g2t = B.arC[:, 10240:12288]
    ctx.dma(G, "g1w", g1t.rearrange("p (c o) -> p c o", o=128),
            B.g1w.rearrange("(c p) o -> p c o", p=128))
    ctx.dma(G, "g2w", g2t, B.g2w[:, :])
    nck = B.TPC // 512
    xfm = [B.arC[:, 0:8192], B.arC[:, 0:8192]]
    g1o = B.arC[:, 12288:13312]
    sg = B.arC[:, 13312:14336]
    for fs in range(4):
        ctx.waitd(Y, f"fm{fs}")
    ctx.waitd(Y, "agent")
    ctx.waitd(T, "g1w")
    ctx.waitd(T, "g2w")
    xf_free = 0
    v_g1 = 0
    s_done = [0, 0]
    for c in range(nck):
        sl = c % 2
        ctx.wait(Y, "t", xf_free)
        ctx.dma(Y, "sgx0", xfm[0].rearrange("p (a t) -> p a t", t=512),
                B.ag_in[0:D, c * 512:(c + 1) * 512].rearrange("(a p) t -> p a t", p=128))
        ctx.waitd(T, "sgx0")
        ctx.wait(T, "v", v_g1)
        tm = 0
        for ic in range(16):
            tm = ctx.mark("t", T.matmul(B.pb4[:, 0:512], g1t[:, ic * 128:(ic + 1) * 128],
                                        xfm[0][:, ic * 512:(ic + 1) * 512],
                                        start=(ic == 0), stop=(ic == 15)))
        xf_free = tm
        ctx.wait(V, "t", tm)
        v_g1 = ctx.v("tensor_copy", g1o[:, sl * 512:(sl + 1) * 512], B.pb4[:, 0:512])
        for oc in range(16):
            pb = [B.pb0, B.pb1][oc % 2]
            ctx.wait(T, "v", v_g1)
            ctx.wait(T, "s", s_done[oc % 2])
            tg = ctx.mark("t", T.matmul(pb[:, 0:512], g2t[:, oc * 128:(oc + 1) * 128],
                                        g1o[:, sl * 512:(sl + 1) * 512],
                                        start=True, stop=True))
            ctx.wait(S, "t", tg)
            ctx.waitd(S, f"sgs{oc % 2}")
            s_done[oc % 2] = ctx.s("activation", sg[:, (oc % 2) * 512:(oc % 2) * 512 + 512],
                                   pb[:, 0:512], AF.Sigmoid,
                                   bias=B.g2b_t[:, oc:oc + 1])
            ctx.wait(G, "s", s_done[oc % 2])
            ctx.dma(G, f"sgs{oc % 2}",
                    B.sgate_d[oc * 128:(oc + 1) * 128, c * 512:(c + 1) * 512],
                    sg[:, (oc % 2) * 512:(oc % 2) * 512 + 512])
        pass
    B.c_sgate_t = ctx.cnt["t"]


def _wmean(ctx, B):
    T, V, S, G, Y = ctx.T, ctx.V, ctx.S, ctx.G, ctx.Y
    wbuf = [B.arB[:, 6656:8704], B.arB[:, 8704:10752]]
    junk = B.arB[:, 4096:6144]
    ctx.waitd(Y, "xnst0")
    ctx.waitd(Y, "xnst1")
    ctx.wait(Y, "s", ctx.cnt["s"])
    cons = [("t", 0), ("t", 0)]
    col = 0
    bounds = []
    for src_ap, nr, ncol in ((B.gateT, D, 2 * INTER), (B.downT, INTER, D)):
        c0 = col
        for r in range(0, nr, 128):
            for cc in range(0, ncol, 2048):
                w = min(2048, ncol - cc)
                sl = col % 2
                ch, cv = cons[sl]
                ctx.wait(Y, ch, cv)
                ctx.dma(Y, f"wm{sl}", wbuf[sl][:, 0:w], src_ap[r:r + 128, cc:cc + w])
                eng = col % 2
                out = B.wpart_t[:, col:col + 1]
                if eng == 0:
                    ctx.waitd(S, f"wm{sl}")
                    m = ctx.s("activation", junk[:, 0:w], wbuf[sl][:, 0:w], AF.Abs,
                              accum_out=out)
                    cons[sl] = ("s", m)
                else:
                    ctx.waitd(V, f"wm{sl}")
                    m = ctx.v("reduce_sum", out, wbuf[sl][:, 0:w], axis=AX.X,
                              apply_absolute_value=True)
                    cons[sl] = ("v", m)
                col += 1
        bounds.append((c0, col, nr * ncol))
    ctx.wait(V, "s", ctx.cnt["s"])
    ctx.wait(V, "g", ctx.cnt["g"])
    pc = B.arD[0:128, 52:54]
    for k, (c0, c1, nel) in enumerate(bounds):
        ctx.v("reduce_sum", pc[:, k:k + 1], B.wpart_t[:, c0:c1], axis=AX.X)
    v_pc = ctx.cnt["v"]
    ctx.wait(T, "v", v_pc)
    ctx.wait(T, "t", B.c_init_t)
    tm = 0
    for k in range(2):
        tm = ctx.mark("t", T.matmul(B.pT2[0:1, 8 + k:9 + k], pc[:, k:k + 1],
                                    B.onescf_t[:, 0:1], start=True, stop=True))
    ctx.wait(V, "t", tm)
    for k, (c0, c1, nel) in enumerate(bounds):
        ctx.v("tensor_scalar", B.scl_t[0:1, 4 + k:5 + k], B.pT2[0:1, 8 + k:9 + k],
              1.0 / nel, 1e-5, ALU.mult, ALU.max)
    ctx.v("reciprocal", B.scl_t[0:1, 6:8], B.scl_t[0:1, 4:6])
    v_sw = ctx.cnt["v"]
    ctx.wait(T, "v", v_sw)
    tb = ctx.mark("t", T.matmul(B.pT2[:, 10:12], B.ones1f_t[0:1, :],
                                B.scl_t[0:1, 6:8], start=True, stop=True))
    ctx.wait(V, "t", tb)
    ctx.v("tensor_copy", B.swcol_t[:, 0:2], B.pT2[:, 10:12])
    B.c_wmean_v = ctx.cnt["v"]


def _wtern(ctx, B):
    T, V, S, G, Y = ctx.T, ctx.V, ctx.S, ctx.G, ctx.Y
    wbuf = [B.arB[:, 6656:8704], B.arB[:, 8704:10752]]
    sgn = [B.arB[:, 0:2048], B.arB[:, 2048:4096]]
    ab = [B.arB[:, 4096:6144], B.arD[:, 1024:3072]]
    tern = [B.arA[:, 12288:14336], B.arA[:, 14336:16384]]
    ctx.wait(S, "v", B.c_wmean_v)
    ctx.wait(S, "t", B.c_init_t)
    ctx.wait(Y, "s", ctx.cnt["s"])
    ctx.wait(Y, "v", ctx.cnt["v"])
    cons = [("v", 0), ("v", 0)]
    vlast = [0, 0]
    cnt = 0
    for (src_ap, dst_ap, nr, ncol, k) in ((B.gateT, B.tern_gt_d, D, 2 * INTER, 0),
                                          (B.downT, B.tern_dn_d, INTER, D, 1)):
        for r in range(0, nr, 128):
            for cc in range(0, ncol, 2048):
                w = min(2048, ncol - cc)
                sl = cnt % 2
                cnt += 1
                ch, cv = cons[sl]
                ctx.wait(Y, ch, cv)
                ctx.dma(Y, f"wt{sl}", wbuf[sl][:, 0:w], src_ap[r:r + 128, cc:cc + w])
                ctx.waitd(S, f"wt{sl}")
                ctx.wait(S, "v", vlast[sl])
                ctx.s("activation", sgn[sl][:, 0:w], wbuf[sl][:, 0:w], AF.Sign,
                      scale=B.swcol_t[:, k:k + 1])
                m_ab = ctx.s("activation", ab[sl][:, 0:w], wbuf[sl][:, 0:w], AF.Abs,
                             scale=B.swcol_t[:, k:k + 1])
                cons[sl] = ("s", m_ab)
                ctx.wait(V, "s", m_ab)
                ctx.v("tensor_scalar", ab[sl][:, 0:w], ab[sl][:, 0:w], 0.5, None, ALU.is_ge)
                ctx.waitd(V, f"wo{sl}")
                mt = ctx.v("tensor_tensor", tern[sl][:, 0:w], ab[sl][:, 0:w],
                           sgn[sl][:, 0:w], ALU.mult)
                vlast[sl] = mt
                ctx.wait(G, "v", mt)
                ctx.dma(G, f"wo{sl}", dst_ap[r:r + 128, cc:cc + w], tern[sl][:, 0:w])
    for sl in range(2):
        ctx.waitd(G, f"wo{sl}")
    B.c_wtern_done = (ctx.cnt["v"], ctx.cnt["s"])


def _ph3_ffn(ctx, B):
    T, V, S, G, Y = ctx.T, ctx.V, ctx.S, ctx.G, ctx.Y
    NBLK = B.TPC // 512
    xqfm = B.arA[:, 0:8192]
    xqtm = [B.arA[:, 16384:18432], B.arA[:, 18432:20480]]
    silug = B.arD[:, 2048:4096]
    tgt = [B.arA[:, 12288:12800], B.arA[:, 12800:13312]]
    dnt = [B.arA[:, 13312:13824], B.arA[:, 13824:14336]]
    hqtm = B.arC[:, 0:11264]
    xin = [B.arB[:, 0:2048], B.arB[:, 2048:4096]]
    rbuf = B.arB[:, 4096:6144]
    hstage = [B.arB[:, 6144:6656], B.arB[:, 6656:7168]]
    hjunk = B.arB[:, 7168:7680]
    dnocp = B.arB[:, 7680:8192]
    hdnb = [B.arB[:, 8192:8704], B.arB[:, 8704:9216]]
    hchunk = [B.arB[:, 4096:5504], B.arB[:, 5504:6912]]
    amax = B.arD[0:128, 56:57]
    stcol = B.arD[0:128, 57:58]
    c1col = B.arD[0:128, 60:64]
    c2col = B.arD[0:128, 64:65]
    sqcol = B.arD[0:128, 68:72]
    mxcol = B.arD[0:128, 72:76]
    part = B.arD[0:128, 76:77]
    rs2 = B.arD[0:128, 67:68]
    ctx.wait(V, "t", B.c_sgate_t)
    ctx.wait(S, "t", B.c_sgate_t)
    for q in range(2):
        ctx.waitd(V, f"sgs{q}")
        ctx.waitd(S, f"sgs{q}")
    ctx.wait(S, "v", B.c_wtern_done[0])
    ctx.wait(V, "s", ctx.cnt["s"])
    ctx.waitd(Y, "wo0")
    ctx.waitd(Y, "wo1")
    ctx.waitd(V, "wt0")
    ctx.waitd(V, "wt1")
    for sl in range(2):
        ctx.waitd(S, f"xnst{sl}")
    t_l1_done = 0
    vq = [0, 0]
    for b in range(NBLK):
        for tt in range(4):
            it = b * 4 + tt
            sl = tt % 2
            ctx.waitd(S, f"ffx{sl}")
            ctx.waitd(S, f"fft{sl}")
            ctx.wait(S, "v", vq[sl])
            ctx.wait(S, "t", ctx.cnt["t"])
            ctx.dma(S, f"ffx{sl}", xin[sl], B.xn_tm_d[it * 128:(it + 1) * 128, :])
            ctx.waitd(V, f"ffx{sl}")
            ctx.wait(V, "s", ctx.cnt["s"])
            rsf = B.stat_t[:, 4 * it + 2:4 * it + 3]
            ctx.v("scalar_tensor_tensor", rbuf, xin[sl], rsf, B.gnb_t[:],
                  ALU.mult, ALU.mult)
            ctx.v("tensor_reduce", amax, rbuf, AX.X, ALU.max,
                  apply_absolute_value=True)
            ctx.v("tensor_scalar", amax, amax, 1e-5, None, ALU.max)
            ctx.v("reciprocal", stcol, amax)
            ctx.v("tensor_scalar", stcol, stcol, 127.0, None, ALU.mult)
            ctx.v("tensor_tensor", c1col[:, tt:tt + 1], stcol, B.swcol_t[:, 0:1],
                  ALU.mult)
            ctx.v("reciprocal", c1col[:, tt:tt + 1], c1col[:, tt:tt + 1])
            ctx.v("tensor_scalar", rbuf, rbuf, stcol, MAGIC, ALU.mult, ALU.add)
            mq = ctx.v("tensor_scalar", xqtm[sl], rbuf, MAGIC, None, ALU.subtract)
            vq[sl] = mq
            ctx.wait(Y, "v", mq)
            ctx.wait(Y, "t", ctx.cnt["t"])
            for c in range(16):
                ctx.dma(Y, f"fft{sl}", xqfm[:, c * 512 + tt * 128:c * 512 + tt * 128 + 128],
                        xqtm[sl][:, c * 128:(c + 1) * 128], transpose=True)
        for sl in range(2):
            ctx.waitd(T, f"fft{sl}")
        ctx.v("memset", sqcol, 0.0)
        ctx.v("memset", mxcol, 0.0)
        s_silu = [0] * 4
        v_h = [0] * 4
        g_h = [0] * 4
        tg_cons = [("t", 0), ("t", 0)]
        for ocp in range(11):
            for phase in range(2):
                oc = ocp + phase * 11
                tmk = 0
                for ic in range(16):
                    sl = ic % 2
                    ch, cv = tg_cons[sl]
                    ctx.wait(Y, ch, cv)
                    ctx.dma(Y, f"tg{sl}", tgt[sl],
                            B.tern_gt_d[ic * 128:(ic + 1) * 128, oc * 512:(oc + 1) * 512])
                    ctx.waitd(T, f"tg{sl}")
                    if ic == 0:
                        if phase == 0:
                            for q in range(4):
                                ctx.wait(T, "v", v_h[q])
                        else:
                            for q in range(4):
                                ctx.wait(T, "s", s_silu[q])
                    for q in range(4):
                        tmk = ctx.mark("t", T.matmul(
                            [B.pb0, B.pb1, B.pb2, B.pb3][q][:, 0:512],
                            xqfm[:, ic * 512 + q * 128:ic * 512 + q * 128 + 128],
                            tgt[sl], start=(ic == 0), stop=(ic == 15)))
                    tg_cons[sl] = ("t", tmk)
                if phase == 0:
                    for q in range(4):
                        ctx.wait(S, "t", tmk)
                        ctx.wait(S, "v", v_h[q])
                        s_silu[q] = ctx.s("activation", silug[:, q * 512:(q + 1) * 512],
                                          [B.pb0, B.pb1, B.pb2, B.pb3][q][:, 0:512],
                                          AF.Silu, scale=c1col[:, q:q + 1])
                else:
                    ctx.wait(T, "v", ctx.cnt["v"])
                    tdn = ctx.mark("t", T.matmul(
                        B.pb4[:, 0:512], B.ones1f_t[0:1, :],
                        B.dnrow_t[0:1, ocp * 512:(ocp + 1) * 512],
                        start=True, stop=True))
                    ctx.wait(V, "t", tdn)
                    ctx.v("tensor_copy", dnocp, B.pb4[:, 0:512])
                    for q in range(4):
                        hsl = q % 2
                        ctx.wait(V, "t", tmk)
                        ctx.wait(V, "s", s_silu[q])
                        ctx.waitd(V, f"hst{hsl}")
                        ctx.v("scalar_tensor_tensor", hstage[hsl],
                              [B.pb0, B.pb1, B.pb2, B.pb3][q][:, 0:512],
                              c1col[:, q:q + 1], silug[:, q * 512:(q + 1) * 512],
                              ALU.mult, ALU.mult)
                        ctx.v("tensor_tensor", hdnb[hsl], hstage[hsl], dnocp,
                              ALU.mult)
                        ctx.wait(S, "v", ctx.cnt["v"])
                        ctx.s("activation", hjunk, hstage[hsl], AF.Square,
                              accum_out=part)
                        ctx.wait(V, "s", ctx.cnt["s"])
                        ctx.v("tensor_tensor", sqcol[:, q:q + 1], sqcol[:, q:q + 1],
                              part, ALU.add)
                        ctx.v("tensor_reduce", part, hdnb[hsl], AX.X, ALU.max,
                              apply_absolute_value=True)
                        v_h[q] = ctx.v("tensor_tensor", mxcol[:, q:q + 1],
                                       mxcol[:, q:q + 1], part, ALU.max)
                        ctx.wait(G, "v", v_h[q])
                        g_h[q] = ctx.dma(G, f"hst{hsl}",
                                         B.h_d[q * 128:(q + 1) * 128,
                                               ocp * 512:(ocp + 1) * 512],
                                         hdnb[hsl])
        t_l1_done = ctx.cnt["t"]
        ctx.waitd(S, "hst0")
        ctx.waitd(S, "hst1")
        rs2c = B.arD[0:128, 80:84]
        st2c = B.arD[0:128, 84:88]
        c2c = B.arD[0:128, 88:92]
        ctx.wait(V, "s", ctx.cnt["s"])
        for tt in range(4):
            ctx.v("tensor_scalar", rs2c[:, tt:tt + 1], sqcol[:, tt:tt + 1],
                  1.0 / INTER, NORM_EPS, ALU.mult, ALU.add)
        ctx.v("reciprocal", rs2c, rs2c)
        v_m = ctx.cnt["v"]
        ctx.wait(S, "v", v_m)
        s_rs = ctx.s("activation", rs2c, rs2c, AF.Sqrt)
        ctx.wait(V, "s", s_rs)
        ctx.v("tensor_scalar", st2c, mxcol, 1e-5, None, ALU.max)
        ctx.v("reciprocal", st2c, st2c)
        ctx.v("tensor_scalar", st2c, st2c, 127.0, None, ALU.mult)
        for tt in range(4):
            ctx.v("tensor_tensor", c2c[:, tt:tt + 1], st2c[:, tt:tt + 1],
                  B.swcol_t[:, 1:2], ALU.mult)
        ctx.v("reciprocal", c2c, c2c)
        v_c2 = ctx.v("tensor_tensor", c2c, c2c, rs2c, ALU.mult)
        qcc = B.arD[0:128, 92:96]
        ctx.v("tensor_copy", qcc, st2c)
        cw = 1408
        for half in range(2):
          tts = (2 * half, 2 * half + 1)
          for c in range(4):
              ctx.waitd(V, f"fq{c}")
          for cchunk in range(4):
            for tt in tts:
                hsl = tt % 2
                ctx.waitd(S, f"hrd{hsl}")
                ctx.wait(S, "v", ctx.cnt["v"])
                ctx.dma(S, f"hrd{hsl}", hchunk[hsl],
                        B.h_d[tt * 128:(tt + 1) * 128, cchunk * cw:(cchunk + 1) * cw])
                ctx.waitd(V, f"hrd{hsl}")
                ctx.v("tensor_scalar", hchunk[hsl], hchunk[hsl], qcc[:, tt:tt + 1],
                      MAGIC, ALU.mult, ALU.add)
                ctx.v("tensor_scalar", hqtm[:, (tt % 2) * 5632 + cchunk * cw:(tt % 2) * 5632 + (cchunk + 1) * cw],
                      hchunk[hsl], MAGIC, None, ALU.subtract)
          mq2 = ctx.cnt["v"]
          if "dbg_hq" in B.dbg and b == 0 and half == 0:
              ctx.wait(G, "v", mq2)
              ctx.dma(G, "dbghq", B.dbg["dbg_hq"][:, :], B.arC[:, 0:11264])
              ctx.waitd(Y, "dbghq")
          for tt in tts:
            it = b * 4 + tt
            ctx.wait(Y, "v", mq2)
            ctx.wait(Y, "t", ctx.cnt["t"])
            for c in range(44):
                ctx.dma(Y, f"fq{c % 4}", B.arA[:, c * 128:(c + 1) * 128],
                        hqtm[:, (tt % 2) * 5632 + c * 128:(tt % 2) * 5632 + (c + 1) * 128],
                        transpose=True)
            for c in range(4):
                ctx.waitd(T, f"fq{c}")
            dn_cons = [("t", ctx.cnt["t"]), ("t", ctx.cnt["t"])]
            ctx.wait(S, "v", ctx.cnt["v"])
            for oc in range(4):
                pb = [B.pb0, B.pb1][oc % 2]
                tmk = 0
                for ic in range(44):
                    sl = ic % 2
                    ch, cv = dn_cons[sl]
                    ctx.wait(G, ch, cv)
                    ctx.dma(G, f"dn{sl}", dnt[sl],
                            B.tern_dn_d[ic * 128:(ic + 1) * 128, oc * 512:(oc + 1) * 512])
                    ctx.waitd(T, f"dn{sl}")
                    tmk = ctx.mark("t", T.matmul(
                        pb[:, 0:512], B.arA[:, ic * 128:(ic + 1) * 128],
                        dnt[sl], start=(ic == 0), stop=(ic == 43)))
                    dn_cons[sl] = ("t", tmk)
                ctx.wait(S, "t", tmk)
                ctx.waitd(S, f"gf{oc % 2}")
                s_g = ctx.s("activation", B.arA[:, 14336 + (oc % 2) * 512:14336 + (oc % 2) * 512 + 512],
                            pb[:, 0:512], AF.Silu, scale=c2c[:, tt:tt + 1])
                ctx.wait(G, "s", s_g)
                ctx.dma(G, f"gf{oc % 2}",
                        B.gffn_d[it * 128:(it + 1) * 128, oc * 512:(oc + 1) * 512],
                        B.arA[:, 14336 + (oc % 2) * 512:14336 + (oc % 2) * 512 + 512])
        for hsl in range(2):
            ctx.waitd(G, f"hrd{hsl}")
    for oc in range(2):
        ctx.waitd(G, f"gf{oc}")
    B.c_ffn_done = (ctx.cnt["t"], ctx.cnt["v"], ctx.cnt["s"])


def _ph0_init(ctx, B):
    T, V, S, G, Y = ctx.T, ctx.V, ctx.S, ctx.G, ctx.Y

    for dst, src in ((B.ident_t, B.identc), (B.identf_t, B.identf),
                     (B.trilS_t, B.trilS), (B.trilI_t, B.trilI),
                     (B.rep32_t, B.rep32), (B.ones1_t, B.ones1),
                     (B.ones1f_t, B.ones1f), (B.onescf_t, B.onescf),
                     (B.trilIf_t, B.trilIf), (B.bd8m_t, B.bd8m),
                     (B.onescb_t, B.onescb), (B.onesff_t, B.onesff),
                     (B.sel96f_t, B.sel96f)):
        ctx.dma(Y, "ini", dst[:], src[:, :])
    ctx.dma(Y, "ini", B.gw5_t[:], B.gw5[:, :])
    ctx.dma(Y, "ini", B.baug_t[0:1, 0:5], B.gb5[:, :])
    ctx.dma(Y, "ini", B.mlpb_t[0:32, 0:1], B.mlpb1[:, :])
    ctx.dma(Y, "ini", B.mlpb_t[0:32, 1:2], B.mlpb2[:, :])
    ctx.dma(Y, "ini", B.he_t[:], B.he_s[:, :])
    ctx.dma(Y, "ini", B.alog_t[:], B.alog_s[:, :])
    ctx.dma(Y, "ini", B.dtb_t[:], B.dtb_s[:, :])
    ctx.dma(Y, "ini", B.onorm_t[:], B.onorm[:, :])
    ctx.dma(Y, "ini", B.g2b_t[0:128, 0:D // 128], B.g2b[:, :])
    c_small = ctx.dcnt["ini"]

    ctx.dma(G, "wts", B.wq_t[:].rearrange("p (c o) -> p c o", o=256), B.wq_s.rearrange("(c p) o -> p c o", p=128))
    ctx.dma(G, "wts", B.wk_t[:].rearrange("p (c o) -> p c o", o=256), B.wk_s.rearrange("(c p) o -> p c o", p=128))
    ctx.dma(G, "wts", B.wv_t[:].rearrange("p (c o) -> p c o", o=256), B.wv_s.rearrange("(c p) o -> p c o", p=128))
    ctx.dma(G, "wts", B.proxy_t[:], B.proxy[:, :])
    ctx.dma(G, "wts", B.unc_t[:].rearrange("p (c o) -> p c o", o=BINS), B.unc.rearrange("(c p) o -> p c o", p=128))
    ctx.dma(G, "wts", B.mlp1_t[:], B.mlp1[:, :])
    ctx.dma(G, "wts", B.mlp2_t[:], B.mlp2[:, :])
    for j, src in enumerate((B.cq_s, B.ck_s, B.cv_s)):
        for h in range(HPC):
            ctx.dma(G, "wts", B.convw[:, (j * HPC + h) * 4:(j * HPC + h) * 4 + 4],
                    src[h * 128:(h + 1) * 128, :])
    c_gw = ctx.dcnt["wts"]

    ctx.waitd(V, "ini", c_small)
    ctx.waitd(V, "wts", c_gw)
    sr = B.arD[0:32, 0:5]
    ctx.v("tensor_add", sr[0:32, 0:1], B.gw5_t[0:32, 0:1], B.gw5_t[0:32, 1:2])
    ctx.v("tensor_sub", sr[0:32, 1:2], B.gw5_t[0:32, 0:1], B.gw5_t[0:32, 1:2])
    ctx.v("tensor_copy", sr[0:32, 2:3], B.gw5_t[0:32, 2:3])
    ctx.v("tensor_add", sr[0:32, 3:4], B.gw5_t[0:32, 3:4], B.gw5_t[0:32, 4:5])
    ctx.v("tensor_sub", sr[0:32, 4:5], B.gw5_t[0:32, 3:4], B.gw5_t[0:32, 4:5])
    if _PH0GAUG:
        ctx.v("memset", B.gaug_t[:], 0.0)
        ctx.v("tensor_copy", B.gaug_t[0:32, 0:5], sr)
    b = B.baug_t
    ctx.v("tensor_add", b[0:1, 8:9], b[0:1, 0:1], b[0:1, 1:2])
    ctx.v("tensor_sub", b[0:1, 9:10], b[0:1, 0:1], b[0:1, 1:2])
    ctx.v("tensor_copy", b[0:1, 10:11], b[0:1, 2:3])
    ctx.v("tensor_add", b[0:1, 11:12], b[0:1, 3:4], b[0:1, 4:5])
    v_aug = ctx.v("tensor_sub", b[0:1, 12:13], b[0:1, 3:4], b[0:1, 4:5])

    ctx.wait(T, "v", v_aug)
    t1 = ctx.mark("t", T.matmul(B.pT2[0:5, 0:1], b[0:1, 8:13],
                                B.ones1f_t[0:1, 0:1], start=True, stop=True))
    ctx.v("memset", B.baugc_t[0:32, 0:1], 0.0)
    ctx.wait(V, "t", t1)
    v_bc = ctx.v("tensor_copy", B.baugc_t[0:5, 0:1], B.pT2[0:5, 0:1])
    ctx.wait(Y, "v", v_bc)
    if _PH0REP:
        for q in range(1, 4):
            ctx.dma(Y, "rep", B.gaug_t[32 * q:32 * q + 32, 32 * q:32 * q + 5],
                    B.gaug_t[0:32, 0:5])
            ctx.dma(Y, "rep", B.baugc_t[32 * q:32 * q + 5, 0:1], B.baugc_t[0:5, 0:1])
    B.c_rep = ctx.dcnt.get("rep", 0)

    ctx.wait(T, "v", v_bc)
    T.matmul(B.pT2[:, 1:2], B.rep32_t[:, :], B.mlpb_t[0:32, 0:1], start=True, stop=True)
    t2 = ctx.mark("t", T.matmul(B.pT2[:, 2:3], B.rep32_t[:, :], B.mlpb_t[0:32, 1:2],
                                start=True, stop=True))
    ctx.wait(V, "t", t2)
    ctx.v("tensor_copy", B.b1rep_t[:], B.pT2[:, 1:2])
    v_b12 = ctx.v("tensor_copy", B.b2rep_t[:], B.pT2[:, 2:3])

    ctx.waitd(S, "ini", c_small)
    sp = B.arD[0:128, 16:16 + HPC]
    ab_ = B.arD[0:128, 20:20 + HPC]
    ctx.s("activation", ab_, B.dtb_t[:], AF.Abs)
    ctx.s("activation", sp, ab_, AF.Exp, scale=-1.0)
    s_sp = ctx.s("activation", sp, sp, AF.Ln, bias=1.0)
    s_ea = ctx.s("activation", B.scl_t[0:1, 0:HPC], B.alog_t[:], AF.Exp)
    ctx.wait(V, "s", s_sp)
    ctx.v("tensor_scalar", ab_, B.dtb_t[:], 0.0, None, ALU.max)
    v_sp = ctx.v("tensor_add", sp, sp, ab_)
    ctx.wait(T, "s", s_ea)
    ctx.wait(T, "v", v_b12)
    t3 = ctx.mark("t", T.matmul(B.pT2[:, 4:4 + HPC], B.ones1f_t[0:1, :],
                                B.scl_t[0:1, 0:HPC], start=True, stop=True))
    ctx.wait(V, "t", t3)
    ctx.wait(V, "v", v_sp)
    ctx.v("scalar_tensor_tensor", B.gcol_t[:], B.pT2[:, 4:4 + HPC], -1.0, sp,
          ALU.mult, ALU.mult)
    ctx.v("tensor_scalar", B.ngcol_t[:], B.gcol_t[:], -1.0, None, ALU.mult)

    gn_row = B.arD[0:1, 1024:1024 + D]
    ctx.dma(Y, "nrm", gn_row, B.gnorm[:, :])
    ctx.dma(Y, "nrm", B.dnrow_t[:], B.dnorm[:, :])
    ctx.waitd(T, "nrm")
    for j in range(D // 512):
        ctx.wait(T, "v", ctx.cnt["v"])
        tmk = ctx.mark("t", T.matmul(B.pb0[:, :], B.ones1f_t[0:1, :],
                                     gn_row[0:1, j * 512:(j + 1) * 512],
                                     start=True, stop=True))
        ctx.wait(V, "t", tmk)
        ctx.v("tensor_copy", B.gnb_t[:, j * 512:(j + 1) * 512], B.pb0[:, :])
    if _PH0HEB:
        ctx.waitd(T, "ini", c_small)
        ctx.wait(T, "v", ctx.cnt["v"])
        t4 = ctx.mark("t", T.matmul(B.pT2[:, 12:12 + HPC * 4], B.ones1f_t[0:1, :],
                                    B.he_t[0:1, :], start=True, stop=True))
        ctx.wait(V, "t", t4)
        ctx.v("tensor_copy", B.hebc_t[:], B.pT2[:, 12:12 + HPC * 4])
    B.c_init_v = ctx.cnt["v"]
    B.c_init_t = ctx.cnt["t"]
    B.c_wts = c_gw


def _ph1_ln(ctx, B):
    T, V, S, G, Y = ctx.T, ctx.V, ctx.S, ctx.G, ctx.Y
    NTT = B.NTT

    xbuf = [B.arB[:, 0:2048], B.arB[:, 2048:4096]]
    xnbuf = [B.arA[:, 8192:10240], B.arA[:, 10240:12288]]
    xnf = [B.arB[:, 6656:8704], B.arB[:, 8704:10752]]
    sq_junk = B.arB[:, 4096:6144]
    tmpv = B.arD[0:128, 40:41]
    tmpw = B.arD[0:128, 41:42]

    v_done = [B.c_init_v, B.c_init_v]
    s_done = [0, 0]
    tr_done = [0, 0]
    xn_rd_t = [0, 0]

    for i in range(NTT):
        bsl = i % 2
        ctx.wait(S, "v", v_done[bsl])
        ctx.wait(S, "s", s_done[bsl])
        ctx.dma(S, f"x{bsl}", xbuf[bsl], B.x_sl[i * 128:(i + 1) * 128, :])
        ctx.waitd(S, f"x{bsl}")
        s_m = ctx.s("activation", sq_junk, xbuf[bsl], AF.Square,
                    accum_out=B.stat_t[:, 4 * i + 2:4 * i + 3])
        s_done[bsl] = s_m
        ctx.waitd(V, f"x{bsl}")
        ctx.v("reduce_sum", B.stat_t[:, 4 * i:4 * i + 1], xbuf[bsl], axis=AX.X)
        ctx.wait(V, "s", s_m)
        mu = B.stat_t[:, 4 * i:4 * i + 1]
        s2 = B.stat_t[:, 4 * i + 2:4 * i + 3]
        rs = B.stat_t[:, 4 * i + 1:4 * i + 2]
        rf = B.stat_t[:, 4 * i + 2:4 * i + 3]
        ctx.v("tensor_scalar", mu, mu, 1.0 / D, None, ALU.mult)
        ctx.v("tensor_scalar", tmpv, s2, 1.0 / D, None, ALU.mult)
        ctx.v("scalar_tensor_tensor", tmpw, mu, mu, tmpv, ALU.mult, ALU.subtract)
        ctx.v("tensor_scalar", tmpw, tmpw, -1.0, LN_EPS, ALU.mult, ALU.add)
        v_r = ctx.v("reciprocal", tmpv, tmpw)
        ctx.wait(S, "v", v_r)
        s_r = ctx.s("activation", rs, tmpv, AF.Sqrt)
        ctx.wait(V, "s", s_r)
        ctx.v("tensor_scalar", tmpw, tmpw, -LN_EPS, None, ALU.add)
        ctx.v("tensor_tensor", tmpw, tmpw, tmpv, ALU.mult)
        ctx.v("tensor_scalar", tmpw, tmpw, NORM_EPS, None, ALU.add)
        ctx.v("reciprocal", tmpw, tmpw)
        ctx.wait(S, "v")
        s_f = ctx.s("activation", rf, tmpw, AF.Sqrt)
        ctx.wait(V, "s", s_f)
        ctx.waitd(V, f"xnst{bsl}")
        ctx.wait(V, "t", xn_rd_t[bsl])
        ctx.v("tensor_scalar", xnf[bsl], xbuf[bsl], mu, rs,
              ALU.subtract, ALU.mult)
        v_xn = ctx.v("tensor_copy", xnbuf[bsl], xnf[bsl])
        v_done[bsl] = v_xn
        ctx.wait(G, "v", v_xn)
        ctx.waitd(G, f"xnst{bsl}")
        ctx.dma(G, f"xnst{bsl}", B.xn_tm_d[i * 128:(i + 1) * 128, :], xnf[bsl])
        for j in range(16 if _PH1 >= 2 else 0):
            jj = i * 16 + j
            bank = [B.pT1a, B.pT1b][jj % 2]
            slot = (jj // 2) % 8
            ctx.wait(T, "v", max(tr_done[jj % 2], v_xn))
            tt = ctx.mark("t", T.transpose(bank[:, slot * 128:(slot + 1) * 128],
                                           xnbuf[bsl][:, j * 128:(j + 1) * 128],
                                           B.ident_t[:]))
            fmslot = j % 4
            fmbuf = B.arC[:, fmslot * 128:(fmslot + 1) * 128]
            ctx.waitd(V, f"fm{fmslot}")
            ctx.wait(V, "t", tt)
            v_fm = ctx.v("tensor_copy", fmbuf, bank[:, slot * 128:(slot + 1) * 128])
            tr_done[jj % 2] = v_fm
            ctx.wait(G, "v", v_fm)
            ctx.dma(G, f"fm{fmslot}", B.ag_in[j * 128:(j + 1) * 128, i * 128:(i + 1) * 128], fmbuf)
        xn_rd_t[bsl] = ctx.cnt["t"]
    c_ln_v = ctx.cnt["v"]

    fmb = [B.arC[:, 512:512 + 2048], B.arC[:, 2560:2560 + 2048]]
    zt = B.arB[:, 6144:6208]
    ez = B.arB[:, 6208:6272]
    junk = B.arB[:, 6272:6336]
    zmax = B.arD[0:128, 44:45]
    sez = B.arD[0:128, 45:46]
    pz = B.arD[0:128, 46:47]
    lse = B.arD[0:128, 47:48]
    isz = B.arD[0:128, 48:49]
    for fs in range(4):
        ctx.waitd(Y, f"fm{fs}")
    eb_free = [0, 0]
    pb1_free = 0
    for i in range(NTT if _PH1 >= 3 else 0):
        eb = fmb[i % 2]
        ctx.wait(Y, "t", eb_free[i % 2])
        ctx.dma(Y, f"eb{i % 2}", eb.rearrange("p (c t) -> p c t", t=128),
                B.ag_in[0:D, i * 128:(i + 1) * 128].rearrange("(c p) t -> p c t", p=128))
        ctx.waitd(T, f"eb{i % 2}")
        ctx.wait(T, "v", pb1_free)
        tmk = 0
        for c in range(16):
            tmk = ctx.mark("t", T.matmul(B.pb1[:, 0:64], eb[:, c * 128:(c + 1) * 128],
                                         B.unc_t[:, c * 64:(c + 1) * 64],
                                         start=(c == 0), stop=(c == 15)))
        eb_free[i % 2] = tmk
        ctx.wait(V, "t", tmk)
        ctx.v("reduce_max", zmax, B.pb1[:, 0:64], axis=AX.X)
        v_z = ctx.v("tensor_scalar", zt, B.pb1[:, 0:64], zmax, None, ALU.subtract)
        pb1_free = v_z
        ctx.wait(S, "v", v_z)
        s_e = ctx.s("activation", ez, zt, AF.Exp, accum_out=sez)
        ctx.wait(V, "s", s_e)
        ctx.v("scalar_tensor_tensor", junk, ez, 1.0, zt, ALU.mult, ALU.mult,
              accum_out=pz)
        v_p = ctx.cnt["v"]
        ctx.wait(S, "v", v_p)
        s_l = ctx.s("activation", lse, sez, AF.Ln)
        ctx.wait(V, "s", s_l)
        ctx.v("reciprocal", isz, sez)
        ctx.v("tensor_tensor", pz, pz, isz, ALU.mult)
        ctx.v("tensor_sub", lse, lse, pz)
        ctx.v("tensor_scalar", B.stat_t[:, 4 * i + 3:4 * i + 4], lse,
              1.0 / LOGBINS, None, ALU.mult)
    if _PH1 < 4:
        return
    ctx.wait(T, "v", ctx.cnt["v"])
    entcols = B.stat_t[:].rearrange("p (i f) -> p i f", f=4)[:, :, 3]
    te = ctx.mark("t", T.transpose(B.pT2[0:NTT, 0:128], entcols, B.identf_t[:]))
    ctx.wait(V, "t", te)
    for fs in range(4):
        ctx.waitd(V, f"fm{fs}")
    erow = B.arC[0:NTT, 0:128]
    v_er = ctx.v("tensor_copy", erow, B.pT2[0:NTT, 0:128])
    ctx.wait(G, "v", v_er)
    ctx.dma(G, "agent", B.ag_in[D:D + 1, :].rearrange("r (i t) -> (r i) t", t=128), erow)


def _ph4_scan(ctx, B):
    T, V, S, G, Y = ctx.T, ctx.V, ctx.S, ctx.G, ctx.Y
    NWIN = B.NWIN
    TPC = B.TPC
    PB = [B.pb0, B.pb1, B.pb2, B.pb3, B.pb4]

    xnw = [B.arA[:, 0:8192], B.arA[:, 12288:20480]]
    qkv = B.arA[:, 8192:11264]
    vhat = B.arA[:, 11264:12288]
    HB = [0, 12288]
    O_M8I, O_M8O, O_S8I, O_E8O, O_LAM, O_ETL = 0, 512, 1024, 1536, 2048, 2560
    O_D16I, O_D16O, O_D32I, O_D32O, O_D64I, O_D64O = 3072, 3328, 3840, 4096, 4352, 4608
    O_QM8, O_QS8, O_QTL, O_Q16, O_Q32, O_Q64 = 4864, 5376, 5888, 6400, 6656, 6912

    vtm = [B.arC[:, 0:128], B.arC[:, 128:256]]
    CSC = 512
    def csc(ip, k):
        return B.arC[:, CSC + ip * 896 + k * 128:CSC + ip * 896 + (k + 1) * 128]
    ofm = [B.arC[:, 2304:2816], B.arC[:, 2816:3328]]
    selt = B.arC[:, 3328:3456]
    ftm = B.arC[:, 3456:3712]
    gfm = B.arC[:, 3712:3968]
    hf1 = B.arC[:, 3968:4096]
    hf2 = B.arC[:, 4096:4224]
    ffm = B.arC[:, 4224:4480]
    sqt = [B.arC[:, 4480:4992], B.arC[:, 4992:5504]]
    entt = B.arC[:, 5504:5632]

    PA = [B.arB[:, i * 520:(i + 1) * 520] for i in range(4)]
    esc = [B.arB[:, 2080:2592], B.arB[:, 2592:3104]]
    convin = [B.arB[:, 3104:3620], B.arB[:, 3620:4136]]
    convacc = [B.arB[:, 4136:4648], B.arB[:, 4648:5160]]
    aspc = B.arB[:, 5160:5672]
    arow = B.arB[:, 5672:6184]
    omx = B.arB[:, 6184:6312]
    ot1 = B.arB[:, 6312:6440]
    ot2 = B.arB[:, 6440:6568]
    krow = B.arB[:, 6568:7592]
    ojnk = B.arD[:, 2354:2482]
    gtm = B.arD[:, 2048:2304]
    nbet = B.arD[:, 2304:2320]
    fsc = B.arD[:, 2320:2336]
    sqc = B.arD[0:128, 2336:2337]
    rcl = B.arD[0:128, 2337:2338]
    ampc = B.arD[:, 2338:2354]
    ebuf = [esc[0], esc[1], convacc[0], convacc[1]]

    for e in (T, V, S, G, Y):
        ctx.wait(e, "cc", B.c_ag)
    ctx.wait(V, "t", B.c_ffn_done[0])
    ctx.wait(S, "t", B.c_ffn_done[0])
    ctx.wait(Y, "t", B.c_ffn_done[0])
    ctx.wait(T, "s", B.c_ffn_done[2])
    ctx.wait(T, "v", B.c_ffn_done[1])
    ctx.waitd(T, "wts", B.c_wts)
    ctx.waitd(V, "wts", B.c_wts)
    ctx.waitd(T, "rep", B.c_rep)
    ctx.waitd(S, "rep", B.c_rep)
    for e in (V, S, Y):
        ctx.waitd(e, "gf0")
        ctx.waitd(e, "gf1")
    ctx.wait(Y, "v", ctx.cnt["v"])
    ctx.wait(Y, "s", ctx.cnt["s"])
    ctx.wait(G, "v", ctx.cnt["v"])
    ctx.v("memset", B.Sf_t[:], 0.0)
    ctx.v("memset", B.Sb_t[:], 0.0)
    ctx.v("memset", B.carry_t[:], 0.0)
    ctx.v("memset", ftm, 0.0)
    ctx.v("memset", gfm, 0.0)
    ctx.v("memset", aspc, 0.0)
    v_init = ctx.cnt["v"]

    t_prev_end = B.c_ffn_done[0]
    v_prev_end = v_init

    for w in range(NWIN):
        wsl = w % 2
        src_core = (w * 512) // TPC
        src_col = (w * 512) % TPC
        ctx.wait(Y, "t", t_prev_end)
        ctx.wait(Y, "v", v_prev_end)
        ctx.dma(Y, f"xw{wsl}", xnw[wsl].rearrange("p (c t) -> p c t", t=512),
                B.ag_out[src_core * B.AGROW:src_core * B.AGROW + D,
                         src_col:src_col + 512].rearrange("(c p) t -> p c t", p=128))
        ctx.dma(Y, "ew", B.arC[0:4, 5504:5632],
                B.ag_out[src_core * B.AGROW + D:src_core * B.AGROW + D + 1,
                         src_col:src_col + 512].rearrange("r (c p) -> (r c) p", p=128))
        ctx.wait(T, "v", v_prev_end)
        ctx.waitd(T, f"xw{wsl}")
        s_conv = [0, 0]
        v_conv = [0, 0]
        for jt in range(6):
            j, h = jt // 2, jt % 2
            pb = PB[jt % 2]
            wt = [B.wq_t, B.wk_t, B.wv_t][j]
            ctx.wait(T, "v", v_conv[jt % 2])
            tm = 0
            for ic in range(16):
                tm = ctx.mark("t", T.matmul(pb[:, 0:512],
                                            wt[:, ic * 256 + h * 128:ic * 256 + (h + 1) * 128],
                                            xnw[wsl][:, ic * 512:(ic + 1) * 512],
                                            start=(ic == 0), stop=(ic == 15)))
            sl = jt % 2
            ctx.wait(V, "t", tm)
            ctx.wait(V, "s", s_conv[sl])
            ctx.v("tensor_copy", convin[sl][:, 3:515], pb[:, 0:512])
            ctx.v("tensor_copy", convin[sl][:, 0:3],
                  B.carry_t[:, (j * HPC + h) * 3:(j * HPC + h) * 3 + 3])
            wc = B.convw[:, (j * HPC + h) * 4:(j * HPC + h) * 4 + 4]
            ctx.v("tensor_scalar", convacc[sl], convin[sl][:, 0:512], wc[:, 0:1],
                  None, ALU.mult)
            for tau in range(1, 4):
                ctx.v("scalar_tensor_tensor", convacc[sl], convin[sl][:, tau:tau + 512],
                      wc[:, tau:tau + 1], convacc[sl], ALU.mult, ALU.add)
            v_conv[sl] = ctx.v("tensor_copy", B.carry_t[:, (j * HPC + h) * 3:(j * HPC + h) * 3 + 3],
                               convin[sl][:, 512:515])
            ctx.wait(S, "v", v_conv[sl])
            s_conv[sl] = ctx.s("activation", qkv[:, jt * 512:(jt + 1) * 512],
                               convacc[sl], AF.Silu)
        ctx.wait(T, "s", max(s_conv))
        ctx.wait(V, "s", max(s_conv))
        ctx.waitd(T, "ew")
        tent = ctx.mark("t", T.transpose(B.pT1a[:, 896:900], B.arC[0:4, 5504:5632],
                                         B.ident_t[0:4, 0:4]))
        ctx.wait(V, "t", tent)
        ctx.v("tensor_copy", entt[:, 0:4], B.pT1a[:, 896:900])
        nps = PB[2]
        t_vh = [0, 0]
        for h in range(HPC):
            kraw = qkv[:, (2 + h) * 512:(3 + h) * 512]
            vraw = qkv[:, (4 + h) * 512:(5 + h) * 512]
            tm = ctx.mark("t", T.matmul(PB[3][:, 0:512], B.proxy_t[:], kraw,
                                        start=True, stop=True))
            ctx.wait(V, "t", tm)
            ctx.v("tensor_copy", vhat[:, h * 512:(h + 1) * 512], PB[3][:, 0:512])
            krow_h = B.arB[0:1, 6568 + h * 512:6568 + (h + 1) * 512]
            ctx.wait(S, "v", ctx.cnt["v"])
            s_k2 = ctx.s("activation", sqt[0], kraw, AF.Square)
            ctx.wait(T, "s", s_k2)
            tm = ctx.mark("t", T.matmul(PB[3][0:1, 0:512],
                                        B.onescb_t[:], sqt[0], start=True, stop=True))
            ctx.wait(S, "t", tm)
            ctx.wait(S, "v", ctx.cnt["v"])
            s_sq = ctx.s("activation", krow_h, PB[3][0:1, 0:512], AF.Sqrt)
            ctx.wait(V, "s", s_sq)
            ctx.v("tensor_scalar", krow_h, krow_h, EPS, None, ALU.add)
            v_kr = ctx.v("reciprocal", krow_h, krow_h)
            ctx.wait(T, "v", v_kr)
            tm = ctx.mark("t", T.matmul(PB[3][:, 0:512], B.ones1f_t[0:1, :],
                                        krow_h, start=True, stop=True))
            ctx.wait(V, "t", tm)
            ctx.v("tensor_tensor", kraw, kraw, PB[3][:, 0:512], ALU.mult)
            ctx.wait(S, "v", ctx.cnt["v"])
            s_v2 = ctx.s("activation", sqt[0], vraw, AF.Square)
            ctx.wait(T, "s", s_v2)
            tm0 = 0
            for c in range(4):
                tm0 = ctx.mark("t", T.matmul(nps[:, c * 8 + h * 4:c * 8 + h * 4 + 1],
                                             sqt[0][:, c * 128:(c + 1) * 128],
                                             B.onescb_t[:], start=True, stop=True))
            s_h2 = ctx.s("activation", sqt[1], vhat[:, h * 512:(h + 1) * 512], AF.Square)
            ctx.wait(T, "s", s_h2)
            for c in range(4):
                tm0 = ctx.mark("t", T.matmul(nps[:, c * 8 + h * 4 + 1:c * 8 + h * 4 + 2],
                                             sqt[1][:, c * 128:(c + 1) * 128],
                                             B.onescb_t[:], start=True, stop=True))
            ctx.wait(V, "t", tm0)
            v_vv = ctx.v("tensor_tensor", sqt[0], vraw, vhat[:, h * 512:(h + 1) * 512],
                         ALU.mult)
            ctx.wait(T, "v", v_vv)
            for c in range(4):
                tm0 = ctx.mark("t", T.matmul(nps[:, c * 8 + h * 4 + 2:c * 8 + h * 4 + 3],
                                             sqt[0][:, c * 128:(c + 1) * 128],
                                             B.onescb_t[:], start=True, stop=True))
            t_vh[h] = tm0
        ctx.wait(V, "t", max(t_vh))
        nsc = B.arD[:, 2482:2514]
        ctx.v("tensor_copy", nsc, nps[:, 0:32])
        v2v = nsc.rearrange("p (c h t) -> p (c h) t", h=2, t=4)[:, :, 0]
        vh2v = nsc.rearrange("p (c h t) -> p (c h) t", h=2, t=4)[:, :, 1]
        vvhv = nsc.rearrange("p (c h t) -> p (c h) t", h=2, t=4)[:, :, 2]
        f_out = lambda f: ftm[:].rearrange("p (i q) -> p i q", q=32)[:, :, f]
        t1 = fsc[:, 0:8]
        t2 = fsc[:, 8:16]
        ctx.v("tensor_copy",
              ftm[:].rearrange("p (c h q) -> p c h q", h=2, q=32)[:, :, :, 0],
              entt[:, 0:4].unsqueeze(2).to_broadcast([128, 4, 2]))
        ctx.v("tensor_add", t1, v2v, vh2v)
        ctx.v("scalar_tensor_tensor", t1, vvhv, -2.0, t1, ALU.mult, ALU.add)
        ctx.v("tensor_scalar", t1, t1, 0.0, None, ALU.max)
        v_e2 = ctx.cnt["v"]
        ctx.wait(S, "v", v_e2)
        s_en = ctx.s("activation", t1, t1, AF.Sqrt)
        s_vn = ctx.s("activation", t2, v2v, AF.Sqrt)
        ctx.wait(V, "s", s_vn)
        ctx.v("tensor_scalar", t2, t2, EPS, None, ALU.add)
        ctx.v("reciprocal", t2, t2)
        ctx.v("tensor_tensor", f_out(1), t1, t2, ALU.mult)
        ctx.wait(S, "v", ctx.cnt["v"])
        s_l1 = ctx.s("activation", f_out(3), t1, AF.Ln, bias=1.0)
        s_vn2 = ctx.s("activation", t1, v2v, AF.Sqrt)
        s_hn = ctx.s("activation", t2, vh2v, AF.Sqrt)
        ctx.wait(V, "s", s_hn)
        ctx.v("tensor_tensor", t1, t1, t2, ALU.mult)
        ctx.v("tensor_scalar", t1, t1, EPS, None, ALU.add)
        ctx.v("reciprocal", t1, t1)
        ctx.v("tensor_tensor", f_out(2), vvhv, t1, ALU.mult)
        v_fe = ctx.v("tensor_copy",
                     ftm[:].rearrange("p (c h q) -> p c h q", h=2, q=32)[:, :, :, 4:8],
                     B.hebc_t[:, 0:8].rearrange("p (h f) -> p h f", f=4)
                         .unsqueeze(1).to_broadcast([128, 4, 2, 4]))
        s_gate = 0
        for ti in range(2):
            ctx.wait(T, "v", v_fe)
            ctx.wait(T, "s", s_gate)
            tt1 = ctx.mark("t", T.transpose(B.pT1a[:, 0:128] if ti == 0 else B.pT1b[:, 0:128],
                                            ftm[:, ti * 128:(ti + 1) * 128], B.ident_t[:]))
            ctx.wait(V, "t", tt1)
            v_ff = ctx.v("tensor_copy", ffm[:, ti * 128:(ti + 1) * 128],
                         (B.pT1a if ti == 0 else B.pT1b)[:, 0:128])
            ctx.wait(T, "v", v_ff)
            tmm = ctx.mark("t", T.matmul(PB[3][:, 0:128], B.mlp1_t[:],
                                         ffm[:, ti * 128:(ti + 1) * 128],
                                         start=True, stop=True))
            ctx.wait(S, "t", tmm)
            ctx.wait(S, "v", ctx.cnt["v"])
            s_h1 = ctx.s("activation", hf1, PB[3][:, 0:128], AF.Silu, bias=B.b1rep_t[:, 0:1])
            ctx.wait(T, "s", s_h1)
            tmm = ctx.mark("t", T.matmul(PB[4][:, 0:128], B.mlp2_t[:], hf1,
                                         start=True, stop=True))
            ctx.wait(S, "t", tmm)
            s_h2b = ctx.s("activation", hf2, PB[4][:, 0:128], AF.Silu, bias=B.b2rep_t[:, 0:1])
            ctx.wait(T, "s", s_h2b)
            tmm = ctx.mark("t", T.matmul(PB[3][:, 128:256], B.gaug_t[:], hf2,
                                         start=True, stop=True))
            for q in range(4):
                ctx.wait(S, "t", tmm)
                s_gate = ctx.s("activation", gfm[:, ti * 128:(ti + 1) * 128][32 * q:32 * q + 5, :],
                               PB[3][32 * q:32 * q + 5, 128:256], AF.Sigmoid,
                               bias=B.baugc_t[32 * q:32 * q + 5, 0:1])
        for ti in range(2):
            ctx.wait(T, "s", s_gate)
            tg = ctx.mark("t", T.transpose(B.pT1a[:, 128:256] if ti == 0 else B.pT1b[:, 128:256],
                                           gfm[:, ti * 128:(ti + 1) * 128], B.ident_t[:]))
            ctx.wait(V, "t", tg)
            ctx.v("tensor_copy", gtm[:, ti * 128:(ti + 1) * 128],
                  (B.pT1a if ti == 0 else B.pT1b)[:, 128:256])
        ctx.v("tensor_scalar", nbet.rearrange("p (i q) -> p i q", q=2),
              gtm[:].rearrange("p (i q) -> p i q", q=32)[:, :, 0:2],
              -1.0, None, ALU.mult)
        ctx.v("tensor_copy", ampc.rearrange("p (i q) -> p i q", q=2),
              gtm[:].rearrange("p (i q) -> p i q", q=32)[:, :, 3:5])
        ctx.wait(T, "v", ctx.cnt["v"])
        ta = ctx.mark("t", T.matmul(PB[3][:, 0:16], B.trilIf_t[:], ampc,
                                    start=True, stop=True))
        ctx.wait(V, "t", ta)
        for hr in range(4):
            ctx.v("tensor_copy", aspc[:, hr * 128:(hr + 1) * 128].rearrange("p (c z) -> p c z", z=32)[:, :, 0:1],
                  PB[3][:, 0:16].rearrange("p (c q) -> p c q", q=4)[:, :, hr:hr + 1])
        for hr in range(4):
            ctx.wait(T, "v", ctx.cnt["v"])
            tt2 = ctx.mark("t", T.transpose(B.pT2[:, 0:128], aspc[:, hr * 128:(hr + 1) * 128],
                                            B.identf_t[:]))
            ctx.wait(V, "t", tt2)
            ctx.v("tensor_copy", arow[:, hr * 128:(hr + 1) * 128], B.pT2[:, 0:128])
        for hr in range(4):
            ctx.v("memset", PA[hr][:, 0:520].rearrange("p (c z) -> p c z", z=130)[:, :, 0:1], 0.0)
        for hr in range(4):
            for cc in range(4):
                ctx.wait(T, "v", ctx.cnt["v"])
                rb, rk = (64, 33) if cc == 3 else (32 * cc, 1)
                lsel = B.sel96f_t if cc == 3 else B.onesff_t
                tb = ctx.mark("t", T.matmul(
                    B.pT2[:, 128:256], lsel[rb:rb + rk, :],
                    arow[rb:rb + rk, hr * 128:(hr + 1) * 128],
                    start=True, stop=True))
                ctx.wait(V, "t", tb)
                ctx.v("tensor_copy", PA[hr][:, cc * 130 + 1:cc * 130 + 129], B.pT2[:, 128:256])
        v_pa = ctx.cnt["v"]
        _scan_pair_loop(ctx, B, locals())
        t_prev_end = ctx.cnt["t"]
        v_prev_end = ctx.cnt["v"]
    B.c_scan = (ctx.cnt["t"], ctx.cnt["v"], ctx.cnt["s"], ctx.cnt["g"])


def _scan_pair_loop(ctx, B, L):
    T, V, S, G, Y = ctx.T, ctx.V, ctx.S, ctx.G, ctx.Y
    qkv = L["qkv"]
    w, wsl = L["w"], L["wsl"]
    HB, PA = L["HB"], L["PA"]
    vtm, ofm, selt = L["vtm"], L["ofm"], L["selt"]
    gtm, nbet = L["gtm"], L["nbet"]
    omx, ot1, ot2, ojnk = L["omx"], L["ot1"], L["ot2"], L["ojnk"]
    csc = L["csc"]
    sqc, rcl = L["sqc"], L["rcl"]
    ebuf = L["ebuf"]
    v_pa = L["v_pa"]
    PB = [B.pb0, B.pb1, B.pb2, B.pb3, B.pb4]
    TPC = B.TPC
    (O_M8I, O_M8O, O_S8I, O_E8O, O_LAM, O_ETL, O_D16I, O_D16O, O_D32I,
     O_D32O, O_D64I, O_D64O, O_QM8, O_QS8, O_QTL, O_Q16, O_Q32, O_Q64) = (
        L["O_M8I"], L["O_M8O"], L["O_S8I"], L["O_E8O"], L["O_LAM"], L["O_ETL"],
        L["O_D16I"], L["O_D16O"], L["O_D32I"], L["O_D32O"], L["O_D64I"],
        L["O_D64O"], L["O_QM8"], L["O_QS8"], L["O_QTL"], L["O_Q16"],
        L["O_Q32"], L["O_Q64"])
    LEVS = ((16, O_D16I, O_D16O, O_Q16), (32, O_D32I, O_D32O, O_Q32),
            (64, O_D64I, O_D64O, O_Q64))

    def slab(ip, off, width=512):
        return B.arA[:, HB[ip] + off:HB[ip] + off + width]

    est = {"i": 0, "marks": [0, 0, 0, 0]}

    def enext():
        i = est["i"]
        est["i"] = (i + 1) % 4
        if est["marks"][i]:
            ctx.wait(V, "s", est["marks"][i])
        return ebuf[i], i

    def vexp(dst, src, idx):
        ctx.wait(S, "v")
        est["marks"][idx] = ctx.s("activation", dst, src, AF.Exp)

    bst = {"n": 0, "free": [0, 0]}

    def tpose(dst, src, src_v=0):
        b = bst["n"] % 2
        bst["n"] += 1
        bk = [B.pT1a, B.pT1b][b]
        ctx.wait(T, "v", max(bst["free"][b], src_v))
        tm = ctx.mark("t", T.transpose(bk[:, 384:512], src, B.ident_t[:]))
        ctx.wait(V, "t", tm)
        vm = ctx.v("tensor_copy", dst, bk[:, 384:512])
        bst["free"][b] = vm
        return vm

    pf = [0, 0]

    for head in range(HPC):
        gc = B.gcol_t[:, head:head + 1]
        ngc = B.ngcol_t[:, head:head + 1]
        for ip in range(2):
            hr = head * 2 + ip
            pa = PA[hr]
            paz = pa[:].rearrange("p (c z) -> p c z", z=130)
            Av = paz[:, :, 1:129]
            Av4 = Av.rearrange("p c (b q) -> p c b q", q=8)
            As8 = paz[:, :, 0:121:8].unsqueeze(3).to_broadcast([128, 4, 16, 8])
            Ae8 = paz[:, :, 8:129:8].unsqueeze(3).to_broadcast([128, 4, 16, 8])
            K = qkv[:, (2 + head) * 512:(3 + head) * 512]
            Q = qkv[:, (0 + head) * 512:(1 + head) * 512]
            ctx.wait(V, "t", ctx.cnt["t"])
            e0, i0 = enext()
            e0c = e0[:].rearrange("p (c t) -> p c t", t=128)
            e0v = e0[:].rearrange("p (c b q) -> p c b q", b=16, q=8)
            ctx.v("tensor_tensor", e0v, As8, Ae8, ALU.add)
            ctx.v("scalar_tensor_tensor", e0c, e0c, -0.5, Av, ALU.mult, ALU.add)
            eA, iA = enext()
            ctx.v("tensor_scalar", eA, e0, gc, CLAMP, ALU.mult, ALU.min)
            vexp(slab(ip, O_M8I), eA, iA)
            eB, iB = enext()
            ctx.v("tensor_scalar", eB, e0, ngc, CLAMP, ALU.mult, ALU.min)
            vexp(slab(ip, O_M8O), eB, iB)
            eC, iC = enext()
            eCv = eC[:].rearrange("p (c b q) -> p c b q", b=16, q=8)
            ctx.v("tensor_tensor", eCv, Av4, As8, ALU.subtract)
            ctx.v("tensor_scalar", eC, eC, gc, None, ALU.mult)
            vexp(slab(ip, O_S8I), eC, iC)
            eD, iD = enext()
            eDv = eD[:].rearrange("p (c b q) -> p c b q", b=16, q=8)
            ctx.v("tensor_tensor", eDv, Ae8, Av4, ALU.subtract)
            ctx.v("tensor_scalar", eD, eD, gc, None, ALU.mult)
            vexp(slab(ip, O_E8O), eD, iD)
            eE, iE = enext()
            eEc = eE[:].rearrange("p (c t) -> p c t", t=128)
            ctx.v("tensor_scalar", eEc, Av, gc, None, ALU.mult)
            vexp(slab(ip, O_LAM), eE, iE)
            A127 = paz[:, :, 128:129].to_broadcast([128, 4, 128])
            eF, iF = enext()
            eFc = eF[:].rearrange("p (c t) -> p c t", t=128)
            ctx.v("tensor_tensor", eFc, A127, Av, ALU.subtract)
            ctx.v("tensor_scalar", eF, eF, gc, None, ALU.mult)
            vexp(slab(ip, O_ETL), eF, iF)
            for (LV, oin, oout, oq) in LEVS:
                nb = 128 // (2 * LV)
                cwid = nb * LV
                eg, ig = enext()
                egv = eg[:].rearrange("p (c m q) -> p c m q", m=nb, q=2 * LV)
                Bnd = paz[:, :, LV:LV + 2 * LV * (nb - 1) + 1:2 * LV]\
                    .unsqueeze(3).to_broadcast([128, 4, nb, 2 * LV])
                AvL = Av.rearrange("p c (m q) -> p c m q", q=2 * LV)
                ctx.v("tensor_tensor", egv, AvL, Bnd, ALU.subtract)
                eh, ih = enext()
                ctx.v("tensor_scalar", eh, eg, gc, CLAMP, ALU.mult, ALU.min)
                ehv = eh[:].rearrange("p (c m q) -> p c m q", m=nb, q=2 * LV)
                vexp(slab(ip, oin, 4 * cwid).rearrange("p (c m q) -> p c m q",
                                                       m=nb, q=LV),
                     ehv[:, :, :, LV:2 * LV], ih)
                ei_, ii_ = enext()
                ctx.v("tensor_scalar", ei_, eg, ngc, CLAMP, ALU.mult, ALU.min)
                eiv = ei_[:].rearrange("p (c m q) -> p c m q", m=nb, q=2 * LV)
                if LV == 16:
                    vexp(slab(ip, oout, 512), ei_, ii_)
                else:
                    vexp(slab(ip, oout, 4 * cwid).rearrange("p (c m q) -> p c m q",
                                                            m=nb, q=LV),
                         eiv[:, :, :, 0:LV], ii_)
            ctx.wait(V, "s")
            ctx.v("scalar_tensor_tensor", slab(ip, O_QM8), Q, QS,
                  slab(ip, O_M8I), ALU.mult, ALU.mult)
            ctx.v("scalar_tensor_tensor", slab(ip, O_QS8), Q, QS,
                  slab(ip, O_S8I), ALU.mult, ALU.mult)
            ctx.v("scalar_tensor_tensor", slab(ip, O_QTL), Q, QS,
                  slab(ip, O_LAM), ALU.mult, ALU.mult)
            for (LV, oin, oout, oq) in LEVS:
                nb = 128 // (2 * LV)
                cwid = nb * LV
                Qh = Q[:].rearrange("p (c m q) -> p c m q", m=nb, q=2 * LV)[:, :, :, LV:2 * LV]
                ctx.v("scalar_tensor_tensor",
                      slab(ip, oq, 4 * cwid).rearrange("p (c m q) -> p c m q", m=nb, q=LV),
                      Qh, QS,
                      slab(ip, oin, 4 * cwid).rearrange("p (c m q) -> p c m q", m=nb, q=LV),
                      ALU.mult, ALU.mult)
            for off in (O_M8I, O_M8O, O_S8I, O_E8O, O_LAM, O_ETL):
                ctx.v("tensor_tensor", slab(ip, off), K, slab(ip, off), ALU.mult)
            for (LV, oin, oout, oq) in LEVS:
                nb = 128 // (2 * LV)
                cwid = nb * LV
                Kin = K[:].rearrange("p (c m q) -> p c m q", m=nb, q=2 * LV)[:, :, :, LV:2 * LV]
                Kout = K[:].rearrange("p (c m q) -> p c m q", m=nb, q=2 * LV)[:, :, :, 0:LV]
                ctx.v("tensor_tensor",
                      slab(ip, oin, 4 * cwid).rearrange("p (c m q) -> p c m q", m=nb, q=LV),
                      Kin,
                      slab(ip, oin, 4 * cwid).rearrange("p (c m q) -> p c m q", m=nb, q=LV),
                      ALU.mult)
                if LV == 16:
                    ctx.v("tensor_tensor", slab(ip, oout, 512), K,
                          slab(ip, oout, 512), ALU.mult)
                else:
                    ctx.v("tensor_tensor",
                          slab(ip, oout, 4 * cwid).rearrange("p (c m q) -> p c m q", m=nb, q=LV),
                          Kout,
                          slab(ip, oout, 4 * cwid).rearrange("p (c m q) -> p c m q", m=nb, q=LV),
                          ALU.mult)

        vb = ctx.cnt["v"]

        for cc in range(4):
            csl = cc % 2
            sl = slice(cc * 128, (cc + 1) * 128)
            v_vtm = tpose(vtm[csl], qkv[:, (4 + head) * 512 + cc * 128:
                                        (4 + head) * 512 + (cc + 1) * 128])
            tmk = {}
            for ip in range(2):
                pbp = PB[ip]
                ctx.wait(T, "v", max(pf[ip], vb))
                ctx.mark("t", T.matmul(pbp[:, 0:128], slab(ip, O_E8O)[:, sl],
                                       slab(ip, O_S8I)[:, sl], start=True, stop=True))
                ctx.mark("t", T.matmul(pbp[:, 256:384], slab(ip, O_E8O)[:, sl],
                                       slab(ip, O_QS8)[:, sl], start=True, stop=True))
                ctx.mark("t", T.matmul(pbp[:, 128:256], slab(ip, O_M8O)[:, sl],
                                       slab(ip, O_M8I)[:, sl], start=True, stop=True))
                ctx.mark("t", T.matmul(pbp[:, 384:512], slab(ip, O_M8O)[:, sl],
                                       slab(ip, O_QM8)[:, sl], start=True, stop=True))
                for (LV, oin, oout, oq) in LEVS:
                    nb = 128 // (2 * LV)
                    cwid = nb * LV
                    for m in range(nb):
                        j0 = 2 * LV * m
                        i0 = j0 + LV
                        cb = cc * cwid + m * LV
                        if LV == 16:
                            jb = 64 if m == 3 else j0
                            lt = slab(ip, oout, 512)[:, cc * 128 + jb:cc * 128 + j0 + LV]
                        else:
                            jb = j0
                            lt = slab(ip, oout, 4 * cwid)[:, cb:cb + LV]
                        ctx.mark("t", T.matmul(
                            pbp[jb:j0 + LV, i0:i0 + LV], lt,
                            slab(ip, oin, 4 * cwid)[:, cb:cb + LV],
                            start=True, stop=True))
                        tmk[ip] = ctx.mark("t", T.matmul(
                            pbp[jb:j0 + LV, 256 + i0:256 + i0 + LV], lt,
                            slab(ip, oq, 4 * cwid)[:, cb:cb + LV],
                            start=True, stop=True))
            for ip in range(2):
                hr = head * 2 + ip
                pbp = PB[ip]
                nb_col = nbet[:, cc * 4 + hr:cc * 4 + hr + 1]
                b_col = gtm[:, (cc * 2 + head) * 32 + ip:(cc * 2 + head) * 32 + ip + 1]
                ctx.wait(V, "t", tmk[ip])
                ctx.v("tensor_copy", selt, pbp[:, 0:128])
                ctx.v("copy_predicated", selt, B.bd8m_t[:], pbp[:, 128:256])
                ctx.v("scalar_tensor_tensor", csc(ip, 0), selt, nb_col,
                      B.trilS_t[:], ALU.mult, ALU.mult)
                ctx.v("tensor_copy", selt, pbp[:, 256:384])
                ctx.v("copy_predicated", selt, B.bd8m_t[:], pbp[:, 384:512])
                pf[ip] = ctx.v("scalar_tensor_tensor", csc(ip, 1), selt, b_col,
                               B.trilI_t[:], ALU.mult, ALU.mult)
            tR = {}
            for ip in range(2):
                hr = head * 2 + ip
                ctx.wait(T, "v", pf[ip])
                tR[ip] = ctx.mark("t", T.matmul(
                    PB[3][:, 256 + ip * 128:384 + ip * 128],
                    slab(ip, O_LAM)[:, sl], B.Sb_t[:, hr * 128:(hr + 1) * 128],
                    start=True, stop=True))
            for ip in range(2):
                tpose(csc(ip, 2), csc(ip, 0))
                ctx.v("tensor_copy", csc(ip, 3), csc(ip, 0))
                ctx.wait(V, "t", tR[ip])
                ctx.v("tensor_tensor", csc(ip, 4), vtm[csl],
                      PB[3][:, 256 + ip * 128:384 + ip * 128], ALU.subtract)
            for k in range(7):
                ctx.wait(T, "v", ctx.cnt["v"])
                tap = {}
                for ip in range(2):
                    tap[ip] = ctx.mark("t", T.matmul(
                        PB[3][:, ip * 128:(ip + 1) * 128], csc(ip, 3), csc(ip, 4),
                        start=True, stop=True))
                tsq = {}
                if k < 6:
                    for ip in range(2):
                        ctx.mark("t", T.matmul(
                            PB[2][:, ip * 256:ip * 256 + 128], csc(ip, 3), csc(ip, 2),
                            start=True, stop=True))
                        tsq[ip] = ctx.mark("t", T.matmul(
                            PB[2][:, ip * 256 + 128:ip * 256 + 256], csc(ip, 2),
                            csc(ip, 3), start=True, stop=True))
                for ip in range(2):
                    ctx.wait(V, "t", tap[ip])
                    ctx.v("tensor_tensor", csc(ip, 4), csc(ip, 4),
                          PB[3][:, ip * 128:(ip + 1) * 128], ALU.add)
                if k < 6:
                    for ip in range(2):
                        ctx.wait(V, "t", tsq[ip])
                        ctx.v("tensor_copy", csc(ip, 2), PB[2][:, ip * 256:ip * 256 + 128])
                        ctx.v("tensor_copy", csc(ip, 3),
                              PB[2][:, ip * 256 + 128:ip * 256 + 256])
            for ip in range(2):
                hr = head * 2 + ip
                b_col = gtm[:, (cc * 2 + head) * 32 + ip:(cc * 2 + head) * 32 + ip + 1]
                ctx.wait(T, "v", ctx.cnt["v"])
                ctx.mark("t", T.matmul(PB[4][:, ip * 128:(ip + 1) * 128],
                                       slab(ip, O_QTL)[:, sl],
                                       B.Sb_t[:, hr * 128:(hr + 1) * 128],
                                       start=True, stop=False))
                tgu = ctx.mark("t", T.matmul(PB[4][:, ip * 128:(ip + 1) * 128],
                                             csc(ip, 1), csc(ip, 4),
                                             start=False, stop=True))
                ctx.v("tensor_scalar", csc(ip, 5), csc(ip, 4), b_col, None, ALU.mult)
                tpose(csc(ip, 6), slab(ip, O_ETL)[:, sl])
                ctx.wait(T, "v", ctx.cnt["v"])
                tsu = ctx.mark("t", T.matmul(PB[4][:, 256 + ip * 128:384 + ip * 128],
                                             csc(ip, 6), csc(ip, 5),
                                             start=True, stop=True))
                ctx.wait(S, "v", v_pa)
                slc = ctx.s("activation", B.lamc_t[:, hr:hr + 1],
                            PA[hr][:, cc * 130 + 128:cc * 130 + 129], AF.Exp,
                            scale=gc)
                ctx.wait(V, "s", slc)
                ctx.wait(V, "t", tsu)
                ctx.v("scalar_tensor_tensor", B.Sf_t[:, hr * 128:(hr + 1) * 128],
                      B.Sf_t[:, hr * 128:(hr + 1) * 128], B.lamc_t[:, hr:hr + 1],
                      PB[4][:, 256 + ip * 128:384 + ip * 128], ALU.mult, ALU.add)
                ctx.v("tensor_copy", B.Sb_t[:, hr * 128:(hr + 1) * 128],
                      B.Sf_t[:, hr * 128:(hr + 1) * 128])
            lam_col = gtm[:, (cc * 2 + head) * 32 + 2:(cc * 2 + head) * 32 + 3]
            ctx.wait(V, "t", ctx.cnt["t"])
            ctx.v("tensor_scalar", ot1, PB[4][:, 0:128], lam_col, None, ALU.mult)
            ctx.v("tensor_copy", ot2, PB[4][:, 128:256])
            ctx.v("scalar_tensor_tensor", omx, ot2, lam_col, ot2,
                  ALU.mult, ALU.subtract)
            ctx.v("tensor_sub", omx, ot1, omx)
            ctx.wait(S, "v")
            ssq = ctx.s("activation", ojnk[:, 0:128], omx, AF.Square, accum_out=sqc)
            ctx.wait(V, "s", ssq)
            ctx.v("tensor_scalar", rcl, sqc, 1.0 / DV, NORM_EPS, ALU.mult, ALU.add)
            ctx.v("reciprocal", rcl, rcl)
            ctx.wait(S, "v")
            src2 = ctx.s("activation", rcl, rcl, AF.Sqrt)
            ctx.wait(V, "s", src2)
            v_on = ctx.v("tensor_scalar", selt, omx, rcl, None, ALU.mult)
            ctx.waitd(V, f"oa{head}")
            tpose(ofm[head][:, cc * 128:(cc + 1) * 128], selt, src_v=v_on)
        d = (w * 512) // TPC
        ctx.wait(G, "v", ctx.cnt["v"])
        ctx.dma(G, f"oa{head}",
                B.a2a_in[d * (HPC * DV) + head * 128:d * (HPC * DV) + (head + 1) * 128,
                         w * 512 - d * TPC:w * 512 - d * TPC + 512],
                ofm[head])


def _ph5_a2a(ctx, B):
    G = ctx.G
    for head in range(HPC):
        ctx.waitd(G, f"oa{head}")
    with allow_ici_all_to_all():
        B.c_a2a = ctx.mark("cc", G.collective_compute(
            "AllToAll", ALU.bypass, replica_groups=[list(range(NCORES))],
            ins=[B.a2a_in.ap().opt()], outs=[B.a2a_out.ap().opt()]))


def _ph6_epilogue(ctx, B):
    T, V, S, G, Y = ctx.T, ctx.V, ctx.S, ctx.G, ctx.Y
    NTT = B.NTT
    woc = B.arC[:, 0:8192]
    gfb = [B.arC[:, 8192:8704], B.arC[:, 8704:9216]]
    sgt = [B.arC[:, 9216:11264], B.arC[:, 11264:13312]]
    xb = [B.arB[:, 0:512], B.arB[:, 512:1024]]
    yt = B.arB[:, 1024:1536]
    yst = [B.arB[:, 1536:2048], B.arB[:, 2048:2560]]

    for e in (T, V, S, G, Y):
        ctx.wait(e, "cc", B.c_a2a)
    ctx.wait(Y, "t", B.c_scan[0])
    ctx.wait(Y, "v", B.c_scan[1])
    ctx.wait(Y, "s", B.c_scan[2])
    ctx.wait(T, "v", B.c_scan[1])
    ctx.wait(G, "t", B.c_scan[0])
    ctx.wait(G, "v", B.c_scan[1])
    for q in range(2):
        ctx.waitd(Y, f"sgs{q}")
        ctx.waitd(Y, f"gf{q}")

    v_of = 0
    t_half = 0
    pbfree = [0, 0]
    for h0 in range(0, NTT, 8):
        group = list(range(h0, min(h0 + 8, NTT)))
        for it in group:
            j = it - h0
            ofs = B.arA[:, j * 2048:(j + 1) * 2048]
            ctx.wait(Y, "t", t_half)
            ctx.wait(Y, "v", v_of)
            ctx.waitd(Y, f"ao{j % 2}")
            ctx.dma(Y, f"ao{j % 2}", ofs.rearrange("p (c t) -> p c t", t=128),
                    B.a2a_out[0:H * DV, it * 128:(it + 1) * 128]
                    .rearrange("(c p) t -> p c t", p=128))
            ctx.waitd(Y, f"so{j % 2}")
            ctx.dma(Y, f"so{j % 2}", sgt[j % 2].rearrange("p (c t) -> p c t", t=128),
                    B.sgate_d[0:D, it * 128:(it + 1) * 128]
                    .rearrange("(c p) t -> p c t", p=128))
            ctx.waitd(V, f"ao{j % 2}")
            ctx.waitd(V, f"so{j % 2}")
            ctx.wait(V, "t", t_half)
            v_of = ctx.v("scalar_tensor_tensor", ofs, ofs, B.onorm_t[:, 0:1],
                         sgt[j % 2], ALU.mult, ALU.mult)
        for oc in range(4):
            for q in range(4):
                ctx.waitd(G, f"wo{q % 2}")
                ctx.wait(G, "t", t_half)
                ctx.dma(G, f"wo{q % 2}",
                        woc[:, q * 2048:(q + 1) * 2048].rearrange("p (c o) -> p c o", o=512),
                        B.wo[q * 512:(q + 1) * 512, oc * 512:(oc + 1) * 512]
                        .rearrange("(c p) o -> p c o", p=128))
            for q in range(4):
                ctx.waitd(T, f"wo{q % 2}")
            ctx.wait(T, "v", v_of)
            for it in group:
                j = it - h0
                pb = [B.pb0, B.pb1][j % 2]
                ctx.wait(T, "v", pbfree[j % 2])
                tmk = 0
                for ch in range(16):
                    tmk = ctx.mark("t", T.matmul(
                        pb[:, 0:512], B.arA[:, (j * 16 + ch) * 128:(j * 16 + ch + 1) * 128],
                        woc[:, ch * 512:(ch + 1) * 512],
                        start=(ch == 0), stop=(ch == 15)))
                ctx.waitd(S, f"x{j % 2}e")
                ctx.waitd(S, f"gf{j % 2}e")
                ctx.wait(S, "v", pbfree[j % 2])
                ctx.dma(S, f"x{j % 2}e", xb[j % 2],
                        B.x_sl[it * 128:(it + 1) * 128, oc * 512:(oc + 1) * 512])
                ctx.dma(S, f"gf{j % 2}e", gfb[j % 2],
                        B.gffn_d[it * 128:(it + 1) * 128, oc * 512:(oc + 1) * 512])
                ctx.wait(V, "t", tmk)
                ctx.waitd(V, f"x{j % 2}e")
                ctx.waitd(V, f"gf{j % 2}e")
                ctx.waitd(V, f"ys{j % 2}")
                ctx.v("tensor_tensor", yt, pb[:, 0:512], gfb[j % 2], ALU.mult)
                pbfree[j % 2] = ctx.v("tensor_tensor", yst[j % 2], yt, xb[j % 2],
                                      ALU.add)
                ctx.wait(G, "v", pbfree[j % 2])
                ctx.dma(G, f"ys{j % 2}",
                        B.y_sl[it * 128:(it + 1) * 128, oc * 512:(oc + 1) * 512],
                        yst[j % 2])
        t_half = ctx.cnt["t"]
    ctx.waitd(G, "ys0")
    ctx.waitd(G, "ys1")
    ctx.waitd(Y, "ys0")
    ctx.waitd(Y, "ys1")


def _ph_stub_out(ctx, B):
    T, V, S, G, Y = ctx.T, ctx.V, ctx.S, ctx.G, ctx.Y
    for fs in range(4):
        ctx.waitd(G, f"fm{fs}")
    ctx.waitd(G, "agent")
    cc = ctx.mark("cc", G.collective_compute(
        "AllGather", ALU.bypass, replica_groups=[list(range(NCORES))],
        ins=[B.ag_in.ap().opt()], outs=[B.ag_out.ap().opt()]))
    ctx.wait(G, "cc", cc)
    ctx.wait(Y, "v", ctx.cnt["v"])
    ctx.wait(Y, "s", ctx.cnt["s"])
    for i in range(B.NTT):
        buf = B.arB[:, 0:2048]
        ctx.waitd(Y, "sty")
        ctx.dma(Y, "stx", buf, B.x_sl[i * 128:(i + 1) * 128, :])
        ctx.waitd(G, "stx")
        ctx.dma(G, "sty", B.y_sl[i * 128:(i + 1) * 128, :], buf)
    ctx.waitd(G, "sty")


_PROG_CACHE = {}


def _get_prog(T):
    if T not in _PROG_CACHE:
        _PROG_CACHE[T] = build_program(T)
    return _PROG_CACHE[T]


def _consts():
    i128b = np.eye(128, dtype=bfdt)
    i128f = np.eye(128, dtype=np.float32)
    jj, ii = np.meshgrid(np.arange(128), np.arange(128), indexing="ij")
    trilS = (jj < ii).astype(bfdt)
    trilI = (jj <= ii).astype(bfdt)
    trilIf = (jj <= ii).astype(np.float32)
    bd8m = ((jj // 8 == ii // 8) & (ii >= jj)).astype(np.uint8)
    rep32 = np.zeros((32, 128), dtype=np.float32)
    for k in range(4):
        rep32[np.arange(32), 32 * k + np.arange(32)] = 1
    return {
        "identc": i128b, "identf": i128f, "trilS": trilS, "trilI": trilI,
        "rep32": rep32, "ones1": np.ones((1, 128), dtype=bfdt),
        "ones1f": np.ones((1, 128), dtype=np.float32),
        "onescf": np.ones((128, 1), dtype=np.float32),
        "trilIf": trilIf, "bd8m": bd8m,
        "onescb": np.ones((128, 1), dtype=bfdt),
        "onesff": np.ones((128, 128), dtype=np.float32),
        "sel96f": (np.arange(128)[:, None] == 96).astype(np.float32) * np.ones((1, 128), np.float32),
    }


def _blockdiag(wm, n):
    r, c = wm.shape
    out = np.zeros((128, 128), dtype=wm.dtype)
    for q in range(n):
        out[32 * q:32 * q + r, 32 * q:32 * q + c] = wm
    return out


def _in_maps(g, T, i):
    TPC = T // NCORES
    hs = slice(i * HPC * DK, (i + 1) * HPC * DK)
    m = {
        "x_sl": np.ascontiguousarray(g["L0_attn_hidden_states"][0, i * TPC:(i + 1) * TPC]),
        "wq_s": np.ascontiguousarray(g["L0_attn_wq"][:, hs]),
        "wk_s": np.ascontiguousarray(g["L0_attn_wk"][:, hs]),
        "wv_s": np.ascontiguousarray(g["L0_attn_wv"][:, hs]),
        "cq_s": np.ascontiguousarray(g["L0_attn_conv_q_w"][hs]),
        "ck_s": np.ascontiguousarray(g["L0_attn_conv_k_w"][hs]),
        "cv_s": np.ascontiguousarray(g["L0_attn_conv_v_w"][hs]),
        "alog_s": np.ascontiguousarray(g["L0_attn_A_log"][None, i * HPC:(i + 1) * HPC]),
        "dtb_s": np.ascontiguousarray(g["L0_attn_dt_bias"][hs].reshape(HPC, 128).T),
        "proxy": g["L0_attn_proxy_w"],
        "unc": g["L0_attn_unc_w"],
        "he_s": np.ascontiguousarray(
            g["L0_attn_head_embed"][i * HPC:(i + 1) * HPC].reshape(1, HPC * 4)),
        "mlp1": _blockdiag(g["L0_attn_mlp_w1"], 4),
        "mlpb1": np.ascontiguousarray(g["L0_attn_mlp_b1"][:, None]),
        "mlp2": _blockdiag(g["L0_attn_mlp_w2"], 4),
        "mlpb2": np.ascontiguousarray(g["L0_attn_mlp_b2"][:, None]),
        "gw5": np.ascontiguousarray(np.stack(
            [g["L0_attn_bb_w"], g["L0_attn_bd_w"], g["L0_attn_lam_w"],
             g["L0_attn_ab_w"], g["L0_attn_ad_w"]], axis=1)),
        "gb5": np.ascontiguousarray(np.concatenate(
            [g["L0_attn_bb_b"], g["L0_attn_bd_b"], g["L0_attn_lam_b"],
             g["L0_attn_ab_b"], g["L0_attn_ad_b"]])[None, :]),
        "g1w": g["L0_attn_g1_w"],
        "g2w": g["L0_attn_g2_w"],
        "g2b": np.ascontiguousarray(g["L0_attn_g2_b"].reshape(-1, 128).T),
        "onorm": np.ascontiguousarray(g["L0_attn_onorm_w"][:, None]),
        "wo": g["L0_attn_wo"],
        "gateT": np.ascontiguousarray(g["L0_ffn_gate_w"].T),
        "downT": np.ascontiguousarray(g["L0_ffn_down_w"].T),
        "gnorm": np.ascontiguousarray(g["L0_ffn_gate_norm_w"][None, :]),
        "dnorm": np.ascontiguousarray(g["L0_ffn_down_norm_w"][None, :]),
    }
    m.update(_consts())
    return m


def kernel(**inputs):
    g = {k: np.asarray(v) for k, v in inputs.items()}
    T = g["L0_attn_hidden_states"].shape[1]
    nc = _get_prog(T)
    in_maps = [_in_maps(g, T, i) for i in range(NCORES)]
    res = run_bass_kernel_spmd(nc, in_maps, core_ids=list(range(NCORES)))
    out = np.concatenate([res.results[i]["y_sl"] for i in range(NCORES)], axis=0)
    return out[None].astype(np.float32)


# version 05
# speedup vs baseline: 1.2260x; 1.2260x over previous
import math
import contextlib
import types
import numpy as np
import ml_dtypes
import concourse.bass as bass
import concourse.mybir as mybir
from concourse.bass_utils import run_bass_kernel_spmd
from concourse.aot_env import allow_ici_all_to_all

F32 = mybir.dt.float32
BF16 = mybir.dt.bfloat16
AF = mybir.ActivationFunctionType
ALU = mybir.AluOpType
AX = mybir.AxisListType

NCORES = 8
D = 2048
H, DK, DV = 16, 128, 128
HPC = H // NCORES
BINS = 64
INTER = 5632
CH = 128
WIN = 512
NCK = WIN // CH
NHR = HPC * 2
QS = 1.0 / math.sqrt(DK)
CLAMP = 85.0
MAGIC = float(3 * 2**22)
EPS = 1e-6
NORM_EPS = 1e-5
LN_EPS = 1e-5
LOGBINS = math.log(BINS)

bfdt = ml_dtypes.bfloat16


class _FakeInstr:
    def then_inc(self, *a, **k):
        return self


_FAKE = _FakeInstr()


class _EngProxy:

    def __init__(self, ctx, name):
        object.__setattr__(self, "_ctx", ctx)
        object.__setattr__(self, "_name", name)

    def __getattr__(self, attr):
        ctx = self._ctx
        if ctx.active == self._name:
            return getattr(ctx.real, attr)

        def _noop(*a, **k):
            return _FAKE
        return _noop


class Ctx:
    CH_NAMES = ("t", "v", "s", "g", "cc")
    NDMA = 72

    def __init__(self, nc, sems, dpool):
        self.nc = nc
        self.sems = sems
        self.dpool = dpool
        self.dslot = {}
        self.dcnt = {}
        self.active = None
        self.real = None
        self.T = _EngProxy(self, "tensor")
        self.V = _EngProxy(self, "vector")
        self.S = _EngProxy(self, "scalar")
        self.G = _EngProxy(self, "gpsimd")
        self.Y = _EngProxy(self, "sync")
        self.reset()

    def reset(self):
        self.cnt = {k: 0 for k in self.CH_NAMES}
        self.dcnt = {k: 0 for k in self.dcnt}

    def mark(self, ch, instr, n=1):
        instr.then_inc(self.sems[ch], n)
        self.cnt[ch] += n
        return self.cnt[ch]

    def dma(self, eng_proxy, name, out, in_, **kw):
        if name not in self.dslot:
            assert len(self.dslot) < self.NDMA, "dma sem pool exhausted"
            self.dslot[name] = len(self.dslot)
            self.dcnt[name] = 0
        ins = eng_proxy.dma_start(out=out, in_=in_, **kw)
        ins.then_inc(self.dpool[self.dslot[name]], 16)
        self.dcnt[name] += 16
        return self.dcnt[name]

    def waitd(self, eng_proxy, name, val=None):
        if name not in self.dslot:
            return
        v = self.dcnt[name] if val is None else val
        if v > 0:
            eng_proxy.wait_ge(self.dpool[self.dslot[name]], v)

    def wait(self, eng_proxy, ch, val=None):
        v = self.cnt[ch] if val is None else val
        if v > 0:
            eng_proxy.wait_ge(self.sems[ch], v)

    def _op(self, proxy, ch, name, *a, dep=True, **k):
        if dep:
            self.wait(proxy, ch)
        ins = getattr(proxy, name)(*a, **k)
        self.mark(ch, ins)
        return self.cnt[ch]

    def v(self, name, *a, **k):
        return self._op(self.V, "v", name, *a, **k)

    def s(self, name, *a, **k):
        return self._op(self.S, "s", name, *a, **k)

    def g(self, name, *a, **k):
        return self._op(self.G, "g", name, *a, **k)


def build_program(T=16384, debug_names=()):
    TPC = T // NCORES
    NTT = TPC // 128
    NWIN = T // WIN
    NBLK = max(1, TPC // 512)
    AGROW = D + 1

    nc = bass.Bass("TRN2", target_bir_lowering=False, debug=False)
    B = types.SimpleNamespace(T=T, TPC=TPC, NTT=NTT, NWIN=NWIN, NBLK=NBLK,
                              AGROW=AGROW, nc=nc)

    def din(name, shape, dt=F32):
        setattr(B, name, nc.dram_tensor(name, list(shape), dt, kind="ExternalInput").ap())

    din("x_sl", (TPC, D))
    din("wq_s", (D, HPC * DK))
    din("wk_s", (D, HPC * DK))
    din("wv_s", (D, HPC * DV))
    din("cq_s", (HPC * DK, 4))
    din("ck_s", (HPC * DK, 4))
    din("cv_s", (HPC * DV, 4))
    din("alog_s", (1, HPC))
    din("dtb_s", (128, HPC))
    din("proxy", (DK, DV))
    din("unc", (D, BINS))
    din("he_s", (1, HPC * 4))
    din("mlp1", (128, 128))
    din("mlpb1", (32, 1))
    din("mlp2", (128, 128))
    din("mlpb2", (32, 1))
    din("gw5", (32, 5))
    din("gb5", (1, 5))
    din("g1w", (D, 128))
    din("g2w", (128, D))
    din("g2b", (128, D // 128))
    din("onorm", (DV, 1))
    din("wo", (H * DV, D))
    din("gateT", (D, 2 * INTER))
    din("downT", (INTER, D))
    din("gnorm", (1, D))
    din("dnorm", (1, INTER))
    din("identc", (128, 128), BF16)
    din("identf", (128, 128), F32)
    din("trilS", (128, 128), BF16)
    din("trilI", (128, 128), BF16)
    din("rep32", (32, 128), F32)
    din("ones1", (1, 128), BF16)
    din("ones1f", (1, 128), F32)
    din("onescf", (128, 1), F32)
    din("trilIf", (128, 128), F32)
    din("bd8m", (128, 128), mybir.dt.uint8)
    din("onescb", (128, 1), BF16)
    din("onesff", (128, 128), F32)
    din("sel96f", (128, 128), F32)

    B.y_sl = nc.dram_tensor("y_sl", [TPC, D], F32, kind="ExternalOutput").ap()
    B.dbg = {}
    for name, shape, dt in debug_names:
        B.dbg[name] = nc.dram_tensor(name, list(shape), dt, kind="ExternalOutput").ap()

    B.xn_tm_d = nc.dram_tensor("xn_tm_d", [TPC, D], F32).ap()
    B.h_d = nc.dram_tensor("h_d", [512, INTER], F32).ap()
    B.ag_in = nc.dram_tensor("ag_in", [AGROW, TPC], BF16)
    B.ag_out = nc.dram_tensor("ag_out", [NCORES * AGROW, TPC], BF16)
    B.a2a_in = nc.dram_tensor("a2a_in", [NCORES * HPC * DV, TPC], BF16)
    B.a2a_out = nc.dram_tensor("a2a_out", [NCORES * HPC * DV, TPC], BF16)
    B.tern_dn_d = nc.dram_tensor("tern_dn_d", [INTER, D], BF16).ap()
    B.tern_gt_d = nc.dram_tensor("tern_gt_d", [D, 2 * INTER], BF16).ap()
    B.gffn_d = nc.dram_tensor("gffn_d", [TPC, D], BF16).ap()
    B.sgate_d = nc.dram_tensor("sgate_d", [D, TPC], BF16).ap()

    es = contextlib.ExitStack()

    def sbuf(name, shape, dt=F32):
        setattr(B, name, es.enter_context(nc.sbuf_tensor(name, list(shape), dt)))

    def psum(name, shape, dt=F32):
        setattr(B, name, es.enter_context(nc.psum_tensor(name, list(shape), dt)))

    with es:
        sbuf("ident_t", (128, 128), BF16)
        sbuf("identf_t", (128, 128), F32)
        sbuf("trilS_t", (128, 128), BF16)
        sbuf("trilI_t", (128, 128), BF16)
        sbuf("rep32_t", (32, 128), F32)
        sbuf("ones1_t", (1, 128), BF16)
        sbuf("ones1f_t", (1, 128), F32)
        sbuf("onescf_t", (128, 1), F32)
        sbuf("trilIf_t", (128, 128), F32)
        sbuf("bd8m_t", (128, 128), mybir.dt.uint8)
        sbuf("onesff_t", (128, 128), F32)
        sbuf("sel96f_t", (128, 128), F32)
        sbuf("onescb_t", (128, 1), BF16)
        sbuf("hebc_t", (128, HPC * 4), F32)
        sbuf("wq_t", (128, 16 * HPC * DK), BF16)
        sbuf("wk_t", (128, 16 * HPC * DK), BF16)
        sbuf("wv_t", (128, 16 * HPC * DV), BF16)
        sbuf("convw", (128, 3 * HPC * 4), F32)
        sbuf("proxy_t", (128, DV), BF16)
        sbuf("unc_t", (128, 16 * BINS), BF16)
        sbuf("he_t", (1, HPC * 4), F32)
        sbuf("mlp1_t", (128, 128), BF16)
        sbuf("mlp2_t", (128, 128), BF16)
        sbuf("gaug_t", (128, 128), BF16)
        sbuf("gw5_t", (32, 5), F32)
        sbuf("baug_t", (1, 16), F32)
        sbuf("baugc_t", (128, 1), F32)
        sbuf("b1rep_t", (128, 1), F32)
        sbuf("b2rep_t", (128, 1), F32)
        sbuf("gcol_t", (128, HPC), F32)
        sbuf("ngcol_t", (128, HPC), F32)
        sbuf("gnb_t", (128, D), F32)
        sbuf("dnrow_t", (1, INTER), F32)
        sbuf("g2b_t", (128, 16), F32)
        sbuf("onorm_t", (128, 1), F32)
        sbuf("swcol_t", (128, 4), F32)
        sbuf("stat_t", (128, 4 * NTT), F32)
        sbuf("wpart_t", (128, 160), F32)
        sbuf("scl_t", (1, 16), F32)
        sbuf("mlpb_t", (32, 2), F32)
        sbuf("dtb_t", (128, HPC), F32)
        sbuf("alog_t", (1, HPC), F32)
        sbuf("arA", (128, 20 * 1024), BF16)
        sbuf("arB", (128, 11904), F32)
        sbuf("arC", (128, 14336), BF16)
        sbuf("arD", (128, 4 * 1024), F32)
        sbuf("Sf_t", (128, NHR * DV), F32)
        sbuf("Sb_t", (128, NHR * DV), BF16)
        sbuf("carry_t", (128, 9 * HPC), F32)
        sbuf("lamc_t", (128, NHR), F32)
        for i in range(5):
            psum(f"pb{i}", (128, 512), F32)
        psum("pT1a", (128, 1024), BF16)
        psum("pT1b", (128, 1024), BF16)
        psum("pT2", (128, 512), F32)

        sems = {}
        for chn in Ctx.CH_NAMES:
            sems[chn] = es.enter_context(nc.semaphore("sem_" + chn))
        dpool = [es.enter_context(nc.semaphore(f"dsem{i}")) for i in range(Ctx.NDMA)]
        block = es.enter_context(nc.Block())
        ctx = Ctx(nc, sems, dpool)

        for starter, eng_name in ((block.sync, "sync"), (block.tensor, "tensor"),
                                  (block.vector, "vector"), (block.scalar, "scalar"),
                                  (block.gpsimd, "gpsimd")):
            def _run(engine, _n=eng_name):
                ctx.active = _n
                ctx.real = engine
                ctx.reset()
                _emit(ctx, B)
                ctx.active = None
                ctx.real = None
            starter(_run)

    return nc


_PH1 = 9
_PH2 = True
_PH3 = True
_PH4 = True
_PH0HEB = True
_PH0REP = True
_PH0GAUG = True


def _emit(ctx, B):
    _ph0_init(ctx, B)
    if _PH1 >= 1:
        _ph1_ln(ctx, B)
    _ag_issue(ctx, B)
    if _PH2:
        _ph2_sgate(ctx, B)
    _wmean(ctx, B)
    if _PH3:
        _wtern(ctx, B)
        _ph3_ffn(ctx, B)
    if _PH4:
        _ph4_scan(ctx, B)
        _ph5_a2a(ctx, B)
        _ph6_epilogue(ctx, B)
    else:
        _ph_stub_out(ctx, B)


def _ag_issue(ctx, B):
    G = ctx.G
    for fs in range(4):
        ctx.waitd(G, f"fm{fs}")
    ctx.waitd(G, "agent")
    B.c_ag = ctx.mark("cc", G.collective_compute(
        "AllGather", ALU.bypass, replica_groups=[list(range(NCORES))],
        ins=[B.ag_in.ap().opt()], outs=[B.ag_out.ap().opt()]))


def _ph2_sgate(ctx, B):
    T, V, S, G, Y = ctx.T, ctx.V, ctx.S, ctx.G, ctx.Y
    g1t = B.arC[:, 8192:10240]
    g2t = B.arC[:, 10240:12288]
    ctx.dma(G, "g1w", g1t.rearrange("p (c o) -> p c o", o=128),
            B.g1w.rearrange("(c p) o -> p c o", p=128))
    ctx.dma(G, "g2w", g2t, B.g2w[:, :])
    nck = B.TPC // 512
    xfm = [B.arC[:, 0:8192], B.arC[:, 0:8192]]
    g1o = B.arC[:, 12288:13312]
    sg = B.arC[:, 13312:14336]
    for fs in range(4):
        ctx.waitd(Y, f"fm{fs}")
    ctx.waitd(Y, "agent")
    ctx.waitd(T, "g1w")
    ctx.waitd(T, "g2w")
    xf_free = 0
    v_g1 = 0
    s_done = [0, 0]
    for c in range(nck):
        sl = c % 2
        ctx.wait(Y, "t", xf_free)
        ctx.dma(Y, "sgx0", xfm[0].rearrange("p (a t) -> p a t", t=512),
                B.ag_in[0:D, c * 512:(c + 1) * 512].rearrange("(a p) t -> p a t", p=128))
        ctx.waitd(T, "sgx0")
        ctx.wait(T, "v", v_g1)
        tm = 0
        for ic in range(16):
            tm = ctx.mark("t", T.matmul(B.pb4[:, 0:512], g1t[:, ic * 128:(ic + 1) * 128],
                                        xfm[0][:, ic * 512:(ic + 1) * 512],
                                        start=(ic == 0), stop=(ic == 15)))
        xf_free = tm
        ctx.wait(V, "t", tm)
        v_g1 = ctx.v("tensor_copy", g1o[:, sl * 512:(sl + 1) * 512], B.pb4[:, 0:512])
        for oc in range(16):
            pb = [B.pb0, B.pb1][oc % 2]
            ctx.wait(T, "v", v_g1)
            ctx.wait(T, "s", s_done[oc % 2])
            tg = ctx.mark("t", T.matmul(pb[:, 0:512], g2t[:, oc * 128:(oc + 1) * 128],
                                        g1o[:, sl * 512:(sl + 1) * 512],
                                        start=True, stop=True))
            ctx.wait(S, "t", tg)
            ctx.waitd(S, f"sgs{oc % 2}")
            s_done[oc % 2] = ctx.s("activation", sg[:, (oc % 2) * 512:(oc % 2) * 512 + 512],
                                   pb[:, 0:512], AF.Sigmoid,
                                   bias=B.g2b_t[:, oc:oc + 1])
            ctx.wait(G, "s", s_done[oc % 2])
            ctx.dma(G, f"sgs{oc % 2}",
                    B.sgate_d[oc * 128:(oc + 1) * 128, c * 512:(c + 1) * 512],
                    sg[:, (oc % 2) * 512:(oc % 2) * 512 + 512])
        pass
    B.c_sgate_t = ctx.cnt["t"]


def _wmean(ctx, B):
    T, V, S, G, Y = ctx.T, ctx.V, ctx.S, ctx.G, ctx.Y
    wbuf = [B.arB[:, 6656:8704], B.arB[:, 8704:10752]]
    junk = B.arB[:, 4096:6144]
    ctx.waitd(Y, "xnst0")
    ctx.waitd(Y, "xnst1")
    ctx.wait(Y, "s", ctx.cnt["s"])
    cons = [("t", 0), ("t", 0)]
    col = 0
    bounds = []
    for src_ap, nr, ncol in ((B.gateT, D, 2 * INTER), (B.downT, INTER, D)):
        c0 = col
        for r in range(0, nr, 128):
            for cc in range(0, ncol, 2048):
                w = min(2048, ncol - cc)
                sl = col % 2
                ch, cv = cons[sl]
                ctx.wait(Y, ch, cv)
                ctx.dma(Y, f"wm{sl}", wbuf[sl][:, 0:w], src_ap[r:r + 128, cc:cc + w])
                eng = col % 2
                out = B.wpart_t[:, col:col + 1]
                if eng == 0:
                    ctx.waitd(S, f"wm{sl}")
                    m = ctx.s("activation", junk[:, 0:w], wbuf[sl][:, 0:w], AF.Abs,
                              accum_out=out)
                    cons[sl] = ("s", m)
                else:
                    ctx.waitd(V, f"wm{sl}")
                    m = ctx.v("reduce_sum", out, wbuf[sl][:, 0:w], axis=AX.X,
                              apply_absolute_value=True)
                    cons[sl] = ("v", m)
                col += 1
        bounds.append((c0, col, nr * ncol))
    ctx.wait(V, "s", ctx.cnt["s"])
    ctx.wait(V, "g", ctx.cnt["g"])
    pc = B.arD[0:128, 52:54]
    for k, (c0, c1, nel) in enumerate(bounds):
        ctx.v("reduce_sum", pc[:, k:k + 1], B.wpart_t[:, c0:c1], axis=AX.X)
    v_pc = ctx.cnt["v"]
    ctx.wait(T, "v", v_pc)
    ctx.wait(T, "t", B.c_init_t)
    tm = 0
    for k in range(2):
        tm = ctx.mark("t", T.matmul(B.pT2[0:1, 8 + k:9 + k], pc[:, k:k + 1],
                                    B.onescf_t[:, 0:1], start=True, stop=True))
    ctx.wait(V, "t", tm)
    for k, (c0, c1, nel) in enumerate(bounds):
        ctx.v("tensor_scalar", B.scl_t[0:1, 4 + k:5 + k], B.pT2[0:1, 8 + k:9 + k],
              1.0 / nel, 1e-5, ALU.mult, ALU.max)
    ctx.v("reciprocal", B.scl_t[0:1, 6:8], B.scl_t[0:1, 4:6])
    v_sw = ctx.cnt["v"]
    ctx.wait(T, "v", v_sw)
    tb = ctx.mark("t", T.matmul(B.pT2[:, 10:12], B.ones1f_t[0:1, :],
                                B.scl_t[0:1, 6:8], start=True, stop=True))
    ctx.wait(V, "t", tb)
    ctx.v("tensor_copy", B.swcol_t[:, 0:2], B.pT2[:, 10:12])
    B.c_wmean_v = ctx.cnt["v"]


def _wtern(ctx, B):
    T, V, S, G, Y = ctx.T, ctx.V, ctx.S, ctx.G, ctx.Y
    wbuf = [B.arB[:, 6656:8704], B.arB[:, 8704:10752]]
    sgn = [B.arB[:, 0:2048], B.arB[:, 2048:4096]]
    ab = [B.arB[:, 4096:6144], B.arD[:, 1024:3072]]
    tern = [B.arA[:, 12288:14336], B.arA[:, 14336:16384]]
    ctx.wait(S, "v", B.c_wmean_v)
    ctx.wait(S, "t", B.c_init_t)
    ctx.wait(Y, "s", ctx.cnt["s"])
    ctx.wait(Y, "v", ctx.cnt["v"])
    cons = [("v", 0), ("v", 0)]
    vlast = [0, 0]
    cnt = 0
    for (src_ap, dst_ap, nr, ncol, k) in ((B.gateT, B.tern_gt_d, D, 2 * INTER, 0),
                                          (B.downT, B.tern_dn_d, INTER, D, 1)):
        for r in range(0, nr, 128):
            for cc in range(0, ncol, 2048):
                w = min(2048, ncol - cc)
                sl = cnt % 2
                cnt += 1
                ch, cv = cons[sl]
                ctx.wait(Y, ch, cv)
                ctx.dma(Y, f"wt{sl}", wbuf[sl][:, 0:w], src_ap[r:r + 128, cc:cc + w])
                ctx.waitd(S, f"wt{sl}")
                ctx.wait(S, "v", vlast[sl])
                ctx.s("activation", sgn[sl][:, 0:w], wbuf[sl][:, 0:w], AF.Sign,
                      scale=B.swcol_t[:, k:k + 1])
                m_ab = ctx.s("activation", ab[sl][:, 0:w], wbuf[sl][:, 0:w], AF.Abs,
                             scale=B.swcol_t[:, k:k + 1])
                cons[sl] = ("s", m_ab)
                ctx.wait(V, "s", m_ab)
                ctx.v("tensor_scalar", ab[sl][:, 0:w], ab[sl][:, 0:w], 0.5, None, ALU.is_ge)
                ctx.waitd(V, f"wo{sl}")
                mt = ctx.v("tensor_tensor", tern[sl][:, 0:w], ab[sl][:, 0:w],
                           sgn[sl][:, 0:w], ALU.mult)
                vlast[sl] = mt
                ctx.wait(G, "v", mt)
                ctx.dma(G, f"wo{sl}", dst_ap[r:r + 128, cc:cc + w], tern[sl][:, 0:w])
    for sl in range(2):
        ctx.waitd(G, f"wo{sl}")
    B.c_wtern_done = (ctx.cnt["v"], ctx.cnt["s"])


def _ph3_ffn(ctx, B):
    T, V, S, G, Y = ctx.T, ctx.V, ctx.S, ctx.G, ctx.Y
    NBLK = B.TPC // 512
    xqfm = B.arA[:, 0:8192]
    xqtm = [B.arA[:, 16384:18432], B.arA[:, 18432:20480]]
    silug = B.arD[:, 2048:4096]
    tgt = [B.arA[:, 8192:10240], B.arA[:, 10240:12288]]
    dnt = [B.arA[:, 12288:14336], B.arA[:, 14336:16384]]
    hqtm = B.arC[:, 0:11264]
    xin = [B.arB[:, 0:2048], B.arB[:, 2048:4096]]
    rbuf = B.arB[:, 4096:6144]
    hstage = [B.arB[:, 6144:6656], B.arB[:, 6656:7168]]
    hjunk = B.arB[:, 7168:7680]
    dnocp = B.arB[:, 7680:8192]
    hdnb = [B.arB[:, 8192:8704], B.arB[:, 8704:9216]]
    hchunk = [B.arB[:, 4096:5504], B.arB[:, 5504:6912]]
    amax = B.arD[0:128, 56:57]
    stcol = B.arD[0:128, 57:58]
    c1col = B.arD[0:128, 60:64]
    c2col = B.arD[0:128, 64:65]
    sqcol = B.arD[0:128, 68:72]
    mxcol = B.arD[0:128, 72:76]
    part = B.arD[0:128, 76:77]
    rs2 = B.arD[0:128, 67:68]
    ctx.wait(V, "t", B.c_sgate_t)
    ctx.wait(S, "t", B.c_sgate_t)
    for q in range(2):
        ctx.waitd(V, f"sgs{q}")
        ctx.waitd(S, f"sgs{q}")
    ctx.wait(S, "v", B.c_wtern_done[0])
    ctx.wait(V, "s", ctx.cnt["s"])
    ctx.waitd(Y, "wo0")
    ctx.waitd(Y, "wo1")
    ctx.waitd(V, "wt0")
    ctx.waitd(V, "wt1")
    for sl in range(2):
        ctx.waitd(S, f"xnst{sl}")
    t_l1_done = 0
    vq = [0, 0]
    for b in range(NBLK):
        for tt in range(4):
            it = b * 4 + tt
            sl = tt % 2
            ctx.waitd(S, f"ffx{sl}")
            ctx.waitd(S, f"fft{sl}")
            ctx.wait(S, "v", vq[sl])
            ctx.wait(S, "t", ctx.cnt["t"])
            ctx.dma(S, f"ffx{sl}", xin[sl], B.xn_tm_d[it * 128:(it + 1) * 128, :])
            ctx.waitd(V, f"ffx{sl}")
            ctx.wait(V, "s", ctx.cnt["s"])
            rsf = B.stat_t[:, 4 * it + 2:4 * it + 3]
            ctx.v("scalar_tensor_tensor", rbuf, xin[sl], rsf, B.gnb_t[:],
                  ALU.mult, ALU.mult)
            ctx.v("tensor_reduce", amax, rbuf, AX.X, ALU.max,
                  apply_absolute_value=True)
            ctx.v("tensor_scalar", amax, amax, 1e-5, None, ALU.max)
            ctx.v("reciprocal", stcol, amax)
            ctx.v("tensor_scalar", stcol, stcol, 127.0, None, ALU.mult)
            ctx.v("tensor_tensor", c1col[:, tt:tt + 1], stcol, B.swcol_t[:, 0:1],
                  ALU.mult)
            ctx.v("reciprocal", c1col[:, tt:tt + 1], c1col[:, tt:tt + 1])
            ctx.v("tensor_scalar", rbuf, rbuf, stcol, MAGIC, ALU.mult, ALU.add)
            mq = ctx.v("tensor_scalar", xqtm[sl], rbuf, MAGIC, None, ALU.subtract)
            vq[sl] = mq
            ctx.wait(Y, "v", mq)
            ctx.wait(Y, "t", ctx.cnt["t"])
            for c in range(16):
                ctx.dma(Y, f"fft{sl}", xqfm[:, c * 512 + tt * 128:c * 512 + tt * 128 + 128],
                        xqtm[sl][:, c * 128:(c + 1) * 128], transpose=True)
        for sl in range(2):
            ctx.waitd(T, f"fft{sl}")
        ctx.v("memset", sqcol, 0.0)
        ctx.v("memset", mxcol, 0.0)
        s_silu = [0] * 4
        v_h = [0] * 4
        g_h = [0] * 4
        tg_cons = [("t", ctx.cnt["t"]), ("t", ctx.cnt["t"])]
        ctx.wait(T, "s", ctx.cnt["s"])
        ctx.wait(T, "v", ctx.cnt["v"])
        for ocp in range(11):
            for phase in range(2):
                oc = ocp + phase * 11
                tmk = 0
                for icg in range(4):
                    sl = icg % 2
                    ch, cv = tg_cons[sl]
                    ctx.wait(Y, ch, cv)
                    ctx.dma(Y, f"tg{sl}", tgt[sl].rearrange("p (c o) -> p c o", o=512),
                            B.tern_gt_d[icg * 512:(icg + 1) * 512,
                                        oc * 512:(oc + 1) * 512]
                            .rearrange("(c p) o -> p c o", p=128))
                    ctx.waitd(T, f"tg{sl}")
                    if icg == 0:
                        if phase == 0:
                            for q in range(4):
                                ctx.wait(T, "v", v_h[q])
                        else:
                            for q in range(4):
                                ctx.wait(T, "s", s_silu[q])
                    for i in range(4):
                        ic = icg * 4 + i
                        for q in range(4):
                            tmk = ctx.mark("t", T.matmul(
                                [B.pb0, B.pb1, B.pb2, B.pb3][q][:, 0:512],
                                xqfm[:, ic * 512 + q * 128:ic * 512 + q * 128 + 128],
                                tgt[sl][:, i * 512:(i + 1) * 512],
                                start=(ic == 0), stop=(ic == 15)))
                    tg_cons[sl] = ("t", tmk)
                if phase == 0:
                    for q in range(4):
                        ctx.wait(S, "t", tmk)
                        ctx.wait(S, "v", v_h[q])
                        s_silu[q] = ctx.s("activation", silug[:, q * 512:(q + 1) * 512],
                                          [B.pb0, B.pb1, B.pb2, B.pb3][q][:, 0:512],
                                          AF.Silu, scale=c1col[:, q:q + 1])
                else:
                    ctx.wait(T, "v", ctx.cnt["v"])
                    tdn = ctx.mark("t", T.matmul(
                        B.pb4[:, 0:512], B.ones1f_t[0:1, :],
                        B.dnrow_t[0:1, ocp * 512:(ocp + 1) * 512],
                        start=True, stop=True))
                    ctx.wait(V, "t", tdn)
                    ctx.v("tensor_copy", dnocp, B.pb4[:, 0:512])
                    for q in range(4):
                        hsl = q % 2
                        ctx.wait(V, "t", tmk)
                        ctx.wait(V, "s", s_silu[q])
                        ctx.waitd(V, f"hst{hsl}")
                        ctx.v("scalar_tensor_tensor", hstage[hsl],
                              [B.pb0, B.pb1, B.pb2, B.pb3][q][:, 0:512],
                              c1col[:, q:q + 1], silug[:, q * 512:(q + 1) * 512],
                              ALU.mult, ALU.mult)
                        ctx.v("tensor_tensor", hdnb[hsl], hstage[hsl], dnocp,
                              ALU.mult)
                        ctx.wait(S, "v", ctx.cnt["v"])
                        ctx.s("activation", hjunk, hstage[hsl], AF.Square,
                              accum_out=part)
                        ctx.wait(V, "s", ctx.cnt["s"])
                        ctx.v("tensor_tensor", sqcol[:, q:q + 1], sqcol[:, q:q + 1],
                              part, ALU.add)
                        ctx.v("tensor_reduce", part, hdnb[hsl], AX.X, ALU.max,
                              apply_absolute_value=True)
                        v_h[q] = ctx.v("tensor_tensor", mxcol[:, q:q + 1],
                                       mxcol[:, q:q + 1], part, ALU.max)
                        ctx.wait(G, "v", v_h[q])
                        g_h[q] = ctx.dma(G, f"hst{hsl}",
                                         B.h_d[q * 128:(q + 1) * 128,
                                               ocp * 512:(ocp + 1) * 512],
                                         hdnb[hsl])
        t_l1_done = ctx.cnt["t"]
        ctx.waitd(S, "hst0")
        ctx.waitd(S, "hst1")
        rs2c = B.arD[0:128, 80:84]
        st2c = B.arD[0:128, 84:88]
        c2c = B.arD[0:128, 88:92]
        ctx.wait(V, "s", ctx.cnt["s"])
        for tt in range(4):
            ctx.v("tensor_scalar", rs2c[:, tt:tt + 1], sqcol[:, tt:tt + 1],
                  1.0 / INTER, NORM_EPS, ALU.mult, ALU.add)
        ctx.v("reciprocal", rs2c, rs2c)
        v_m = ctx.cnt["v"]
        ctx.wait(S, "v", v_m)
        s_rs = ctx.s("activation", rs2c, rs2c, AF.Sqrt)
        ctx.wait(V, "s", s_rs)
        ctx.v("tensor_scalar", st2c, mxcol, 1e-5, None, ALU.max)
        ctx.v("reciprocal", st2c, st2c)
        ctx.v("tensor_scalar", st2c, st2c, 127.0, None, ALU.mult)
        for tt in range(4):
            ctx.v("tensor_tensor", c2c[:, tt:tt + 1], st2c[:, tt:tt + 1],
                  B.swcol_t[:, 1:2], ALU.mult)
        ctx.v("reciprocal", c2c, c2c)
        v_c2 = ctx.v("tensor_tensor", c2c, c2c, rs2c, ALU.mult)
        qcc = B.arD[0:128, 92:96]
        ctx.v("tensor_copy", qcc, st2c)
        cw = 1408
        for half in range(2):
          tts = (2 * half, 2 * half + 1)
          for c in range(4):
              ctx.waitd(V, f"fq{c}")
          for cchunk in range(4):
            for tt in tts:
                hsl = tt % 2
                ctx.waitd(S, f"hrd{hsl}")
                ctx.wait(S, "v", ctx.cnt["v"])
                ctx.dma(S, f"hrd{hsl}", hchunk[hsl],
                        B.h_d[tt * 128:(tt + 1) * 128, cchunk * cw:(cchunk + 1) * cw])
                ctx.waitd(V, f"hrd{hsl}")
                ctx.v("tensor_scalar", hchunk[hsl], hchunk[hsl], qcc[:, tt:tt + 1],
                      MAGIC, ALU.mult, ALU.add)
                ctx.v("tensor_scalar", hqtm[:, (tt % 2) * 5632 + cchunk * cw:(tt % 2) * 5632 + (cchunk + 1) * cw],
                      hchunk[hsl], MAGIC, None, ALU.subtract)
          mq2 = ctx.cnt["v"]
          if "dbg_hq" in B.dbg and b == 0 and half == 0:
              ctx.wait(G, "v", mq2)
              ctx.dma(G, "dbghq", B.dbg["dbg_hq"][:, :], B.arC[:, 0:11264])
              ctx.waitd(Y, "dbghq")
          ctx.wait(Y, "v", mq2)
          ctx.wait(Y, "t", ctx.cnt["t"])
          for j, tt in enumerate(tts):
            for c in range(44):
                ctx.dma(Y, f"fq{c % 4}", B.arA[:, j * 5632 + c * 128:j * 5632 + (c + 1) * 128],
                        hqtm[:, (tt % 2) * 5632 + c * 128:(tt % 2) * 5632 + (c + 1) * 128],
                        transpose=True)
          for c in range(4):
              ctx.waitd(T, f"fq{c}")
          dn_cons = [("t", ctx.cnt["t"]), ("t", ctx.cnt["t"])]
          ctx.wait(S, "v", ctx.cnt["v"])
          for oc in range(4):
                ctx.wait(T, "s", ctx.cnt["s"])
                tmk = 0
                for icg in range(11):
                    sl = icg % 2
                    ch, cv = dn_cons[sl]
                    ctx.wait(G, ch, cv)
                    ctx.dma(G, f"dn{sl}", dnt[sl].rearrange("p (c o) -> p c o", o=512),
                            B.tern_dn_d[icg * 512:(icg + 1) * 512,
                                        oc * 512:(oc + 1) * 512]
                            .rearrange("(c p) o -> p c o", p=128))
                    ctx.waitd(T, f"dn{sl}")
                    for j, tt in enumerate(tts):
                        for i in range(4):
                            ic = icg * 4 + i
                            tmk = ctx.mark("t", T.matmul(
                                [B.pb0, B.pb1][j][:, 0:512],
                                B.arA[:, j * 5632 + ic * 128:j * 5632 + (ic + 1) * 128],
                                dnt[sl][:, i * 512:(i + 1) * 512],
                                start=(ic == 0), stop=(ic == 43)))
                    dn_cons[sl] = ("t", tmk)
                for j, tt in enumerate(tts):
                    it = b * 4 + tt
                    ctx.wait(S, "t", tmk)
                    ctx.waitd(S, f"gf{j}")
                    s_g = ctx.s("activation",
                                B.arC[:, 11264 + j * 512:11264 + (j + 1) * 512],
                                [B.pb0, B.pb1][j][:, 0:512], AF.Silu,
                                scale=c2c[:, tt:tt + 1])
                    ctx.wait(G, "s", s_g)
                    ctx.dma(G, f"gf{j}",
                            B.gffn_d[it * 128:(it + 1) * 128, oc * 512:(oc + 1) * 512],
                            B.arC[:, 11264 + j * 512:11264 + (j + 1) * 512])
        for hsl in range(2):
            ctx.waitd(G, f"hrd{hsl}")
    for oc in range(2):
        ctx.waitd(G, f"gf{oc}")
    B.c_ffn_done = (ctx.cnt["t"], ctx.cnt["v"], ctx.cnt["s"])


def _ph0_init(ctx, B):
    T, V, S, G, Y = ctx.T, ctx.V, ctx.S, ctx.G, ctx.Y

    for dst, src in ((B.ident_t, B.identc), (B.identf_t, B.identf),
                     (B.trilS_t, B.trilS), (B.trilI_t, B.trilI),
                     (B.rep32_t, B.rep32), (B.ones1_t, B.ones1),
                     (B.ones1f_t, B.ones1f), (B.onescf_t, B.onescf),
                     (B.trilIf_t, B.trilIf), (B.bd8m_t, B.bd8m),
                     (B.onescb_t, B.onescb), (B.onesff_t, B.onesff),
                     (B.sel96f_t, B.sel96f)):
        ctx.dma(Y, "ini", dst[:], src[:, :])
    ctx.dma(Y, "ini", B.gw5_t[:], B.gw5[:, :])
    ctx.dma(Y, "ini", B.baug_t[0:1, 0:5], B.gb5[:, :])
    ctx.dma(Y, "ini", B.mlpb_t[0:32, 0:1], B.mlpb1[:, :])
    ctx.dma(Y, "ini", B.mlpb_t[0:32, 1:2], B.mlpb2[:, :])
    ctx.dma(Y, "ini", B.he_t[:], B.he_s[:, :])
    ctx.dma(Y, "ini", B.alog_t[:], B.alog_s[:, :])
    ctx.dma(Y, "ini", B.dtb_t[:], B.dtb_s[:, :])
    ctx.dma(Y, "ini", B.onorm_t[:], B.onorm[:, :])
    ctx.dma(Y, "ini", B.g2b_t[0:128, 0:D // 128], B.g2b[:, :])
    c_small = ctx.dcnt["ini"]

    ctx.dma(G, "wts", B.wq_t[:].rearrange("p (c o) -> p c o", o=256), B.wq_s.rearrange("(c p) o -> p c o", p=128))
    ctx.dma(G, "wts", B.wk_t[:].rearrange("p (c o) -> p c o", o=256), B.wk_s.rearrange("(c p) o -> p c o", p=128))
    ctx.dma(G, "wts", B.wv_t[:].rearrange("p (c o) -> p c o", o=256), B.wv_s.rearrange("(c p) o -> p c o", p=128))
    ctx.dma(G, "wts", B.proxy_t[:], B.proxy[:, :])
    ctx.dma(G, "wts", B.unc_t[:].rearrange("p (c o) -> p c o", o=BINS), B.unc.rearrange("(c p) o -> p c o", p=128))
    ctx.dma(G, "wts", B.mlp1_t[:], B.mlp1[:, :])
    ctx.dma(G, "wts", B.mlp2_t[:], B.mlp2[:, :])
    for j, src in enumerate((B.cq_s, B.ck_s, B.cv_s)):
        for h in range(HPC):
            ctx.dma(G, "wts", B.convw[:, (j * HPC + h) * 4:(j * HPC + h) * 4 + 4],
                    src[h * 128:(h + 1) * 128, :])
    c_gw = ctx.dcnt["wts"]

    ctx.waitd(V, "ini", c_small)
    ctx.waitd(V, "wts", c_gw)
    sr = B.arD[0:32, 0:5]
    ctx.v("tensor_add", sr[0:32, 0:1], B.gw5_t[0:32, 0:1], B.gw5_t[0:32, 1:2])
    ctx.v("tensor_sub", sr[0:32, 1:2], B.gw5_t[0:32, 0:1], B.gw5_t[0:32, 1:2])
    ctx.v("tensor_copy", sr[0:32, 2:3], B.gw5_t[0:32, 2:3])
    ctx.v("tensor_add", sr[0:32, 3:4], B.gw5_t[0:32, 3:4], B.gw5_t[0:32, 4:5])
    ctx.v("tensor_sub", sr[0:32, 4:5], B.gw5_t[0:32, 3:4], B.gw5_t[0:32, 4:5])
    if _PH0GAUG:
        ctx.v("memset", B.gaug_t[:], 0.0)
        ctx.v("tensor_copy", B.gaug_t[0:32, 0:5], sr)
    b = B.baug_t
    ctx.v("tensor_add", b[0:1, 8:9], b[0:1, 0:1], b[0:1, 1:2])
    ctx.v("tensor_sub", b[0:1, 9:10], b[0:1, 0:1], b[0:1, 1:2])
    ctx.v("tensor_copy", b[0:1, 10:11], b[0:1, 2:3])
    ctx.v("tensor_add", b[0:1, 11:12], b[0:1, 3:4], b[0:1, 4:5])
    v_aug = ctx.v("tensor_sub", b[0:1, 12:13], b[0:1, 3:4], b[0:1, 4:5])

    ctx.wait(T, "v", v_aug)
    t1 = ctx.mark("t", T.matmul(B.pT2[0:5, 0:1], b[0:1, 8:13],
                                B.ones1f_t[0:1, 0:1], start=True, stop=True))
    ctx.v("memset", B.baugc_t[0:32, 0:1], 0.0)
    ctx.wait(V, "t", t1)
    v_bc = ctx.v("tensor_copy", B.baugc_t[0:5, 0:1], B.pT2[0:5, 0:1])
    ctx.wait(Y, "v", v_bc)
    if _PH0REP:
        for q in range(1, 4):
            ctx.dma(Y, "rep", B.gaug_t[32 * q:32 * q + 32, 32 * q:32 * q + 5],
                    B.gaug_t[0:32, 0:5])
            ctx.dma(Y, "rep", B.baugc_t[32 * q:32 * q + 5, 0:1], B.baugc_t[0:5, 0:1])
    B.c_rep = ctx.dcnt.get("rep", 0)

    ctx.wait(T, "v", v_bc)
    T.matmul(B.pT2[:, 1:2], B.rep32_t[:, :], B.mlpb_t[0:32, 0:1], start=True, stop=True)
    t2 = ctx.mark("t", T.matmul(B.pT2[:, 2:3], B.rep32_t[:, :], B.mlpb_t[0:32, 1:2],
                                start=True, stop=True))
    ctx.wait(V, "t", t2)
    ctx.v("tensor_copy", B.b1rep_t[:], B.pT2[:, 1:2])
    v_b12 = ctx.v("tensor_copy", B.b2rep_t[:], B.pT2[:, 2:3])

    ctx.waitd(S, "ini", c_small)
    sp = B.arD[0:128, 16:16 + HPC]
    ab_ = B.arD[0:128, 20:20 + HPC]
    ctx.s("activation", ab_, B.dtb_t[:], AF.Abs)
    ctx.s("activation", sp, ab_, AF.Exp, scale=-1.0)
    s_sp = ctx.s("activation", sp, sp, AF.Ln, bias=1.0)
    s_ea = ctx.s("activation", B.scl_t[0:1, 0:HPC], B.alog_t[:], AF.Exp)
    ctx.wait(V, "s", s_sp)
    ctx.v("tensor_scalar", ab_, B.dtb_t[:], 0.0, None, ALU.max)
    v_sp = ctx.v("tensor_add", sp, sp, ab_)
    ctx.wait(T, "s", s_ea)
    ctx.wait(T, "v", v_b12)
    t3 = ctx.mark("t", T.matmul(B.pT2[:, 4:4 + HPC], B.ones1f_t[0:1, :],
                                B.scl_t[0:1, 0:HPC], start=True, stop=True))
    ctx.wait(V, "t", t3)
    ctx.wait(V, "v", v_sp)
    ctx.v("scalar_tensor_tensor", B.gcol_t[:], B.pT2[:, 4:4 + HPC], -1.0, sp,
          ALU.mult, ALU.mult)
    ctx.v("tensor_scalar", B.ngcol_t[:], B.gcol_t[:], -1.0, None, ALU.mult)

    gn_row = B.arD[0:1, 1024:1024 + D]
    ctx.dma(Y, "nrm", gn_row, B.gnorm[:, :])
    ctx.dma(Y, "nrm", B.dnrow_t[:], B.dnorm[:, :])
    ctx.waitd(T, "nrm")
    for j in range(D // 512):
        ctx.wait(T, "v", ctx.cnt["v"])
        tmk = ctx.mark("t", T.matmul(B.pb0[:, :], B.ones1f_t[0:1, :],
                                     gn_row[0:1, j * 512:(j + 1) * 512],
                                     start=True, stop=True))
        ctx.wait(V, "t", tmk)
        ctx.v("tensor_copy", B.gnb_t[:, j * 512:(j + 1) * 512], B.pb0[:, :])
    if _PH0HEB:
        ctx.waitd(T, "ini", c_small)
        ctx.wait(T, "v", ctx.cnt["v"])
        t4 = ctx.mark("t", T.matmul(B.pT2[:, 12:12 + HPC * 4], B.ones1f_t[0:1, :],
                                    B.he_t[0:1, :], start=True, stop=True))
        ctx.wait(V, "t", t4)
        ctx.v("tensor_copy", B.hebc_t[:], B.pT2[:, 12:12 + HPC * 4])
    B.c_init_v = ctx.cnt["v"]
    B.c_init_t = ctx.cnt["t"]
    B.c_wts = c_gw


def _ph1_ln(ctx, B):
    T, V, S, G, Y = ctx.T, ctx.V, ctx.S, ctx.G, ctx.Y
    NTT = B.NTT

    xbuf = [B.arB[:, 0:2048], B.arB[:, 2048:4096]]
    xnbuf = [B.arA[:, 8192:10240], B.arA[:, 10240:12288]]
    xnf = [B.arB[:, 6656:8704], B.arB[:, 8704:10752]]
    sq_junk = B.arB[:, 4096:6144]
    tmpv = B.arD[0:128, 40:41]
    tmpw = B.arD[0:128, 41:42]

    v_done = [B.c_init_v, B.c_init_v]
    s_done = [0, 0]
    tr_done = [0, 0]
    xn_rd_t = [0, 0]

    for i in range(NTT):
        bsl = i % 2
        ctx.wait(S, "v", v_done[bsl])
        ctx.wait(S, "s", s_done[bsl])
        ctx.dma(S, f"x{bsl}", xbuf[bsl], B.x_sl[i * 128:(i + 1) * 128, :])
        ctx.waitd(S, f"x{bsl}")
        s_m = ctx.s("activation", sq_junk, xbuf[bsl], AF.Square,
                    accum_out=B.stat_t[:, 4 * i + 2:4 * i + 3])
        s_done[bsl] = s_m
        ctx.waitd(V, f"x{bsl}")
        ctx.v("reduce_sum", B.stat_t[:, 4 * i:4 * i + 1], xbuf[bsl], axis=AX.X)
        ctx.wait(V, "s", s_m)
        mu = B.stat_t[:, 4 * i:4 * i + 1]
        s2 = B.stat_t[:, 4 * i + 2:4 * i + 3]
        rs = B.stat_t[:, 4 * i + 1:4 * i + 2]
        rf = B.stat_t[:, 4 * i + 2:4 * i + 3]
        ctx.v("tensor_scalar", mu, mu, 1.0 / D, None, ALU.mult)
        ctx.v("tensor_scalar", tmpv, s2, 1.0 / D, None, ALU.mult)
        ctx.v("scalar_tensor_tensor", tmpw, mu, mu, tmpv, ALU.mult, ALU.subtract)
        ctx.v("tensor_scalar", tmpw, tmpw, -1.0, LN_EPS, ALU.mult, ALU.add)
        v_r = ctx.v("reciprocal", tmpv, tmpw)
        ctx.wait(S, "v", v_r)
        s_r = ctx.s("activation", rs, tmpv, AF.Sqrt)
        ctx.wait(V, "s", s_r)
        ctx.v("tensor_scalar", tmpw, tmpw, -LN_EPS, None, ALU.add)
        ctx.v("tensor_tensor", tmpw, tmpw, tmpv, ALU.mult)
        ctx.v("tensor_scalar", tmpw, tmpw, NORM_EPS, None, ALU.add)
        ctx.v("reciprocal", tmpw, tmpw)
        ctx.wait(S, "v")
        s_f = ctx.s("activation", rf, tmpw, AF.Sqrt)
        ctx.wait(V, "s", s_f)
        ctx.waitd(V, f"xnst{bsl}")
        ctx.wait(V, "t", xn_rd_t[bsl])
        ctx.v("tensor_scalar", xnf[bsl], xbuf[bsl], mu, rs,
              ALU.subtract, ALU.mult)
        v_xn = ctx.v("tensor_copy", xnbuf[bsl], xnf[bsl])
        v_done[bsl] = v_xn
        ctx.wait(G, "v", v_xn)
        ctx.waitd(G, f"xnst{bsl}")
        ctx.dma(G, f"xnst{bsl}", B.xn_tm_d[i * 128:(i + 1) * 128, :], xnf[bsl])
        for j in range(16 if _PH1 >= 2 else 0):
            jj = i * 16 + j
            bank = [B.pT1a, B.pT1b][jj % 2]
            slot = (jj // 2) % 8
            ctx.wait(T, "v", max(tr_done[jj % 2], v_xn))
            tt = ctx.mark("t", T.transpose(bank[:, slot * 128:(slot + 1) * 128],
                                           xnbuf[bsl][:, j * 128:(j + 1) * 128],
                                           B.ident_t[:]))
            fmslot = j % 4
            fmbuf = B.arC[:, fmslot * 128:(fmslot + 1) * 128]
            ctx.waitd(V, f"fm{fmslot}")
            ctx.wait(V, "t", tt)
            v_fm = ctx.v("tensor_copy", fmbuf, bank[:, slot * 128:(slot + 1) * 128])
            tr_done[jj % 2] = v_fm
            ctx.wait(G, "v", v_fm)
            ctx.dma(G, f"fm{fmslot}", B.ag_in[j * 128:(j + 1) * 128, i * 128:(i + 1) * 128], fmbuf)
        xn_rd_t[bsl] = ctx.cnt["t"]
    c_ln_v = ctx.cnt["v"]

    fmb = [B.arC[:, 512:512 + 2048], B.arC[:, 2560:2560 + 2048]]
    zt = B.arB[:, 6144:6208]
    ez = B.arB[:, 6208:6272]
    junk = B.arB[:, 6272:6336]
    zmax = B.arD[0:128, 44:45]
    sez = B.arD[0:128, 45:46]
    pz = B.arD[0:128, 46:47]
    lse = B.arD[0:128, 47:48]
    isz = B.arD[0:128, 48:49]
    for fs in range(4):
        ctx.waitd(Y, f"fm{fs}")
    eb_free = [0, 0]
    pb1_free = 0
    for i in range(NTT if _PH1 >= 3 else 0):
        eb = fmb[i % 2]
        ctx.wait(Y, "t", eb_free[i % 2])
        ctx.dma(Y, f"eb{i % 2}", eb.rearrange("p (c t) -> p c t", t=128),
                B.ag_in[0:D, i * 128:(i + 1) * 128].rearrange("(c p) t -> p c t", p=128))
        ctx.waitd(T, f"eb{i % 2}")
        ctx.wait(T, "v", pb1_free)
        tmk = 0
        for c in range(16):
            tmk = ctx.mark("t", T.matmul(B.pb1[:, 0:64], eb[:, c * 128:(c + 1) * 128],
                                         B.unc_t[:, c * 64:(c + 1) * 64],
                                         start=(c == 0), stop=(c == 15)))
        eb_free[i % 2] = tmk
        ctx.wait(V, "t", tmk)
        ctx.v("reduce_max", zmax, B.pb1[:, 0:64], axis=AX.X)
        v_z = ctx.v("tensor_scalar", zt, B.pb1[:, 0:64], zmax, None, ALU.subtract)
        pb1_free = v_z
        ctx.wait(S, "v", v_z)
        s_e = ctx.s("activation", ez, zt, AF.Exp, accum_out=sez)
        ctx.wait(V, "s", s_e)
        ctx.v("scalar_tensor_tensor", junk, ez, 1.0, zt, ALU.mult, ALU.mult,
              accum_out=pz)
        v_p = ctx.cnt["v"]
        ctx.wait(S, "v", v_p)
        s_l = ctx.s("activation", lse, sez, AF.Ln)
        ctx.wait(V, "s", s_l)
        ctx.v("reciprocal", isz, sez)
        ctx.v("tensor_tensor", pz, pz, isz, ALU.mult)
        ctx.v("tensor_sub", lse, lse, pz)
        ctx.v("tensor_scalar", B.stat_t[:, 4 * i + 3:4 * i + 4], lse,
              1.0 / LOGBINS, None, ALU.mult)
    if _PH1 < 4:
        return
    ctx.wait(T, "v", ctx.cnt["v"])
    entcols = B.stat_t[:].rearrange("p (i f) -> p i f", f=4)[:, :, 3]
    te = ctx.mark("t", T.transpose(B.pT2[0:NTT, 0:128], entcols, B.identf_t[:]))
    ctx.wait(V, "t", te)
    for fs in range(4):
        ctx.waitd(V, f"fm{fs}")
    erow = B.arC[0:NTT, 0:128]
    v_er = ctx.v("tensor_copy", erow, B.pT2[0:NTT, 0:128])
    ctx.wait(G, "v", v_er)
    ctx.dma(G, "agent", B.ag_in[D:D + 1, :].rearrange("r (i t) -> (r i) t", t=128), erow)


def _ph4_scan(ctx, B):
    T, V, S, G, Y = ctx.T, ctx.V, ctx.S, ctx.G, ctx.Y
    NWIN = B.NWIN
    TPC = B.TPC
    PB = [B.pb0, B.pb1, B.pb2, B.pb3, B.pb4]

    xnc = B.arC[:, 5632:13824]
    qkv = B.arA[:, 8192:11264]
    vhat = B.arA[:, 11264:12288]
    HB = [0, 12288]
    O_M8I, O_M8O, O_S8I, O_E8O, O_LAM, O_ETL = 0, 512, 1024, 1536, 2048, 2560
    O_D16I, O_D16O, O_D32I, O_D32O, O_D64I, O_D64O = 3072, 3328, 3840, 4096, 4352, 4608
    O_QM8, O_QS8, O_QTL, O_Q16, O_Q32, O_Q64 = 4864, 5376, 5888, 6400, 6656, 6912

    vtm = [B.arC[:, 0:128], B.arC[:, 128:256]]
    CSC = 512
    def csc(ip, k):
        return B.arC[:, CSC + ip * 896 + k * 128:CSC + ip * 896 + (k + 1) * 128]
    ofm = [B.arC[:, 2304:2816], B.arC[:, 2816:3328]]
    selt = B.arC[:, 3328:3456]
    ftm = B.arC[:, 3456:3712]
    gfm = B.arC[:, 3712:3968]
    hf1 = B.arC[:, 3968:4096]
    hf2 = B.arC[:, 4096:4224]
    ffm = B.arC[:, 4224:4480]
    sqt = [B.arC[:, 4480:4992], B.arC[:, 4992:5504]]
    entt = B.arC[:, 5504:5632]

    PA = [B.arB[:, i * 520:(i + 1) * 520] for i in range(4)]
    esc = [B.arB[:, 2080:2592], B.arB[:, 2592:3104]]
    convin = [B.arB[:, 3104:3620], B.arB[:, 3620:4136]]
    convacc = [B.arB[:, 4136:4648], B.arB[:, 4648:5160]]
    aspc = B.arB[:, 5160:5672]
    arow = B.arB[:, 5672:6184]
    omx = B.arB[:, 6184:6312]
    ot1 = B.arB[:, 6312:6440]
    ot2 = B.arB[:, 6440:6568]
    krow = B.arB[:, 6568:7592]
    ojnk = B.arD[:, 2354:2482]
    gtm = B.arD[:, 2048:2304]
    nbet = B.arD[:, 2304:2320]
    fsc = B.arD[:, 2320:2336]
    sqc = B.arD[0:128, 2336:2337]
    rcl = B.arD[0:128, 2337:2338]
    ampc = B.arD[:, 2338:2354]
    ebuf = [esc[0], esc[1], convacc[0], convacc[1]]

    for e in (T, V, S, G, Y):
        ctx.wait(e, "cc", B.c_ag)
    ctx.wait(V, "t", B.c_ffn_done[0])
    ctx.wait(S, "t", B.c_ffn_done[0])
    ctx.wait(Y, "t", B.c_ffn_done[0])
    ctx.wait(T, "s", B.c_ffn_done[2])
    ctx.wait(T, "v", B.c_ffn_done[1])
    ctx.waitd(T, "wts", B.c_wts)
    ctx.waitd(V, "wts", B.c_wts)
    ctx.waitd(T, "rep", B.c_rep)
    ctx.waitd(S, "rep", B.c_rep)
    for e in (V, S, Y):
        ctx.waitd(e, "gf0")
        ctx.waitd(e, "gf1")
    ctx.wait(Y, "v", ctx.cnt["v"])
    ctx.wait(Y, "s", ctx.cnt["s"])
    ctx.wait(G, "v", ctx.cnt["v"])
    ctx.v("memset", B.Sf_t[:], 0.0)
    ctx.v("memset", B.Sb_t[:], 0.0)
    ctx.v("memset", B.carry_t[:], 0.0)
    ctx.v("memset", ftm, 0.0)
    ctx.v("memset", gfm, 0.0)
    ctx.v("memset", aspc, 0.0)
    v_init = ctx.cnt["v"]

    t_prev_end = B.c_ffn_done[0]
    v_prev_end = v_init

    def xn_load(wn, tdep):
        sc = (wn * 512) // TPC
        scol = (wn * 512) % TPC
        ctx.wait(Y, "t", tdep)
        for qq in range(4):
            ctx.dma(Y, f"xw{qq}",
                    xnc[:, qq * 2048:(qq + 1) * 2048].rearrange("p (c t) -> p c t", t=512),
                    B.ag_out[sc * B.AGROW + qq * 512:sc * B.AGROW + (qq + 1) * 512,
                             scol:scol + 512].rearrange("(c p) t -> p c t", p=128))

    xn_load(0, B.c_ffn_done[0])

    for w in range(NWIN):
        wsl = w % 2
        src_core = (w * 512) // TPC
        src_col = (w * 512) % TPC
        ctx.wait(Y, "v", v_prev_end)
        ctx.dma(Y, "ew", B.arC[0:4, 5504:5632],
                B.ag_out[src_core * B.AGROW + D:src_core * B.AGROW + D + 1,
                         src_col:src_col + 512].rearrange("r (c p) -> (r c) p", p=128))
        ctx.wait(T, "v", v_prev_end)
        for qq in range(4):
            ctx.waitd(T, f"xw{qq}")
        s_conv = [0, 0]
        v_conv = [0, 0]
        for jt in range(6):
            j, h = jt // 2, jt % 2
            pb = PB[jt % 2]
            wt = [B.wq_t, B.wk_t, B.wv_t][j]
            ctx.wait(T, "v", v_conv[jt % 2])
            tm = 0
            for ic in range(16):
                tm = ctx.mark("t", T.matmul(pb[:, 0:512],
                                            wt[:, ic * 256 + h * 128:ic * 256 + (h + 1) * 128],
                                            xnc[:, ic * 512:(ic + 1) * 512],
                                            start=(ic == 0), stop=(ic == 15)))
            sl = jt % 2
            ctx.wait(V, "t", tm)
            ctx.wait(V, "s", s_conv[sl])
            ctx.v("tensor_copy", convin[sl][:, 3:515], pb[:, 0:512])
            ctx.v("tensor_copy", convin[sl][:, 0:3],
                  B.carry_t[:, (j * HPC + h) * 3:(j * HPC + h) * 3 + 3])
            wc = B.convw[:, (j * HPC + h) * 4:(j * HPC + h) * 4 + 4]
            ctx.v("tensor_scalar", convacc[sl], convin[sl][:, 0:512], wc[:, 0:1],
                  None, ALU.mult)
            for tau in range(1, 4):
                ctx.v("scalar_tensor_tensor", convacc[sl], convin[sl][:, tau:tau + 512],
                      wc[:, tau:tau + 1], convacc[sl], ALU.mult, ALU.add)
            v_conv[sl] = ctx.v("tensor_copy", B.carry_t[:, (j * HPC + h) * 3:(j * HPC + h) * 3 + 3],
                               convin[sl][:, 512:515])
            ctx.wait(S, "v", v_conv[sl])
            s_conv[sl] = ctx.s("activation", qkv[:, jt * 512:(jt + 1) * 512],
                               convacc[sl], AF.Silu)
        if w + 1 < NWIN:
            xn_load(w + 1, tm)
        ctx.wait(T, "s", max(s_conv))
        ctx.wait(V, "s", max(s_conv))
        ctx.waitd(T, "ew")
        tent = ctx.mark("t", T.transpose(B.pT1a[:, 896:900], B.arC[0:4, 5504:5632],
                                         B.ident_t[0:4, 0:4]))
        ctx.wait(V, "t", tent)
        ctx.v("tensor_copy", entt[:, 0:4], B.pT1a[:, 896:900])
        nps = PB[2]
        t_vh = [0, 0]
        for h in range(HPC):
            kraw = qkv[:, (2 + h) * 512:(3 + h) * 512]
            vraw = qkv[:, (4 + h) * 512:(5 + h) * 512]
            tm = ctx.mark("t", T.matmul(PB[3][:, 0:512], B.proxy_t[:], kraw,
                                        start=True, stop=True))
            ctx.wait(V, "t", tm)
            ctx.v("tensor_copy", vhat[:, h * 512:(h + 1) * 512], PB[3][:, 0:512])
            krow_h = B.arB[0:1, 6568 + h * 512:6568 + (h + 1) * 512]
            ctx.wait(S, "v", ctx.cnt["v"])
            s_k2 = ctx.s("activation", sqt[0], kraw, AF.Square)
            ctx.wait(T, "s", s_k2)
            tm = ctx.mark("t", T.matmul(PB[3][0:1, 0:512],
                                        B.onescb_t[:], sqt[0], start=True, stop=True))
            ctx.wait(S, "t", tm)
            ctx.wait(S, "v", ctx.cnt["v"])
            s_sq = ctx.s("activation", krow_h, PB[3][0:1, 0:512], AF.Sqrt)
            ctx.wait(V, "s", s_sq)
            ctx.v("tensor_scalar", krow_h, krow_h, EPS, None, ALU.add)
            v_kr = ctx.v("reciprocal", krow_h, krow_h)
            ctx.wait(T, "v", v_kr)
            tm = ctx.mark("t", T.matmul(PB[3][:, 0:512], B.ones1f_t[0:1, :],
                                        krow_h, start=True, stop=True))
            ctx.wait(V, "t", tm)
            ctx.v("tensor_tensor", kraw, kraw, PB[3][:, 0:512], ALU.mult)
            ctx.wait(S, "v", ctx.cnt["v"])
            s_v2 = ctx.s("activation", sqt[0], vraw, AF.Square)
            ctx.wait(T, "s", s_v2)
            tm0 = 0
            for c in range(4):
                tm0 = ctx.mark("t", T.matmul(nps[:, c * 8 + h * 4:c * 8 + h * 4 + 1],
                                             sqt[0][:, c * 128:(c + 1) * 128],
                                             B.onescb_t[:], start=True, stop=True))
            s_h2 = ctx.s("activation", sqt[1], vhat[:, h * 512:(h + 1) * 512], AF.Square)
            ctx.wait(T, "s", s_h2)
            for c in range(4):
                tm0 = ctx.mark("t", T.matmul(nps[:, c * 8 + h * 4 + 1:c * 8 + h * 4 + 2],
                                             sqt[1][:, c * 128:(c + 1) * 128],
                                             B.onescb_t[:], start=True, stop=True))
            ctx.wait(V, "t", tm0)
            v_vv = ctx.v("tensor_tensor", sqt[0], vraw, vhat[:, h * 512:(h + 1) * 512],
                         ALU.mult)
            ctx.wait(T, "v", v_vv)
            for c in range(4):
                tm0 = ctx.mark("t", T.matmul(nps[:, c * 8 + h * 4 + 2:c * 8 + h * 4 + 3],
                                             sqt[0][:, c * 128:(c + 1) * 128],
                                             B.onescb_t[:], start=True, stop=True))
            t_vh[h] = tm0
        ctx.wait(V, "t", max(t_vh))
        nsc = B.arD[:, 2482:2514]
        ctx.v("tensor_copy", nsc, nps[:, 0:32])
        v2v = nsc.rearrange("p (c h t) -> p (c h) t", h=2, t=4)[:, :, 0]
        vh2v = nsc.rearrange("p (c h t) -> p (c h) t", h=2, t=4)[:, :, 1]
        vvhv = nsc.rearrange("p (c h t) -> p (c h) t", h=2, t=4)[:, :, 2]
        f_out = lambda f: ftm[:].rearrange("p (i q) -> p i q", q=32)[:, :, f]
        t1 = fsc[:, 0:8]
        t2 = fsc[:, 8:16]
        ctx.v("tensor_copy",
              ftm[:].rearrange("p (c h q) -> p c h q", h=2, q=32)[:, :, :, 0],
              entt[:, 0:4].unsqueeze(2).to_broadcast([128, 4, 2]))
        ctx.v("tensor_add", t1, v2v, vh2v)
        ctx.v("scalar_tensor_tensor", t1, vvhv, -2.0, t1, ALU.mult, ALU.add)
        ctx.v("tensor_scalar", t1, t1, 0.0, None, ALU.max)
        v_e2 = ctx.cnt["v"]
        ctx.wait(S, "v", v_e2)
        s_en = ctx.s("activation", t1, t1, AF.Sqrt)
        s_vn = ctx.s("activation", t2, v2v, AF.Sqrt)
        ctx.wait(V, "s", s_vn)
        ctx.v("tensor_scalar", t2, t2, EPS, None, ALU.add)
        ctx.v("reciprocal", t2, t2)
        ctx.v("tensor_tensor", f_out(1), t1, t2, ALU.mult)
        ctx.wait(S, "v", ctx.cnt["v"])
        s_l1 = ctx.s("activation", f_out(3), t1, AF.Ln, bias=1.0)
        s_vn2 = ctx.s("activation", t1, v2v, AF.Sqrt)
        s_hn = ctx.s("activation", t2, vh2v, AF.Sqrt)
        ctx.wait(V, "s", s_hn)
        ctx.v("tensor_tensor", t1, t1, t2, ALU.mult)
        ctx.v("tensor_scalar", t1, t1, EPS, None, ALU.add)
        ctx.v("reciprocal", t1, t1)
        ctx.v("tensor_tensor", f_out(2), vvhv, t1, ALU.mult)
        v_fe = ctx.v("tensor_copy",
                     ftm[:].rearrange("p (c h q) -> p c h q", h=2, q=32)[:, :, :, 4:8],
                     B.hebc_t[:, 0:8].rearrange("p (h f) -> p h f", f=4)
                         .unsqueeze(1).to_broadcast([128, 4, 2, 4]))
        s_gate = 0
        for ti in range(2):
            ctx.wait(T, "v", v_fe)
            ctx.wait(T, "s", s_gate)
            tt1 = ctx.mark("t", T.transpose(B.pT1a[:, 0:128] if ti == 0 else B.pT1b[:, 0:128],
                                            ftm[:, ti * 128:(ti + 1) * 128], B.ident_t[:]))
            ctx.wait(V, "t", tt1)
            v_ff = ctx.v("tensor_copy", ffm[:, ti * 128:(ti + 1) * 128],
                         (B.pT1a if ti == 0 else B.pT1b)[:, 0:128])
            ctx.wait(T, "v", v_ff)
            tmm = ctx.mark("t", T.matmul(PB[3][:, 0:128], B.mlp1_t[:],
                                         ffm[:, ti * 128:(ti + 1) * 128],
                                         start=True, stop=True))
            ctx.wait(S, "t", tmm)
            ctx.wait(S, "v", ctx.cnt["v"])
            s_h1 = ctx.s("activation", hf1, PB[3][:, 0:128], AF.Silu, bias=B.b1rep_t[:, 0:1])
            ctx.wait(T, "s", s_h1)
            tmm = ctx.mark("t", T.matmul(PB[4][:, 0:128], B.mlp2_t[:], hf1,
                                         start=True, stop=True))
            ctx.wait(S, "t", tmm)
            s_h2b = ctx.s("activation", hf2, PB[4][:, 0:128], AF.Silu, bias=B.b2rep_t[:, 0:1])
            ctx.wait(T, "s", s_h2b)
            tmm = ctx.mark("t", T.matmul(PB[3][:, 128:256], B.gaug_t[:], hf2,
                                         start=True, stop=True))
            for q in range(4):
                ctx.wait(S, "t", tmm)
                s_gate = ctx.s("activation", gfm[:, ti * 128:(ti + 1) * 128][32 * q:32 * q + 5, :],
                               PB[3][32 * q:32 * q + 5, 128:256], AF.Sigmoid,
                               bias=B.baugc_t[32 * q:32 * q + 5, 0:1])
        for ti in range(2):
            ctx.wait(T, "s", s_gate)
            tg = ctx.mark("t", T.transpose(B.pT1a[:, 128:256] if ti == 0 else B.pT1b[:, 128:256],
                                           gfm[:, ti * 128:(ti + 1) * 128], B.ident_t[:]))
            ctx.wait(V, "t", tg)
            ctx.v("tensor_copy", gtm[:, ti * 128:(ti + 1) * 128],
                  (B.pT1a if ti == 0 else B.pT1b)[:, 128:256])
        ctx.v("tensor_scalar", nbet.rearrange("p (i q) -> p i q", q=2),
              gtm[:].rearrange("p (i q) -> p i q", q=32)[:, :, 0:2],
              -1.0, None, ALU.mult)
        ctx.v("tensor_copy", ampc.rearrange("p (i q) -> p i q", q=2),
              gtm[:].rearrange("p (i q) -> p i q", q=32)[:, :, 3:5])
        ctx.wait(T, "v", ctx.cnt["v"])
        ta = ctx.mark("t", T.matmul(PB[3][:, 0:16], B.trilIf_t[:], ampc,
                                    start=True, stop=True))
        ctx.wait(V, "t", ta)
        for hr in range(4):
            ctx.v("tensor_copy", aspc[:, hr * 128:(hr + 1) * 128].rearrange("p (c z) -> p c z", z=32)[:, :, 0:1],
                  PB[3][:, 0:16].rearrange("p (c q) -> p c q", q=4)[:, :, hr:hr + 1])
        for hr in range(4):
            ctx.wait(T, "v", ctx.cnt["v"])
            tt2 = ctx.mark("t", T.transpose(B.pT2[:, 0:128], aspc[:, hr * 128:(hr + 1) * 128],
                                            B.identf_t[:]))
            ctx.wait(V, "t", tt2)
            ctx.v("tensor_copy", arow[:, hr * 128:(hr + 1) * 128], B.pT2[:, 0:128])
        for hr in range(4):
            ctx.v("memset", PA[hr][:, 0:520].rearrange("p (c z) -> p c z", z=130)[:, :, 0:1], 0.0)
        for hr in range(4):
            for cc in range(4):
                ctx.wait(T, "v", ctx.cnt["v"])
                rb, rk = (64, 33) if cc == 3 else (32 * cc, 1)
                lsel = B.sel96f_t if cc == 3 else B.onesff_t
                tb = ctx.mark("t", T.matmul(
                    B.pT2[:, 128:256], lsel[rb:rb + rk, :],
                    arow[rb:rb + rk, hr * 128:(hr + 1) * 128],
                    start=True, stop=True))
                ctx.wait(V, "t", tb)
                ctx.v("tensor_copy", PA[hr][:, cc * 130 + 1:cc * 130 + 129], B.pT2[:, 128:256])
        v_pa = ctx.cnt["v"]
        _scan_pair_loop(ctx, B, locals())
        t_prev_end = ctx.cnt["t"]
        v_prev_end = ctx.cnt["v"]
    B.c_scan = (ctx.cnt["t"], ctx.cnt["v"], ctx.cnt["s"], ctx.cnt["g"])


def _scan_pair_loop(ctx, B, L):
    T, V, S, G, Y = ctx.T, ctx.V, ctx.S, ctx.G, ctx.Y
    qkv = L["qkv"]
    w, wsl = L["w"], L["wsl"]
    HB, PA = L["HB"], L["PA"]
    vtm, ofm, selt = L["vtm"], L["ofm"], L["selt"]
    gtm, nbet = L["gtm"], L["nbet"]
    omx, ot1, ot2, ojnk = L["omx"], L["ot1"], L["ot2"], L["ojnk"]
    csc = L["csc"]
    sqc, rcl = L["sqc"], L["rcl"]
    ebuf = L["ebuf"]
    v_pa = L["v_pa"]
    PB = [B.pb0, B.pb1, B.pb2, B.pb3, B.pb4]
    TPC = B.TPC
    (O_M8I, O_M8O, O_S8I, O_E8O, O_LAM, O_ETL, O_D16I, O_D16O, O_D32I,
     O_D32O, O_D64I, O_D64O, O_QM8, O_QS8, O_QTL, O_Q16, O_Q32, O_Q64) = (
        L["O_M8I"], L["O_M8O"], L["O_S8I"], L["O_E8O"], L["O_LAM"], L["O_ETL"],
        L["O_D16I"], L["O_D16O"], L["O_D32I"], L["O_D32O"], L["O_D64I"],
        L["O_D64O"], L["O_QM8"], L["O_QS8"], L["O_QTL"], L["O_Q16"],
        L["O_Q32"], L["O_Q64"])
    LEVS = ((16, O_D16I, O_D16O, O_Q16), (32, O_D32I, O_D32O, O_Q32),
            (64, O_D64I, O_D64O, O_Q64))

    def slab(ip, off, width=512):
        return B.arA[:, HB[ip] + off:HB[ip] + off + width]

    est = {"i": 0, "marks": [0, 0, 0, 0]}

    def enext():
        i = est["i"]
        est["i"] = (i + 1) % 4
        if est["marks"][i]:
            ctx.wait(V, "s", est["marks"][i])
        return ebuf[i], i

    def vexp(dst, src, idx):
        ctx.wait(S, "v")
        est["marks"][idx] = ctx.s("activation", dst, src, AF.Exp)

    bst = {"n": 0, "free": [0, 0]}

    def tpose(dst, src, src_v=0):
        b = bst["n"] % 2
        bst["n"] += 1
        bk = [B.pT1a, B.pT1b][b]
        ctx.wait(T, "v", max(bst["free"][b], src_v))
        tm = ctx.mark("t", T.transpose(bk[:, 384:512], src, B.ident_t[:]))
        ctx.wait(V, "t", tm)
        vm = ctx.v("tensor_copy", dst, bk[:, 384:512])
        bst["free"][b] = vm
        return vm

    pf = [0, 0]

    for head in range(HPC):
        gc = B.gcol_t[:, head:head + 1]
        ngc = B.ngcol_t[:, head:head + 1]
        for ip in range(2):
            hr = head * 2 + ip
            pa = PA[hr]
            paz = pa[:].rearrange("p (c z) -> p c z", z=130)
            Av = paz[:, :, 1:129]
            Av4 = Av.rearrange("p c (b q) -> p c b q", q=8)
            As8 = paz[:, :, 0:121:8].unsqueeze(3).to_broadcast([128, 4, 16, 8])
            Ae8 = paz[:, :, 8:129:8].unsqueeze(3).to_broadcast([128, 4, 16, 8])
            K = qkv[:, (2 + head) * 512:(3 + head) * 512]
            Q = qkv[:, (0 + head) * 512:(1 + head) * 512]
            ctx.wait(V, "t", ctx.cnt["t"])
            e0, i0 = enext()
            e0c = e0[:].rearrange("p (c t) -> p c t", t=128)
            e0v = e0[:].rearrange("p (c b q) -> p c b q", b=16, q=8)
            ctx.v("tensor_tensor", e0v, As8, Ae8, ALU.add)
            ctx.v("scalar_tensor_tensor", e0c, e0c, -0.5, Av, ALU.mult, ALU.add)
            eA, iA = enext()
            ctx.v("tensor_scalar", eA, e0, gc, CLAMP, ALU.mult, ALU.min)
            vexp(slab(ip, O_M8I), eA, iA)
            eB, iB = enext()
            ctx.v("tensor_scalar", eB, e0, ngc, CLAMP, ALU.mult, ALU.min)
            vexp(slab(ip, O_M8O), eB, iB)
            eC, iC = enext()
            eCv = eC[:].rearrange("p (c b q) -> p c b q", b=16, q=8)
            ctx.v("tensor_tensor", eCv, Av4, As8, ALU.subtract)
            ctx.v("tensor_scalar", eC, eC, gc, None, ALU.mult)
            vexp(slab(ip, O_S8I), eC, iC)
            eD, iD = enext()
            eDv = eD[:].rearrange("p (c b q) -> p c b q", b=16, q=8)
            ctx.v("tensor_tensor", eDv, Ae8, Av4, ALU.subtract)
            ctx.v("tensor_scalar", eD, eD, gc, None, ALU.mult)
            vexp(slab(ip, O_E8O), eD, iD)
            eE, iE = enext()
            eEc = eE[:].rearrange("p (c t) -> p c t", t=128)
            ctx.v("tensor_scalar", eEc, Av, gc, None, ALU.mult)
            vexp(slab(ip, O_LAM), eE, iE)
            A127 = paz[:, :, 128:129].to_broadcast([128, 4, 128])
            eF, iF = enext()
            eFc = eF[:].rearrange("p (c t) -> p c t", t=128)
            ctx.v("tensor_tensor", eFc, A127, Av, ALU.subtract)
            ctx.v("tensor_scalar", eF, eF, gc, None, ALU.mult)
            vexp(slab(ip, O_ETL), eF, iF)
            for (LV, oin, oout, oq) in LEVS:
                nb = 128 // (2 * LV)
                cwid = nb * LV
                eg, ig = enext()
                egv = eg[:].rearrange("p (c m q) -> p c m q", m=nb, q=2 * LV)
                Bnd = paz[:, :, LV:LV + 2 * LV * (nb - 1) + 1:2 * LV]\
                    .unsqueeze(3).to_broadcast([128, 4, nb, 2 * LV])
                AvL = Av.rearrange("p c (m q) -> p c m q", q=2 * LV)
                ctx.v("tensor_tensor", egv, AvL, Bnd, ALU.subtract)
                eh, ih = enext()
                ctx.v("tensor_scalar", eh, eg, gc, CLAMP, ALU.mult, ALU.min)
                ehv = eh[:].rearrange("p (c m q) -> p c m q", m=nb, q=2 * LV)
                vexp(slab(ip, oin, 4 * cwid).rearrange("p (c m q) -> p c m q",
                                                       m=nb, q=LV),
                     ehv[:, :, :, LV:2 * LV], ih)
                ei_, ii_ = enext()
                ctx.v("tensor_scalar", ei_, eg, ngc, CLAMP, ALU.mult, ALU.min)
                eiv = ei_[:].rearrange("p (c m q) -> p c m q", m=nb, q=2 * LV)
                if LV == 16:
                    vexp(slab(ip, oout, 512), ei_, ii_)
                else:
                    vexp(slab(ip, oout, 4 * cwid).rearrange("p (c m q) -> p c m q",
                                                            m=nb, q=LV),
                         eiv[:, :, :, 0:LV], ii_)
            ctx.wait(V, "s")
            ctx.v("scalar_tensor_tensor", slab(ip, O_QM8), Q, QS,
                  slab(ip, O_M8I), ALU.mult, ALU.mult)
            ctx.v("scalar_tensor_tensor", slab(ip, O_QS8), Q, QS,
                  slab(ip, O_S8I), ALU.mult, ALU.mult)
            ctx.v("scalar_tensor_tensor", slab(ip, O_QTL), Q, QS,
                  slab(ip, O_LAM), ALU.mult, ALU.mult)
            for (LV, oin, oout, oq) in LEVS:
                nb = 128 // (2 * LV)
                cwid = nb * LV
                Qh = Q[:].rearrange("p (c m q) -> p c m q", m=nb, q=2 * LV)[:, :, :, LV:2 * LV]
                ctx.v("scalar_tensor_tensor",
                      slab(ip, oq, 4 * cwid).rearrange("p (c m q) -> p c m q", m=nb, q=LV),
                      Qh, QS,
                      slab(ip, oin, 4 * cwid).rearrange("p (c m q) -> p c m q", m=nb, q=LV),
                      ALU.mult, ALU.mult)
            for off in (O_M8I, O_M8O, O_S8I, O_E8O, O_LAM, O_ETL):
                ctx.v("tensor_tensor", slab(ip, off), K, slab(ip, off), ALU.mult)
            for (LV, oin, oout, oq) in LEVS:
                nb = 128 // (2 * LV)
                cwid = nb * LV
                Kin = K[:].rearrange("p (c m q) -> p c m q", m=nb, q=2 * LV)[:, :, :, LV:2 * LV]
                Kout = K[:].rearrange("p (c m q) -> p c m q", m=nb, q=2 * LV)[:, :, :, 0:LV]
                ctx.v("tensor_tensor",
                      slab(ip, oin, 4 * cwid).rearrange("p (c m q) -> p c m q", m=nb, q=LV),
                      Kin,
                      slab(ip, oin, 4 * cwid).rearrange("p (c m q) -> p c m q", m=nb, q=LV),
                      ALU.mult)
                if LV == 16:
                    ctx.v("tensor_tensor", slab(ip, oout, 512), K,
                          slab(ip, oout, 512), ALU.mult)
                else:
                    ctx.v("tensor_tensor",
                          slab(ip, oout, 4 * cwid).rearrange("p (c m q) -> p c m q", m=nb, q=LV),
                          Kout,
                          slab(ip, oout, 4 * cwid).rearrange("p (c m q) -> p c m q", m=nb, q=LV),
                          ALU.mult)

        vb = ctx.cnt["v"]

        for cc in range(4):
            csl = cc % 2
            sl = slice(cc * 128, (cc + 1) * 128)
            v_vtm = tpose(vtm[csl], qkv[:, (4 + head) * 512 + cc * 128:
                                        (4 + head) * 512 + (cc + 1) * 128])
            tmk = {}
            for ip in range(2):
                pbp = PB[ip]
                ctx.wait(T, "v", max(pf[ip], vb))
                ctx.mark("t", T.matmul(pbp[:, 0:128], slab(ip, O_E8O)[:, sl],
                                       slab(ip, O_S8I)[:, sl], start=True, stop=True))
                ctx.mark("t", T.matmul(pbp[:, 256:384], slab(ip, O_E8O)[:, sl],
                                       slab(ip, O_QS8)[:, sl], start=True, stop=True))
                ctx.mark("t", T.matmul(pbp[:, 128:256], slab(ip, O_M8O)[:, sl],
                                       slab(ip, O_M8I)[:, sl], start=True, stop=True))
                ctx.mark("t", T.matmul(pbp[:, 384:512], slab(ip, O_M8O)[:, sl],
                                       slab(ip, O_QM8)[:, sl], start=True, stop=True))
                for (LV, oin, oout, oq) in LEVS:
                    nb = 128 // (2 * LV)
                    cwid = nb * LV
                    for m in range(nb):
                        j0 = 2 * LV * m
                        i0 = j0 + LV
                        cb = cc * cwid + m * LV
                        if LV == 16:
                            jb = 64 if m == 3 else j0
                            lt = slab(ip, oout, 512)[:, cc * 128 + jb:cc * 128 + j0 + LV]
                        else:
                            jb = j0
                            lt = slab(ip, oout, 4 * cwid)[:, cb:cb + LV]
                        ctx.mark("t", T.matmul(
                            pbp[jb:j0 + LV, i0:i0 + LV], lt,
                            slab(ip, oin, 4 * cwid)[:, cb:cb + LV],
                            start=True, stop=True))
                        tmk[ip] = ctx.mark("t", T.matmul(
                            pbp[jb:j0 + LV, 256 + i0:256 + i0 + LV], lt,
                            slab(ip, oq, 4 * cwid)[:, cb:cb + LV],
                            start=True, stop=True))
            for ip in range(2):
                hr = head * 2 + ip
                pbp = PB[ip]
                nb_col = nbet[:, cc * 4 + hr:cc * 4 + hr + 1]
                b_col = gtm[:, (cc * 2 + head) * 32 + ip:(cc * 2 + head) * 32 + ip + 1]
                ctx.wait(V, "t", tmk[ip])
                ctx.v("tensor_copy", selt, pbp[:, 0:128])
                ctx.v("copy_predicated", selt, B.bd8m_t[:], pbp[:, 128:256])
                ctx.v("scalar_tensor_tensor", csc(ip, 0), selt, nb_col,
                      B.trilS_t[:], ALU.mult, ALU.mult)
                ctx.v("tensor_copy", selt, pbp[:, 256:384])
                ctx.v("copy_predicated", selt, B.bd8m_t[:], pbp[:, 384:512])
                pf[ip] = ctx.v("scalar_tensor_tensor", csc(ip, 1), selt, b_col,
                               B.trilI_t[:], ALU.mult, ALU.mult)
            tR = {}
            for ip in range(2):
                hr = head * 2 + ip
                ctx.wait(T, "v", pf[ip])
                tR[ip] = ctx.mark("t", T.matmul(
                    PB[3][:, 256 + ip * 128:384 + ip * 128],
                    slab(ip, O_LAM)[:, sl], B.Sb_t[:, hr * 128:(hr + 1) * 128],
                    start=True, stop=True))
            for ip in range(2):
                tpose(csc(ip, 2), csc(ip, 0))
                ctx.v("tensor_copy", csc(ip, 3), csc(ip, 0))
                ctx.wait(V, "t", tR[ip])
                ctx.v("tensor_tensor", csc(ip, 4), vtm[csl],
                      PB[3][:, 256 + ip * 128:384 + ip * 128], ALU.subtract)
            for k in range(7):
                ctx.wait(T, "v", ctx.cnt["v"])
                tap = {}
                for ip in range(2):
                    tap[ip] = ctx.mark("t", T.matmul(
                        PB[3][:, ip * 128:(ip + 1) * 128], csc(ip, 3), csc(ip, 4),
                        start=True, stop=True))
                tsq = {}
                if k < 6:
                    for ip in range(2):
                        ctx.mark("t", T.matmul(
                            PB[2][:, ip * 256:ip * 256 + 128], csc(ip, 3), csc(ip, 2),
                            start=True, stop=True))
                        tsq[ip] = ctx.mark("t", T.matmul(
                            PB[2][:, ip * 256 + 128:ip * 256 + 256], csc(ip, 2),
                            csc(ip, 3), start=True, stop=True))
                for ip in range(2):
                    ctx.wait(V, "t", tap[ip])
                    ctx.v("tensor_tensor", csc(ip, 4), csc(ip, 4),
                          PB[3][:, ip * 128:(ip + 1) * 128], ALU.add)
                if k < 6:
                    for ip in range(2):
                        ctx.wait(V, "t", tsq[ip])
                        ctx.v("tensor_copy", csc(ip, 2), PB[2][:, ip * 256:ip * 256 + 128])
                        ctx.v("tensor_copy", csc(ip, 3),
                              PB[2][:, ip * 256 + 128:ip * 256 + 256])
            for ip in range(2):
                hr = head * 2 + ip
                b_col = gtm[:, (cc * 2 + head) * 32 + ip:(cc * 2 + head) * 32 + ip + 1]
                ctx.wait(T, "v", ctx.cnt["v"])
                ctx.mark("t", T.matmul(PB[4][:, ip * 128:(ip + 1) * 128],
                                       slab(ip, O_QTL)[:, sl],
                                       B.Sb_t[:, hr * 128:(hr + 1) * 128],
                                       start=True, stop=False))
                tgu = ctx.mark("t", T.matmul(PB[4][:, ip * 128:(ip + 1) * 128],
                                             csc(ip, 1), csc(ip, 4),
                                             start=False, stop=True))
                ctx.v("tensor_scalar", csc(ip, 5), csc(ip, 4), b_col, None, ALU.mult)
                tpose(csc(ip, 6), slab(ip, O_ETL)[:, sl])
                ctx.wait(T, "v", ctx.cnt["v"])
                tsu = ctx.mark("t", T.matmul(PB[4][:, 256 + ip * 128:384 + ip * 128],
                                             csc(ip, 6), csc(ip, 5),
                                             start=True, stop=True))
                ctx.wait(S, "v", v_pa)
                slc = ctx.s("activation", B.lamc_t[:, hr:hr + 1],
                            PA[hr][:, cc * 130 + 128:cc * 130 + 129], AF.Exp,
                            scale=gc)
                ctx.wait(V, "s", slc)
                ctx.wait(V, "t", tsu)
                ctx.v("scalar_tensor_tensor", B.Sf_t[:, hr * 128:(hr + 1) * 128],
                      B.Sf_t[:, hr * 128:(hr + 1) * 128], B.lamc_t[:, hr:hr + 1],
                      PB[4][:, 256 + ip * 128:384 + ip * 128], ALU.mult, ALU.add)
                ctx.v("tensor_copy", B.Sb_t[:, hr * 128:(hr + 1) * 128],
                      B.Sf_t[:, hr * 128:(hr + 1) * 128])
            lam_col = gtm[:, (cc * 2 + head) * 32 + 2:(cc * 2 + head) * 32 + 3]
            ctx.wait(V, "t", ctx.cnt["t"])
            ctx.v("tensor_scalar", ot1, PB[4][:, 0:128], lam_col, None, ALU.mult)
            ctx.v("tensor_copy", ot2, PB[4][:, 128:256])
            ctx.v("scalar_tensor_tensor", omx, ot2, lam_col, ot2,
                  ALU.mult, ALU.subtract)
            ctx.v("tensor_sub", omx, ot1, omx)
            ctx.wait(S, "v")
            ssq = ctx.s("activation", ojnk[:, 0:128], omx, AF.Square, accum_out=sqc)
            ctx.wait(V, "s", ssq)
            ctx.v("tensor_scalar", rcl, sqc, 1.0 / DV, NORM_EPS, ALU.mult, ALU.add)
            ctx.v("reciprocal", rcl, rcl)
            ctx.wait(S, "v")
            src2 = ctx.s("activation", rcl, rcl, AF.Sqrt)
            ctx.wait(V, "s", src2)
            v_on = ctx.v("tensor_scalar", selt, omx, rcl, None, ALU.mult)
            ctx.waitd(V, f"oa{head}")
            tpose(ofm[head][:, cc * 128:(cc + 1) * 128], selt, src_v=v_on)
        d = (w * 512) // TPC
        ctx.wait(G, "v", ctx.cnt["v"])
        ctx.dma(G, f"oa{head}",
                B.a2a_in[d * (HPC * DV) + head * 128:d * (HPC * DV) + (head + 1) * 128,
                         w * 512 - d * TPC:w * 512 - d * TPC + 512],
                ofm[head])


def _ph5_a2a(ctx, B):
    G = ctx.G
    for head in range(HPC):
        ctx.waitd(G, f"oa{head}")
    with allow_ici_all_to_all():
        B.c_a2a = ctx.mark("cc", G.collective_compute(
            "AllToAll", ALU.bypass, replica_groups=[list(range(NCORES))],
            ins=[B.a2a_in.ap().opt()], outs=[B.a2a_out.ap().opt()]))


def _ph6_epilogue(ctx, B):
    T, V, S, G, Y = ctx.T, ctx.V, ctx.S, ctx.G, ctx.Y
    NTT = B.NTT
    woc = B.arC[:, 0:8192]
    gfb = [B.arC[:, 8192:8704], B.arC[:, 8704:9216]]
    sgt = [B.arC[:, 9216:11264], B.arC[:, 11264:13312]]
    xb = [B.arB[:, 0:512], B.arB[:, 512:1024]]
    yt = B.arB[:, 1024:1536]
    yst = [B.arB[:, 1536:2048], B.arB[:, 2048:2560]]

    for e in (T, V, S, G, Y):
        ctx.wait(e, "cc", B.c_a2a)
    ctx.wait(Y, "t", B.c_scan[0])
    ctx.wait(Y, "v", B.c_scan[1])
    ctx.wait(Y, "s", B.c_scan[2])
    ctx.wait(T, "v", B.c_scan[1])
    ctx.wait(G, "t", B.c_scan[0])
    ctx.wait(G, "v", B.c_scan[1])
    for q in range(2):
        ctx.waitd(Y, f"sgs{q}")
        ctx.waitd(Y, f"gf{q}")

    v_of = 0
    t_half = 0
    pbfree = [0, 0]
    for h0 in range(0, NTT, 8):
        group = list(range(h0, min(h0 + 8, NTT)))
        for it in group:
            j = it - h0
            ofs = B.arA[:, j * 2048:(j + 1) * 2048]
            ctx.wait(Y, "t", t_half)
            ctx.wait(Y, "v", v_of)
            ctx.waitd(Y, f"ao{j % 2}")
            ctx.dma(Y, f"ao{j % 2}", ofs.rearrange("p (c t) -> p c t", t=128),
                    B.a2a_out[0:H * DV, it * 128:(it + 1) * 128]
                    .rearrange("(c p) t -> p c t", p=128))
            ctx.waitd(Y, f"so{j % 2}")
            ctx.dma(Y, f"so{j % 2}", sgt[j % 2].rearrange("p (c t) -> p c t", t=128),
                    B.sgate_d[0:D, it * 128:(it + 1) * 128]
                    .rearrange("(c p) t -> p c t", p=128))
            ctx.waitd(V, f"ao{j % 2}")
            ctx.waitd(V, f"so{j % 2}")
            ctx.wait(V, "t", t_half)
            v_of = ctx.v("scalar_tensor_tensor", ofs, ofs, B.onorm_t[:, 0:1],
                         sgt[j % 2], ALU.mult, ALU.mult)
        for oc in range(4):
            for q in range(4):
                ctx.waitd(G, f"wo{q % 2}")
                ctx.wait(G, "t", t_half)
                ctx.dma(G, f"wo{q % 2}",
                        woc[:, q * 2048:(q + 1) * 2048].rearrange("p (c o) -> p c o", o=512),
                        B.wo[q * 512:(q + 1) * 512, oc * 512:(oc + 1) * 512]
                        .rearrange("(c p) o -> p c o", p=128))
            for q in range(4):
                ctx.waitd(T, f"wo{q % 2}")
            ctx.wait(T, "v", v_of)
            for it in group:
                j = it - h0
                pb = [B.pb0, B.pb1][j % 2]
                ctx.wait(T, "v", pbfree[j % 2])
                tmk = 0
                for ch in range(16):
                    tmk = ctx.mark("t", T.matmul(
                        pb[:, 0:512], B.arA[:, (j * 16 + ch) * 128:(j * 16 + ch + 1) * 128],
                        woc[:, ch * 512:(ch + 1) * 512],
                        start=(ch == 0), stop=(ch == 15)))
                ctx.waitd(S, f"x{j % 2}e")
                ctx.waitd(S, f"gf{j % 2}e")
                ctx.wait(S, "v", pbfree[j % 2])
                ctx.dma(S, f"x{j % 2}e", xb[j % 2],
                        B.x_sl[it * 128:(it + 1) * 128, oc * 512:(oc + 1) * 512])
                ctx.dma(S, f"gf{j % 2}e", gfb[j % 2],
                        B.gffn_d[it * 128:(it + 1) * 128, oc * 512:(oc + 1) * 512])
                ctx.wait(V, "t", tmk)
                ctx.waitd(V, f"x{j % 2}e")
                ctx.waitd(V, f"gf{j % 2}e")
                ctx.waitd(V, f"ys{j % 2}")
                ctx.v("tensor_tensor", yt, pb[:, 0:512], gfb[j % 2], ALU.mult)
                pbfree[j % 2] = ctx.v("tensor_tensor", yst[j % 2], yt, xb[j % 2],
                                      ALU.add)
                ctx.wait(G, "v", pbfree[j % 2])
                ctx.dma(G, f"ys{j % 2}",
                        B.y_sl[it * 128:(it + 1) * 128, oc * 512:(oc + 1) * 512],
                        yst[j % 2])
        t_half = ctx.cnt["t"]
    ctx.waitd(G, "ys0")
    ctx.waitd(G, "ys1")
    ctx.waitd(Y, "ys0")
    ctx.waitd(Y, "ys1")


def _ph_stub_out(ctx, B):
    T, V, S, G, Y = ctx.T, ctx.V, ctx.S, ctx.G, ctx.Y
    for fs in range(4):
        ctx.waitd(G, f"fm{fs}")
    ctx.waitd(G, "agent")
    cc = ctx.mark("cc", G.collective_compute(
        "AllGather", ALU.bypass, replica_groups=[list(range(NCORES))],
        ins=[B.ag_in.ap().opt()], outs=[B.ag_out.ap().opt()]))
    ctx.wait(G, "cc", cc)
    ctx.wait(Y, "v", ctx.cnt["v"])
    ctx.wait(Y, "s", ctx.cnt["s"])
    for i in range(B.NTT):
        buf = B.arB[:, 0:2048]
        ctx.waitd(Y, "sty")
        ctx.dma(Y, "stx", buf, B.x_sl[i * 128:(i + 1) * 128, :])
        ctx.waitd(G, "stx")
        ctx.dma(G, "sty", B.y_sl[i * 128:(i + 1) * 128, :], buf)
    ctx.waitd(G, "sty")


_PROG_CACHE = {}


def _get_prog(T):
    if T not in _PROG_CACHE:
        _PROG_CACHE[T] = build_program(T)
    return _PROG_CACHE[T]


def _consts():
    i128b = np.eye(128, dtype=bfdt)
    i128f = np.eye(128, dtype=np.float32)
    jj, ii = np.meshgrid(np.arange(128), np.arange(128), indexing="ij")
    trilS = (jj < ii).astype(bfdt)
    trilI = (jj <= ii).astype(bfdt)
    trilIf = (jj <= ii).astype(np.float32)
    bd8m = ((jj // 8 == ii // 8) & (ii >= jj)).astype(np.uint8)
    rep32 = np.zeros((32, 128), dtype=np.float32)
    for k in range(4):
        rep32[np.arange(32), 32 * k + np.arange(32)] = 1
    return {
        "identc": i128b, "identf": i128f, "trilS": trilS, "trilI": trilI,
        "rep32": rep32, "ones1": np.ones((1, 128), dtype=bfdt),
        "ones1f": np.ones((1, 128), dtype=np.float32),
        "onescf": np.ones((128, 1), dtype=np.float32),
        "trilIf": trilIf, "bd8m": bd8m,
        "onescb": np.ones((128, 1), dtype=bfdt),
        "onesff": np.ones((128, 128), dtype=np.float32),
        "sel96f": (np.arange(128)[:, None] == 96).astype(np.float32) * np.ones((1, 128), np.float32),
    }


def _blockdiag(wm, n):
    r, c = wm.shape
    out = np.zeros((128, 128), dtype=wm.dtype)
    for q in range(n):
        out[32 * q:32 * q + r, 32 * q:32 * q + c] = wm
    return out


def _in_maps(g, T, i):
    TPC = T // NCORES
    hs = slice(i * HPC * DK, (i + 1) * HPC * DK)
    m = {
        "x_sl": np.ascontiguousarray(g["L0_attn_hidden_states"][0, i * TPC:(i + 1) * TPC]),
        "wq_s": np.ascontiguousarray(g["L0_attn_wq"][:, hs]),
        "wk_s": np.ascontiguousarray(g["L0_attn_wk"][:, hs]),
        "wv_s": np.ascontiguousarray(g["L0_attn_wv"][:, hs]),
        "cq_s": np.ascontiguousarray(g["L0_attn_conv_q_w"][hs]),
        "ck_s": np.ascontiguousarray(g["L0_attn_conv_k_w"][hs]),
        "cv_s": np.ascontiguousarray(g["L0_attn_conv_v_w"][hs]),
        "alog_s": np.ascontiguousarray(g["L0_attn_A_log"][None, i * HPC:(i + 1) * HPC]),
        "dtb_s": np.ascontiguousarray(g["L0_attn_dt_bias"][hs].reshape(HPC, 128).T),
        "proxy": g["L0_attn_proxy_w"],
        "unc": g["L0_attn_unc_w"],
        "he_s": np.ascontiguousarray(
            g["L0_attn_head_embed"][i * HPC:(i + 1) * HPC].reshape(1, HPC * 4)),
        "mlp1": _blockdiag(g["L0_attn_mlp_w1"], 4),
        "mlpb1": np.ascontiguousarray(g["L0_attn_mlp_b1"][:, None]),
        "mlp2": _blockdiag(g["L0_attn_mlp_w2"], 4),
        "mlpb2": np.ascontiguousarray(g["L0_attn_mlp_b2"][:, None]),
        "gw5": np.ascontiguousarray(np.stack(
            [g["L0_attn_bb_w"], g["L0_attn_bd_w"], g["L0_attn_lam_w"],
             g["L0_attn_ab_w"], g["L0_attn_ad_w"]], axis=1)),
        "gb5": np.ascontiguousarray(np.concatenate(
            [g["L0_attn_bb_b"], g["L0_attn_bd_b"], g["L0_attn_lam_b"],
             g["L0_attn_ab_b"], g["L0_attn_ad_b"]])[None, :]),
        "g1w": g["L0_attn_g1_w"],
        "g2w": g["L0_attn_g2_w"],
        "g2b": np.ascontiguousarray(g["L0_attn_g2_b"].reshape(-1, 128).T),
        "onorm": np.ascontiguousarray(g["L0_attn_onorm_w"][:, None]),
        "wo": g["L0_attn_wo"],
        "gateT": np.ascontiguousarray(g["L0_ffn_gate_w"].T),
        "downT": np.ascontiguousarray(g["L0_ffn_down_w"].T),
        "gnorm": np.ascontiguousarray(g["L0_ffn_gate_norm_w"][None, :]),
        "dnorm": np.ascontiguousarray(g["L0_ffn_down_norm_w"][None, :]),
    }
    m.update(_consts())
    return m


def kernel(**inputs):
    g = {k: np.asarray(v) for k, v in inputs.items()}
    T = g["L0_attn_hidden_states"].shape[1]
    nc = _get_prog(T)
    in_maps = [_in_maps(g, T, i) for i in range(NCORES)]
    res = run_bass_kernel_spmd(nc, in_maps, core_ids=list(range(NCORES)))
    out = np.concatenate([res.results[i]["y_sl"] for i in range(NCORES)], axis=0)
    return out[None].astype(np.float32)


# version 06
# speedup vs baseline: 1.3343x; 1.0883x over previous
import math
import contextlib
import types
import numpy as np
import ml_dtypes
import concourse.bass as bass
import concourse.mybir as mybir
from concourse.bass_utils import run_bass_kernel_spmd
from concourse.aot_env import allow_ici_all_to_all

F32 = mybir.dt.float32
BF16 = mybir.dt.bfloat16
AF = mybir.ActivationFunctionType
ALU = mybir.AluOpType
AX = mybir.AxisListType

NCORES = 8
D = 2048
H, DK, DV = 16, 128, 128
HPC = H // NCORES
BINS = 64
INTER = 5632
CH = 128
WIN = 512
NCK = WIN // CH
NHR = HPC * 2
QS = 1.0 / math.sqrt(DK)
CLAMP = 85.0
MAGIC = float(3 * 2**22)
EPS = 1e-6
NORM_EPS = 1e-5
LN_EPS = 1e-5
LOGBINS = math.log(BINS)

bfdt = ml_dtypes.bfloat16


class _FakeInstr:
    def then_inc(self, *a, **k):
        return self


_FAKE = _FakeInstr()


class _EngProxy:

    def __init__(self, ctx, name):
        object.__setattr__(self, "_ctx", ctx)
        object.__setattr__(self, "_name", name)

    def __getattr__(self, attr):
        ctx = self._ctx
        if ctx.active == self._name:
            return getattr(ctx.real, attr)

        def _noop(*a, **k):
            return _FAKE
        return _noop


class Ctx:
    CH_NAMES = ("t", "v", "s", "g", "cc")
    NDMA = 72

    def __init__(self, nc, sems, dpool):
        self.nc = nc
        self.sems = sems
        self.dpool = dpool
        self.dslot = {}
        self.dcnt = {}
        self.active = None
        self.real = None
        self.T = _EngProxy(self, "tensor")
        self.V = _EngProxy(self, "vector")
        self.S = _EngProxy(self, "scalar")
        self.G = _EngProxy(self, "gpsimd")
        self.Y = _EngProxy(self, "sync")
        self.reset()

    def reset(self):
        self.cnt = {k: 0 for k in self.CH_NAMES}
        self.dcnt = {k: 0 for k in self.dcnt}

    def mark(self, ch, instr, n=1):
        instr.then_inc(self.sems[ch], n)
        self.cnt[ch] += n
        return self.cnt[ch]

    def dma(self, eng_proxy, name, out, in_, **kw):
        if name not in self.dslot:
            assert len(self.dslot) < self.NDMA, "dma sem pool exhausted"
            self.dslot[name] = len(self.dslot)
            self.dcnt[name] = 0
        ins = eng_proxy.dma_start(out=out, in_=in_, **kw)
        ins.then_inc(self.dpool[self.dslot[name]], 16)
        self.dcnt[name] += 16
        return self.dcnt[name]

    def waitd(self, eng_proxy, name, val=None):
        if name not in self.dslot:
            return
        v = self.dcnt[name] if val is None else val
        if v > 0:
            eng_proxy.wait_ge(self.dpool[self.dslot[name]], v)

    def wait(self, eng_proxy, ch, val=None):
        v = self.cnt[ch] if val is None else val
        if v > 0:
            eng_proxy.wait_ge(self.sems[ch], v)

    def _op(self, proxy, ch, name, *a, dep=True, **k):
        if dep:
            self.wait(proxy, ch)
        ins = getattr(proxy, name)(*a, **k)
        self.mark(ch, ins)
        return self.cnt[ch]

    def v(self, name, *a, **k):
        return self._op(self.V, "v", name, *a, **k)

    def s(self, name, *a, **k):
        return self._op(self.S, "s", name, *a, **k)

    def g(self, name, *a, **k):
        return self._op(self.G, "g", name, *a, **k)


def build_program(T=16384, debug_names=()):
    TPC = T // NCORES
    NTT = TPC // 128
    NWIN = T // WIN
    NBLK = max(1, TPC // 512)
    AGROW = D + 1

    nc = bass.Bass("TRN2", target_bir_lowering=False, debug=False)
    B = types.SimpleNamespace(T=T, TPC=TPC, NTT=NTT, NWIN=NWIN, NBLK=NBLK,
                              AGROW=AGROW, nc=nc)

    def din(name, shape, dt=F32):
        setattr(B, name, nc.dram_tensor(name, list(shape), dt, kind="ExternalInput").ap())

    din("x_sl", (TPC, D))
    din("wq_s", (D, HPC * DK))
    din("wk_s", (D, HPC * DK))
    din("wv_s", (D, HPC * DV))
    din("cq_s", (HPC * DK, 4))
    din("ck_s", (HPC * DK, 4))
    din("cv_s", (HPC * DV, 4))
    din("alog_s", (1, HPC))
    din("dtb_s", (128, HPC))
    din("proxy", (DK, DV))
    din("unc", (D, BINS))
    din("he_s", (1, HPC * 4))
    din("mlp1", (128, 128))
    din("mlpb1", (32, 1))
    din("mlp2", (128, 128))
    din("mlpb2", (32, 1))
    din("gw5", (32, 5))
    din("gb5", (1, 5))
    din("g1w", (D, 128))
    din("g2w", (128, D))
    din("g2b", (128, D // 128))
    din("onorm", (DV, 1))
    din("wo", (H * DV, D))
    din("gateT", (D, 2 * INTER))
    din("downT", (INTER, D))
    din("gnorm", (1, D))
    din("dnorm", (1, INTER))
    din("identc", (128, 128), BF16)
    din("identf", (128, 128), F32)
    din("trilS", (128, 128), BF16)
    din("trilI", (128, 128), BF16)
    din("rep32", (32, 128), F32)
    din("ones1", (1, 128), BF16)
    din("ones1f", (1, 128), F32)
    din("onescf", (128, 1), F32)
    din("trilIf", (128, 128), F32)
    din("bd8m", (128, 256), mybir.dt.uint8)
    din("onescb", (128, 1), BF16)
    din("onesff", (128, 128), F32)
    din("sel96f", (128, 128), F32)

    B.y_sl = nc.dram_tensor("y_sl", [TPC, D], F32, kind="ExternalOutput").ap()
    B.dbg = {}
    for name, shape, dt in debug_names:
        B.dbg[name] = nc.dram_tensor(name, list(shape), dt, kind="ExternalOutput").ap()

    B.xn_tm_d = nc.dram_tensor("xn_tm_d", [TPC, D], F32).ap()
    B.h_d = nc.dram_tensor("h_d", [512, INTER], F32).ap()
    B.ag_in = nc.dram_tensor("ag_in", [AGROW, TPC], BF16)
    B.ag_out = nc.dram_tensor("ag_out", [NCORES * AGROW, TPC], BF16)
    B.a2a_in = nc.dram_tensor("a2a_in", [NCORES * HPC * DV, TPC], BF16)
    B.a2a_out = nc.dram_tensor("a2a_out", [NCORES * HPC * DV, TPC], BF16)
    B.tern_dn_d = nc.dram_tensor("tern_dn_d", [INTER, D], BF16).ap()
    B.tern_gt_d = nc.dram_tensor("tern_gt_d", [D, 2 * INTER], BF16).ap()
    B.gffn_d = nc.dram_tensor("gffn_d", [TPC, D], BF16).ap()
    B.sgate_d = nc.dram_tensor("sgate_d", [D, TPC], BF16).ap()

    es = contextlib.ExitStack()

    def sbuf(name, shape, dt=F32):
        setattr(B, name, es.enter_context(nc.sbuf_tensor(name, list(shape), dt)))

    def psum(name, shape, dt=F32):
        setattr(B, name, es.enter_context(nc.psum_tensor(name, list(shape), dt)))

    with es:
        sbuf("ident_t", (128, 128), BF16)
        sbuf("identf_t", (128, 128), F32)
        sbuf("trilS_t", (128, 128), BF16)
        sbuf("trilI_t", (128, 128), BF16)
        sbuf("rep32_t", (32, 128), F32)
        sbuf("ones1_t", (1, 128), BF16)
        sbuf("ones1f_t", (1, 128), F32)
        sbuf("onescf_t", (128, 1), F32)
        sbuf("trilIf_t", (128, 128), F32)
        sbuf("bd8m_t", (128, 256), mybir.dt.uint8)
        sbuf("onesff_t", (128, 128), F32)
        sbuf("sel96f_t", (128, 128), F32)
        sbuf("onescb_t", (128, 1), BF16)
        sbuf("hebc_t", (128, HPC * 4), F32)
        sbuf("wq_t", (128, 16 * HPC * DK), BF16)
        sbuf("wk_t", (128, 16 * HPC * DK), BF16)
        sbuf("wv_t", (128, 16 * HPC * DV), BF16)
        sbuf("convw", (128, 3 * HPC * 4), F32)
        sbuf("proxy_t", (128, DV), BF16)
        sbuf("unc_t", (128, 16 * BINS), BF16)
        sbuf("he_t", (1, HPC * 4), F32)
        sbuf("mlp1_t", (128, 128), BF16)
        sbuf("mlp2_t", (128, 128), BF16)
        sbuf("gaug_t", (128, 128), BF16)
        sbuf("gw5_t", (32, 5), F32)
        sbuf("baug_t", (1, 16), F32)
        sbuf("baugc_t", (128, 1), F32)
        sbuf("b1rep_t", (128, 1), F32)
        sbuf("b2rep_t", (128, 1), F32)
        sbuf("gcol_t", (128, HPC), F32)
        sbuf("ngcol_t", (128, HPC), F32)
        sbuf("gnb_t", (128, D), F32)
        sbuf("dnrow_t", (1, INTER), F32)
        sbuf("g2b_t", (128, 16), F32)
        sbuf("onorm_t", (128, 1), F32)
        sbuf("swcol_t", (128, 4), F32)
        sbuf("stat_t", (128, 4 * NTT), F32)
        sbuf("wpart_t", (128, 160), F32)
        sbuf("scl_t", (1, 16), F32)
        sbuf("mlpb_t", (32, 2), F32)
        sbuf("dtb_t", (128, HPC), F32)
        sbuf("alog_t", (1, HPC), F32)
        sbuf("arA", (128, 20 * 1024), BF16)
        sbuf("arB", (128, 11904), F32)
        sbuf("arC", (128, 14336), BF16)
        sbuf("arD", (128, 4 * 1024), F32)
        sbuf("Sf_t", (128, NHR * DV), F32)
        sbuf("Sb_t", (128, NHR * DV), BF16)
        sbuf("carry_t", (128, 9 * HPC), F32)
        sbuf("lamc_t", (128, NHR), F32)
        for i in range(5):
            psum(f"pb{i}", (128, 512), F32)
        psum("pT1a", (128, 1024), BF16)
        psum("pT1b", (128, 1024), BF16)
        psum("pT2", (128, 512), F32)

        sems = {}
        for chn in Ctx.CH_NAMES:
            sems[chn] = es.enter_context(nc.semaphore("sem_" + chn))
        dpool = [es.enter_context(nc.semaphore(f"dsem{i}")) for i in range(Ctx.NDMA)]
        block = es.enter_context(nc.Block())
        ctx = Ctx(nc, sems, dpool)

        for starter, eng_name in ((block.sync, "sync"), (block.tensor, "tensor"),
                                  (block.vector, "vector"), (block.scalar, "scalar"),
                                  (block.gpsimd, "gpsimd")):
            def _run(engine, _n=eng_name):
                ctx.active = _n
                ctx.real = engine
                ctx.reset()
                _emit(ctx, B)
                ctx.active = None
                ctx.real = None
            starter(_run)

    return nc


_PH1 = 9
_PH2 = True
_PH3 = True
_PH4 = True
_PH0HEB = True
_PH0REP = True
_PH0GAUG = True


def _emit(ctx, B):
    _ph0_init(ctx, B)
    if _PH1 >= 1:
        _ph1_ln(ctx, B)
    _ag_issue(ctx, B)
    if _PH2:
        _ph2_sgate(ctx, B)
    _wmean(ctx, B)
    if _PH3:
        _wtern(ctx, B)
        _ph3_ffn(ctx, B)
    if _PH4:
        _ph4_scan(ctx, B)
        _ph5_a2a(ctx, B)
        _ph6_epilogue(ctx, B)
    else:
        _ph_stub_out(ctx, B)


def _ag_issue(ctx, B):
    G = ctx.G
    for fs in range(4):
        ctx.waitd(G, f"fm{fs}")
    ctx.waitd(G, "agent")
    B.c_ag = ctx.mark("cc", G.collective_compute(
        "AllGather", ALU.bypass, replica_groups=[list(range(NCORES))],
        ins=[B.ag_in.ap().opt()], outs=[B.ag_out.ap().opt()]))


def _ph2_sgate(ctx, B):
    T, V, S, G, Y = ctx.T, ctx.V, ctx.S, ctx.G, ctx.Y
    g1t = B.arC[:, 8192:10240]
    g2t = B.arC[:, 10240:12288]
    ctx.dma(G, "g1w", g1t.rearrange("p (c o) -> p c o", o=128),
            B.g1w.rearrange("(c p) o -> p c o", p=128))
    ctx.dma(G, "g2w", g2t, B.g2w[:, :])
    nck = B.TPC // 512
    xfm = [B.arC[:, 0:8192], B.arC[:, 0:8192]]
    g1o = B.arC[:, 12288:13312]
    sg = B.arC[:, 13312:14336]
    for fs in range(4):
        ctx.waitd(Y, f"fm{fs}")
    ctx.waitd(Y, "agent")
    ctx.waitd(T, "g1w")
    ctx.waitd(T, "g2w")
    xf_free = 0
    v_g1 = 0
    s_done = [0, 0]
    for c in range(nck):
        sl = c % 2
        ctx.wait(Y, "t", xf_free)
        ctx.dma(Y, "sgx0", xfm[0].rearrange("p (a t) -> p a t", t=512),
                B.ag_in[0:D, c * 512:(c + 1) * 512].rearrange("(a p) t -> p a t", p=128))
        ctx.waitd(T, "sgx0")
        ctx.wait(T, "v", v_g1)
        tm = 0
        for ic in range(16):
            tm = ctx.mark("t", T.matmul(B.pb4[:, 0:512], g1t[:, ic * 128:(ic + 1) * 128],
                                        xfm[0][:, ic * 512:(ic + 1) * 512],
                                        start=(ic == 0), stop=(ic == 15)))
        xf_free = tm
        ctx.wait(V, "t", tm)
        v_g1 = ctx.v("tensor_copy", g1o[:, sl * 512:(sl + 1) * 512], B.pb4[:, 0:512])
        for oc in range(16):
            pb = [B.pb0, B.pb1][oc % 2]
            ctx.wait(T, "v", v_g1)
            ctx.wait(T, "s", s_done[oc % 2])
            tg = ctx.mark("t", T.matmul(pb[:, 0:512], g2t[:, oc * 128:(oc + 1) * 128],
                                        g1o[:, sl * 512:(sl + 1) * 512],
                                        start=True, stop=True))
            ctx.wait(S, "t", tg)
            ctx.waitd(S, f"sgs{oc % 2}")
            s_done[oc % 2] = ctx.s("activation", sg[:, (oc % 2) * 512:(oc % 2) * 512 + 512],
                                   pb[:, 0:512], AF.Sigmoid,
                                   bias=B.g2b_t[:, oc:oc + 1])
            ctx.wait(G, "s", s_done[oc % 2])
            ctx.dma(G, f"sgs{oc % 2}",
                    B.sgate_d[oc * 128:(oc + 1) * 128, c * 512:(c + 1) * 512],
                    sg[:, (oc % 2) * 512:(oc % 2) * 512 + 512])
        pass
    B.c_sgate_t = ctx.cnt["t"]


def _wmean(ctx, B):
    T, V, S, G, Y = ctx.T, ctx.V, ctx.S, ctx.G, ctx.Y
    wbuf = [B.arB[:, 6656:8704], B.arB[:, 8704:10752]]
    junk = B.arB[:, 4096:6144]
    ctx.waitd(Y, "xnst0")
    ctx.waitd(Y, "xnst1")
    ctx.wait(Y, "s", ctx.cnt["s"])
    cons = [("t", 0), ("t", 0)]
    col = 0
    bounds = []
    for src_ap, nr, ncol in ((B.gateT, D, 2 * INTER), (B.downT, INTER, D)):
        c0 = col
        for r in range(0, nr, 128):
            for cc in range(0, ncol, 2048):
                w = min(2048, ncol - cc)
                sl = col % 2
                ch, cv = cons[sl]
                ctx.wait(Y, ch, cv)
                ctx.dma(Y, f"wm{sl}", wbuf[sl][:, 0:w], src_ap[r:r + 128, cc:cc + w])
                eng = col % 2
                out = B.wpart_t[:, col:col + 1]
                if eng == 0:
                    ctx.waitd(S, f"wm{sl}")
                    m = ctx.s("activation", junk[:, 0:w], wbuf[sl][:, 0:w], AF.Abs,
                              accum_out=out)
                    cons[sl] = ("s", m)
                else:
                    ctx.waitd(V, f"wm{sl}")
                    m = ctx.v("reduce_sum", out, wbuf[sl][:, 0:w], axis=AX.X,
                              apply_absolute_value=True)
                    cons[sl] = ("v", m)
                col += 1
        bounds.append((c0, col, nr * ncol))
    ctx.wait(V, "s", ctx.cnt["s"])
    ctx.wait(V, "g", ctx.cnt["g"])
    pc = B.arD[0:128, 52:54]
    for k, (c0, c1, nel) in enumerate(bounds):
        ctx.v("reduce_sum", pc[:, k:k + 1], B.wpart_t[:, c0:c1], axis=AX.X)
    v_pc = ctx.cnt["v"]
    ctx.wait(T, "v", v_pc)
    ctx.wait(T, "t", B.c_init_t)
    tm = 0
    for k in range(2):
        tm = ctx.mark("t", T.matmul(B.pT2[0:1, 8 + k:9 + k], pc[:, k:k + 1],
                                    B.onescf_t[:, 0:1], start=True, stop=True))
    ctx.wait(V, "t", tm)
    for k, (c0, c1, nel) in enumerate(bounds):
        ctx.v("tensor_scalar", B.scl_t[0:1, 4 + k:5 + k], B.pT2[0:1, 8 + k:9 + k],
              1.0 / nel, 1e-5, ALU.mult, ALU.max)
    ctx.v("reciprocal", B.scl_t[0:1, 6:8], B.scl_t[0:1, 4:6])
    v_sw = ctx.cnt["v"]
    ctx.wait(T, "v", v_sw)
    tb = ctx.mark("t", T.matmul(B.pT2[:, 10:12], B.ones1f_t[0:1, :],
                                B.scl_t[0:1, 6:8], start=True, stop=True))
    ctx.wait(V, "t", tb)
    ctx.v("tensor_copy", B.swcol_t[:, 0:2], B.pT2[:, 10:12])
    B.c_wmean_v = ctx.cnt["v"]


def _wtern(ctx, B):
    T, V, S, G, Y = ctx.T, ctx.V, ctx.S, ctx.G, ctx.Y
    wbuf = [B.arB[:, 6656:8704], B.arB[:, 8704:10752]]
    sgn = [B.arB[:, 0:2048], B.arB[:, 2048:4096]]
    ab = [B.arB[:, 4096:6144], B.arD[:, 1024:3072]]
    tern = [B.arA[:, 12288:14336], B.arA[:, 14336:16384]]
    ctx.wait(S, "v", B.c_wmean_v)
    ctx.wait(S, "t", B.c_init_t)
    ctx.wait(Y, "s", ctx.cnt["s"])
    ctx.wait(Y, "v", ctx.cnt["v"])
    cons = [("v", 0), ("v", 0)]
    vlast = [0, 0]
    cnt = 0
    for (src_ap, dst_ap, nr, ncol, k) in ((B.gateT, B.tern_gt_d, D, 2 * INTER, 0),
                                          (B.downT, B.tern_dn_d, INTER, D, 1)):
        for r in range(0, nr, 128):
            for cc in range(0, ncol, 2048):
                w = min(2048, ncol - cc)
                sl = cnt % 2
                cnt += 1
                ch, cv = cons[sl]
                ctx.wait(Y, ch, cv)
                ctx.dma(Y, f"wt{sl}", wbuf[sl][:, 0:w], src_ap[r:r + 128, cc:cc + w])
                ctx.waitd(S, f"wt{sl}")
                ctx.wait(S, "v", vlast[sl])
                ctx.s("activation", sgn[sl][:, 0:w], wbuf[sl][:, 0:w], AF.Sign,
                      scale=B.swcol_t[:, k:k + 1])
                m_ab = ctx.s("activation", ab[sl][:, 0:w], wbuf[sl][:, 0:w], AF.Abs,
                             scale=B.swcol_t[:, k:k + 1])
                cons[sl] = ("s", m_ab)
                ctx.wait(V, "s", m_ab)
                ctx.v("tensor_scalar", ab[sl][:, 0:w], ab[sl][:, 0:w], 0.5, None, ALU.is_ge)
                ctx.waitd(V, f"wo{sl}")
                mt = ctx.v("tensor_tensor", tern[sl][:, 0:w], ab[sl][:, 0:w],
                           sgn[sl][:, 0:w], ALU.mult)
                vlast[sl] = mt
                ctx.wait(G, "v", mt)
                ctx.dma(G, f"wo{sl}", dst_ap[r:r + 128, cc:cc + w], tern[sl][:, 0:w])
    for sl in range(2):
        ctx.waitd(G, f"wo{sl}")
    B.c_wtern_done = (ctx.cnt["v"], ctx.cnt["s"])


def _ph3_ffn(ctx, B):
    T, V, S, G, Y = ctx.T, ctx.V, ctx.S, ctx.G, ctx.Y
    NBLK = B.TPC // 512
    xqfm = B.arA[:, 0:8192]
    xqtm = [B.arA[:, 16384:18432], B.arA[:, 18432:20480]]
    silug = B.arD[:, 2048:4096]
    tgt = [B.arA[:, 8192:10240], B.arA[:, 10240:12288]]
    dnt = [B.arA[:, 12288:14336], B.arA[:, 14336:16384]]
    hqtm = B.arC[:, 0:11264]
    xin = [B.arB[:, 0:2048], B.arB[:, 2048:4096]]
    rbuf = B.arB[:, 4096:6144]
    hstage = [B.arB[:, 6144:6656], B.arB[:, 6656:7168]]
    hjunk = B.arB[:, 7168:7680]
    dnocp = B.arB[:, 7680:8192]
    hdnb = [B.arB[:, 8192:8704], B.arB[:, 8704:9216]]
    hchunk = [B.arB[:, 4096:5504], B.arB[:, 5504:6912]]
    amax = B.arD[0:128, 56:57]
    stcol = B.arD[0:128, 57:58]
    c1col = B.arD[0:128, 60:64]
    c2col = B.arD[0:128, 64:65]
    sqcol = B.arD[0:128, 68:72]
    mxcol = B.arD[0:128, 72:76]
    part = B.arD[0:128, 76:77]
    rs2 = B.arD[0:128, 67:68]
    ctx.wait(V, "t", B.c_sgate_t)
    ctx.wait(S, "t", B.c_sgate_t)
    for q in range(2):
        ctx.waitd(V, f"sgs{q}")
        ctx.waitd(S, f"sgs{q}")
    ctx.wait(S, "v", B.c_wtern_done[0])
    ctx.wait(V, "s", ctx.cnt["s"])
    ctx.waitd(Y, "wo0")
    ctx.waitd(Y, "wo1")
    ctx.waitd(V, "wt0")
    ctx.waitd(V, "wt1")
    for sl in range(2):
        ctx.waitd(S, f"xnst{sl}")
    t_l1_done = 0
    vq = [0, 0]
    for b in range(NBLK):
        for tt in range(4):
            it = b * 4 + tt
            sl = tt % 2
            ctx.waitd(S, f"ffx{sl}")
            ctx.waitd(S, f"fft{sl}")
            ctx.wait(S, "v", vq[sl])
            ctx.wait(S, "t", ctx.cnt["t"])
            ctx.dma(S, f"ffx{sl}", xin[sl], B.xn_tm_d[it * 128:(it + 1) * 128, :])
            ctx.waitd(V, f"ffx{sl}")
            ctx.wait(V, "s", ctx.cnt["s"])
            rsf = B.stat_t[:, 4 * it + 2:4 * it + 3]
            ctx.v("scalar_tensor_tensor", rbuf, xin[sl], rsf, B.gnb_t[:],
                  ALU.mult, ALU.mult)
            ctx.v("tensor_reduce", amax, rbuf, AX.X, ALU.max,
                  apply_absolute_value=True)
            ctx.v("tensor_scalar", amax, amax, 1e-5, None, ALU.max)
            ctx.v("reciprocal", stcol, amax)
            ctx.v("tensor_scalar", stcol, stcol, 127.0, None, ALU.mult)
            ctx.v("tensor_tensor", c1col[:, tt:tt + 1], stcol, B.swcol_t[:, 0:1],
                  ALU.mult)
            ctx.v("reciprocal", c1col[:, tt:tt + 1], c1col[:, tt:tt + 1])
            ctx.v("tensor_scalar", rbuf, rbuf, stcol, MAGIC, ALU.mult, ALU.add)
            mq = ctx.v("tensor_scalar", xqtm[sl], rbuf, MAGIC, None, ALU.subtract)
            vq[sl] = mq
            ctx.wait(Y, "v", mq)
            ctx.wait(Y, "t", ctx.cnt["t"])
            for c in range(16):
                ctx.dma(Y, f"fft{sl}", xqfm[:, c * 512 + tt * 128:c * 512 + tt * 128 + 128],
                        xqtm[sl][:, c * 128:(c + 1) * 128], transpose=True)
        for sl in range(2):
            ctx.waitd(T, f"fft{sl}")
        ctx.v("memset", sqcol, 0.0)
        ctx.v("memset", mxcol, 0.0)
        s_silu = [0] * 4
        v_h = [0] * 4
        g_h = [0] * 4
        tg_cons = [("t", ctx.cnt["t"]), ("t", ctx.cnt["t"])]
        ctx.wait(T, "s", ctx.cnt["s"])
        ctx.wait(T, "v", ctx.cnt["v"])
        for ocp in range(11):
            for phase in range(2):
                oc = ocp + phase * 11
                tmk = 0
                for icg in range(4):
                    sl = icg % 2
                    ch, cv = tg_cons[sl]
                    ctx.wait(Y, ch, cv)
                    ctx.dma(Y, f"tg{sl}", tgt[sl].rearrange("p (c o) -> p c o", o=512),
                            B.tern_gt_d[icg * 512:(icg + 1) * 512,
                                        oc * 512:(oc + 1) * 512]
                            .rearrange("(c p) o -> p c o", p=128))
                    ctx.waitd(T, f"tg{sl}")
                    if icg == 0:
                        if phase == 0:
                            for q in range(4):
                                ctx.wait(T, "v", v_h[q])
                        else:
                            for q in range(4):
                                ctx.wait(T, "s", s_silu[q])
                    for i in range(4):
                        ic = icg * 4 + i
                        for q in range(4):
                            tmk = ctx.mark("t", T.matmul(
                                [B.pb0, B.pb1, B.pb2, B.pb3][q][:, 0:512],
                                xqfm[:, ic * 512 + q * 128:ic * 512 + q * 128 + 128],
                                tgt[sl][:, i * 512:(i + 1) * 512],
                                start=(ic == 0), stop=(ic == 15)))
                    tg_cons[sl] = ("t", tmk)
                if phase == 0:
                    for q in range(4):
                        ctx.wait(S, "t", tmk)
                        ctx.wait(S, "v", v_h[q])
                        s_silu[q] = ctx.s("activation", silug[:, q * 512:(q + 1) * 512],
                                          [B.pb0, B.pb1, B.pb2, B.pb3][q][:, 0:512],
                                          AF.Silu, scale=c1col[:, q:q + 1])
                else:
                    ctx.wait(T, "v", ctx.cnt["v"])
                    tdn = ctx.mark("t", T.matmul(
                        B.pb4[:, 0:512], B.ones1f_t[0:1, :],
                        B.dnrow_t[0:1, ocp * 512:(ocp + 1) * 512],
                        start=True, stop=True))
                    ctx.wait(V, "t", tdn)
                    ctx.v("tensor_copy", dnocp, B.pb4[:, 0:512])
                    for q in range(4):
                        hsl = q % 2
                        ctx.wait(V, "t", tmk)
                        ctx.wait(V, "s", s_silu[q])
                        ctx.waitd(V, f"hst{hsl}")
                        ctx.v("scalar_tensor_tensor", hstage[hsl],
                              [B.pb0, B.pb1, B.pb2, B.pb3][q][:, 0:512],
                              c1col[:, q:q + 1], silug[:, q * 512:(q + 1) * 512],
                              ALU.mult, ALU.mult)
                        ctx.v("tensor_tensor", hdnb[hsl], hstage[hsl], dnocp,
                              ALU.mult)
                        ctx.wait(S, "v", ctx.cnt["v"])
                        ctx.s("activation", hjunk, hstage[hsl], AF.Square,
                              accum_out=part)
                        ctx.wait(V, "s", ctx.cnt["s"])
                        ctx.v("tensor_tensor", sqcol[:, q:q + 1], sqcol[:, q:q + 1],
                              part, ALU.add)
                        ctx.v("tensor_reduce", part, hdnb[hsl], AX.X, ALU.max,
                              apply_absolute_value=True)
                        v_h[q] = ctx.v("tensor_tensor", mxcol[:, q:q + 1],
                                       mxcol[:, q:q + 1], part, ALU.max)
                        ctx.wait(G, "v", v_h[q])
                        g_h[q] = ctx.dma(G, f"hst{hsl}",
                                         B.h_d[q * 128:(q + 1) * 128,
                                               ocp * 512:(ocp + 1) * 512],
                                         hdnb[hsl])
        t_l1_done = ctx.cnt["t"]
        ctx.waitd(S, "hst0")
        ctx.waitd(S, "hst1")
        rs2c = B.arD[0:128, 80:84]
        st2c = B.arD[0:128, 84:88]
        c2c = B.arD[0:128, 88:92]
        ctx.wait(V, "s", ctx.cnt["s"])
        for tt in range(4):
            ctx.v("tensor_scalar", rs2c[:, tt:tt + 1], sqcol[:, tt:tt + 1],
                  1.0 / INTER, NORM_EPS, ALU.mult, ALU.add)
        ctx.v("reciprocal", rs2c, rs2c)
        v_m = ctx.cnt["v"]
        ctx.wait(S, "v", v_m)
        s_rs = ctx.s("activation", rs2c, rs2c, AF.Sqrt)
        ctx.wait(V, "s", s_rs)
        ctx.v("tensor_scalar", st2c, mxcol, 1e-5, None, ALU.max)
        ctx.v("reciprocal", st2c, st2c)
        ctx.v("tensor_scalar", st2c, st2c, 127.0, None, ALU.mult)
        for tt in range(4):
            ctx.v("tensor_tensor", c2c[:, tt:tt + 1], st2c[:, tt:tt + 1],
                  B.swcol_t[:, 1:2], ALU.mult)
        ctx.v("reciprocal", c2c, c2c)
        v_c2 = ctx.v("tensor_tensor", c2c, c2c, rs2c, ALU.mult)
        qcc = B.arD[0:128, 92:96]
        ctx.v("tensor_copy", qcc, st2c)
        cw = 1408
        for half in range(2):
          tts = (2 * half, 2 * half + 1)
          for c in range(4):
              ctx.waitd(V, f"fq{c}")
          for cchunk in range(4):
            for tt in tts:
                hsl = tt % 2
                ctx.waitd(S, f"hrd{hsl}")
                ctx.wait(S, "v", ctx.cnt["v"])
                ctx.dma(S, f"hrd{hsl}", hchunk[hsl],
                        B.h_d[tt * 128:(tt + 1) * 128, cchunk * cw:(cchunk + 1) * cw])
                ctx.waitd(V, f"hrd{hsl}")
                ctx.v("tensor_scalar", hchunk[hsl], hchunk[hsl], qcc[:, tt:tt + 1],
                      MAGIC, ALU.mult, ALU.add)
                ctx.v("tensor_scalar", hqtm[:, (tt % 2) * 5632 + cchunk * cw:(tt % 2) * 5632 + (cchunk + 1) * cw],
                      hchunk[hsl], MAGIC, None, ALU.subtract)
          mq2 = ctx.cnt["v"]
          if "dbg_hq" in B.dbg and b == 0 and half == 0:
              ctx.wait(G, "v", mq2)
              ctx.dma(G, "dbghq", B.dbg["dbg_hq"][:, :], B.arC[:, 0:11264])
              ctx.waitd(Y, "dbghq")
          ctx.wait(Y, "v", mq2)
          ctx.wait(Y, "t", ctx.cnt["t"])
          for j, tt in enumerate(tts):
            for c in range(44):
                ctx.dma(Y, f"fq{c % 4}", B.arA[:, j * 5632 + c * 128:j * 5632 + (c + 1) * 128],
                        hqtm[:, (tt % 2) * 5632 + c * 128:(tt % 2) * 5632 + (c + 1) * 128],
                        transpose=True)
          for c in range(4):
              ctx.waitd(T, f"fq{c}")
          dn_cons = [("t", ctx.cnt["t"]), ("t", ctx.cnt["t"])]
          ctx.wait(S, "v", ctx.cnt["v"])
          for oc in range(4):
                ctx.wait(T, "s", ctx.cnt["s"])
                tmk = 0
                for icg in range(11):
                    sl = icg % 2
                    ch, cv = dn_cons[sl]
                    ctx.wait(G, ch, cv)
                    ctx.dma(G, f"dn{sl}", dnt[sl].rearrange("p (c o) -> p c o", o=512),
                            B.tern_dn_d[icg * 512:(icg + 1) * 512,
                                        oc * 512:(oc + 1) * 512]
                            .rearrange("(c p) o -> p c o", p=128))
                    ctx.waitd(T, f"dn{sl}")
                    for j, tt in enumerate(tts):
                        for i in range(4):
                            ic = icg * 4 + i
                            tmk = ctx.mark("t", T.matmul(
                                [B.pb0, B.pb1][j][:, 0:512],
                                B.arA[:, j * 5632 + ic * 128:j * 5632 + (ic + 1) * 128],
                                dnt[sl][:, i * 512:(i + 1) * 512],
                                start=(ic == 0), stop=(ic == 43)))
                    dn_cons[sl] = ("t", tmk)
                for j, tt in enumerate(tts):
                    it = b * 4 + tt
                    ctx.wait(S, "t", tmk)
                    ctx.waitd(S, f"gf{j}")
                    s_g = ctx.s("activation",
                                B.arC[:, 11264 + j * 512:11264 + (j + 1) * 512],
                                [B.pb0, B.pb1][j][:, 0:512], AF.Silu,
                                scale=c2c[:, tt:tt + 1])
                    ctx.wait(G, "s", s_g)
                    ctx.dma(G, f"gf{j}",
                            B.gffn_d[it * 128:(it + 1) * 128, oc * 512:(oc + 1) * 512],
                            B.arC[:, 11264 + j * 512:11264 + (j + 1) * 512])
        for hsl in range(2):
            ctx.waitd(G, f"hrd{hsl}")
    for oc in range(2):
        ctx.waitd(G, f"gf{oc}")
    B.c_ffn_done = (ctx.cnt["t"], ctx.cnt["v"], ctx.cnt["s"])


def _ph0_init(ctx, B):
    T, V, S, G, Y = ctx.T, ctx.V, ctx.S, ctx.G, ctx.Y

    for dst, src in ((B.ident_t, B.identc), (B.identf_t, B.identf),
                     (B.trilS_t, B.trilS), (B.trilI_t, B.trilI),
                     (B.rep32_t, B.rep32), (B.ones1_t, B.ones1),
                     (B.ones1f_t, B.ones1f), (B.onescf_t, B.onescf),
                     (B.trilIf_t, B.trilIf), (B.bd8m_t, B.bd8m),
                     (B.onescb_t, B.onescb), (B.onesff_t, B.onesff),
                     (B.sel96f_t, B.sel96f)):
        ctx.dma(Y, "ini", dst[:], src[:, :])
    ctx.dma(Y, "ini", B.gw5_t[:], B.gw5[:, :])
    ctx.dma(Y, "ini", B.baug_t[0:1, 0:5], B.gb5[:, :])
    ctx.dma(Y, "ini", B.mlpb_t[0:32, 0:1], B.mlpb1[:, :])
    ctx.dma(Y, "ini", B.mlpb_t[0:32, 1:2], B.mlpb2[:, :])
    ctx.dma(Y, "ini", B.he_t[:], B.he_s[:, :])
    ctx.dma(Y, "ini", B.alog_t[:], B.alog_s[:, :])
    ctx.dma(Y, "ini", B.dtb_t[:], B.dtb_s[:, :])
    ctx.dma(Y, "ini", B.onorm_t[:], B.onorm[:, :])
    ctx.dma(Y, "ini", B.g2b_t[0:128, 0:D // 128], B.g2b[:, :])
    c_small = ctx.dcnt["ini"]

    ctx.dma(G, "wts", B.wq_t[:].rearrange("p (c o) -> p c o", o=256), B.wq_s.rearrange("(c p) o -> p c o", p=128))
    ctx.dma(G, "wts", B.wk_t[:].rearrange("p (c o) -> p c o", o=256), B.wk_s.rearrange("(c p) o -> p c o", p=128))
    ctx.dma(G, "wts", B.wv_t[:].rearrange("p (c o) -> p c o", o=256), B.wv_s.rearrange("(c p) o -> p c o", p=128))
    ctx.dma(G, "wts", B.proxy_t[:], B.proxy[:, :])
    ctx.dma(G, "wts", B.unc_t[:].rearrange("p (c o) -> p c o", o=BINS), B.unc.rearrange("(c p) o -> p c o", p=128))
    ctx.dma(G, "wts", B.mlp1_t[:], B.mlp1[:, :])
    ctx.dma(G, "wts", B.mlp2_t[:], B.mlp2[:, :])
    for j, src in enumerate((B.cq_s, B.ck_s, B.cv_s)):
        for h in range(HPC):
            ctx.dma(G, "wts", B.convw[:, (j * HPC + h) * 4:(j * HPC + h) * 4 + 4],
                    src[h * 128:(h + 1) * 128, :])
    c_gw = ctx.dcnt["wts"]

    ctx.waitd(V, "ini", c_small)
    ctx.waitd(V, "wts", c_gw)
    sr = B.arD[0:32, 0:5]
    ctx.v("tensor_add", sr[0:32, 0:1], B.gw5_t[0:32, 0:1], B.gw5_t[0:32, 1:2])
    ctx.v("tensor_sub", sr[0:32, 1:2], B.gw5_t[0:32, 0:1], B.gw5_t[0:32, 1:2])
    ctx.v("tensor_copy", sr[0:32, 2:3], B.gw5_t[0:32, 2:3])
    ctx.v("tensor_add", sr[0:32, 3:4], B.gw5_t[0:32, 3:4], B.gw5_t[0:32, 4:5])
    ctx.v("tensor_sub", sr[0:32, 4:5], B.gw5_t[0:32, 3:4], B.gw5_t[0:32, 4:5])
    if _PH0GAUG:
        ctx.v("memset", B.gaug_t[:], 0.0)
        ctx.v("tensor_copy", B.gaug_t[0:32, 0:5], sr)
    b = B.baug_t
    ctx.v("tensor_add", b[0:1, 8:9], b[0:1, 0:1], b[0:1, 1:2])
    ctx.v("tensor_sub", b[0:1, 9:10], b[0:1, 0:1], b[0:1, 1:2])
    ctx.v("tensor_copy", b[0:1, 10:11], b[0:1, 2:3])
    ctx.v("tensor_add", b[0:1, 11:12], b[0:1, 3:4], b[0:1, 4:5])
    v_aug = ctx.v("tensor_sub", b[0:1, 12:13], b[0:1, 3:4], b[0:1, 4:5])

    ctx.wait(T, "v", v_aug)
    t1 = ctx.mark("t", T.matmul(B.pT2[0:5, 0:1], b[0:1, 8:13],
                                B.ones1f_t[0:1, 0:1], start=True, stop=True))
    ctx.v("memset", B.baugc_t[0:32, 0:1], 0.0)
    ctx.wait(V, "t", t1)
    v_bc = ctx.v("tensor_copy", B.baugc_t[0:5, 0:1], B.pT2[0:5, 0:1])
    ctx.wait(Y, "v", v_bc)
    if _PH0REP:
        for q in range(1, 4):
            ctx.dma(Y, "rep", B.gaug_t[32 * q:32 * q + 32, 32 * q:32 * q + 5],
                    B.gaug_t[0:32, 0:5])
            ctx.dma(Y, "rep", B.baugc_t[32 * q:32 * q + 5, 0:1], B.baugc_t[0:5, 0:1])
    B.c_rep = ctx.dcnt.get("rep", 0)

    ctx.wait(T, "v", v_bc)
    T.matmul(B.pT2[:, 1:2], B.rep32_t[:, :], B.mlpb_t[0:32, 0:1], start=True, stop=True)
    t2 = ctx.mark("t", T.matmul(B.pT2[:, 2:3], B.rep32_t[:, :], B.mlpb_t[0:32, 1:2],
                                start=True, stop=True))
    ctx.wait(V, "t", t2)
    ctx.v("tensor_copy", B.b1rep_t[:], B.pT2[:, 1:2])
    v_b12 = ctx.v("tensor_copy", B.b2rep_t[:], B.pT2[:, 2:3])

    ctx.waitd(S, "ini", c_small)
    sp = B.arD[0:128, 16:16 + HPC]
    ab_ = B.arD[0:128, 20:20 + HPC]
    ctx.s("activation", ab_, B.dtb_t[:], AF.Abs)
    ctx.s("activation", sp, ab_, AF.Exp, scale=-1.0)
    s_sp = ctx.s("activation", sp, sp, AF.Ln, bias=1.0)
    s_ea = ctx.s("activation", B.scl_t[0:1, 0:HPC], B.alog_t[:], AF.Exp)
    ctx.wait(V, "s", s_sp)
    ctx.v("tensor_scalar", ab_, B.dtb_t[:], 0.0, None, ALU.max)
    v_sp = ctx.v("tensor_add", sp, sp, ab_)
    ctx.wait(T, "s", s_ea)
    ctx.wait(T, "v", v_b12)
    t3 = ctx.mark("t", T.matmul(B.pT2[:, 4:4 + HPC], B.ones1f_t[0:1, :],
                                B.scl_t[0:1, 0:HPC], start=True, stop=True))
    ctx.wait(V, "t", t3)
    ctx.wait(V, "v", v_sp)
    ctx.v("scalar_tensor_tensor", B.gcol_t[:], B.pT2[:, 4:4 + HPC], -1.0, sp,
          ALU.mult, ALU.mult)
    ctx.v("tensor_scalar", B.ngcol_t[:], B.gcol_t[:], -1.0, None, ALU.mult)

    gn_row = B.arD[0:1, 1024:1024 + D]
    ctx.dma(Y, "nrm", gn_row, B.gnorm[:, :])
    ctx.dma(Y, "nrm", B.dnrow_t[:], B.dnorm[:, :])
    ctx.waitd(T, "nrm")
    for j in range(D // 512):
        ctx.wait(T, "v", ctx.cnt["v"])
        tmk = ctx.mark("t", T.matmul(B.pb0[:, :], B.ones1f_t[0:1, :],
                                     gn_row[0:1, j * 512:(j + 1) * 512],
                                     start=True, stop=True))
        ctx.wait(V, "t", tmk)
        ctx.v("tensor_copy", B.gnb_t[:, j * 512:(j + 1) * 512], B.pb0[:, :])
    if _PH0HEB:
        ctx.waitd(T, "ini", c_small)
        ctx.wait(T, "v", ctx.cnt["v"])
        t4 = ctx.mark("t", T.matmul(B.pT2[:, 12:12 + HPC * 4], B.ones1f_t[0:1, :],
                                    B.he_t[0:1, :], start=True, stop=True))
        ctx.wait(V, "t", t4)
        ctx.v("tensor_copy", B.hebc_t[:], B.pT2[:, 12:12 + HPC * 4])
    B.c_init_v = ctx.cnt["v"]
    B.c_init_t = ctx.cnt["t"]
    B.c_wts = c_gw


def _ph1_ln(ctx, B):
    T, V, S, G, Y = ctx.T, ctx.V, ctx.S, ctx.G, ctx.Y
    NTT = B.NTT

    xbuf = [B.arB[:, 0:2048], B.arB[:, 2048:4096]]
    xnbuf = [B.arA[:, 8192:10240], B.arA[:, 10240:12288]]
    xnf = [B.arB[:, 6656:8704], B.arB[:, 8704:10752]]
    sq_junk = B.arB[:, 4096:6144]
    tmpv = B.arD[0:128, 40:41]
    tmpw = B.arD[0:128, 41:42]

    v_done = [B.c_init_v, B.c_init_v]
    s_done = [0, 0]
    tr_done = [0, 0]
    xn_rd_t = [0, 0]

    for i in range(NTT):
        bsl = i % 2
        ctx.wait(S, "v", v_done[bsl])
        ctx.wait(S, "s", s_done[bsl])
        ctx.dma(S, f"x{bsl}", xbuf[bsl], B.x_sl[i * 128:(i + 1) * 128, :])
        ctx.waitd(S, f"x{bsl}")
        s_m = ctx.s("activation", sq_junk, xbuf[bsl], AF.Square,
                    accum_out=B.stat_t[:, 4 * i + 2:4 * i + 3])
        s_done[bsl] = s_m
        ctx.waitd(V, f"x{bsl}")
        ctx.v("reduce_sum", B.stat_t[:, 4 * i:4 * i + 1], xbuf[bsl], axis=AX.X)
        ctx.wait(V, "s", s_m)
        mu = B.stat_t[:, 4 * i:4 * i + 1]
        s2 = B.stat_t[:, 4 * i + 2:4 * i + 3]
        rs = B.stat_t[:, 4 * i + 1:4 * i + 2]
        rf = B.stat_t[:, 4 * i + 2:4 * i + 3]
        ctx.v("tensor_scalar", mu, mu, 1.0 / D, None, ALU.mult)
        ctx.v("tensor_scalar", tmpv, s2, 1.0 / D, None, ALU.mult)
        ctx.v("scalar_tensor_tensor", tmpw, mu, mu, tmpv, ALU.mult, ALU.subtract)
        ctx.v("tensor_scalar", tmpw, tmpw, -1.0, LN_EPS, ALU.mult, ALU.add)
        v_r = ctx.v("reciprocal", tmpv, tmpw)
        ctx.wait(S, "v", v_r)
        s_r = ctx.s("activation", rs, tmpv, AF.Sqrt)
        ctx.wait(V, "s", s_r)
        ctx.v("tensor_scalar", tmpw, tmpw, -LN_EPS, None, ALU.add)
        ctx.v("tensor_tensor", tmpw, tmpw, tmpv, ALU.mult)
        ctx.v("tensor_scalar", tmpw, tmpw, NORM_EPS, None, ALU.add)
        ctx.v("reciprocal", tmpw, tmpw)
        ctx.wait(S, "v")
        s_f = ctx.s("activation", rf, tmpw, AF.Sqrt)
        ctx.wait(V, "s", s_f)
        ctx.waitd(V, f"xnst{bsl}")
        ctx.wait(V, "t", xn_rd_t[bsl])
        ctx.v("tensor_scalar", xnf[bsl], xbuf[bsl], mu, rs,
              ALU.subtract, ALU.mult)
        v_xn = ctx.v("tensor_copy", xnbuf[bsl], xnf[bsl])
        v_done[bsl] = v_xn
        ctx.wait(G, "v", v_xn)
        ctx.waitd(G, f"xnst{bsl}")
        ctx.dma(G, f"xnst{bsl}", B.xn_tm_d[i * 128:(i + 1) * 128, :], xnf[bsl])
        for j in range(16 if _PH1 >= 2 else 0):
            jj = i * 16 + j
            bank = [B.pT1a, B.pT1b][jj % 2]
            slot = (jj // 2) % 8
            ctx.wait(T, "v", max(tr_done[jj % 2], v_xn))
            tt = ctx.mark("t", T.transpose(bank[:, slot * 128:(slot + 1) * 128],
                                           xnbuf[bsl][:, j * 128:(j + 1) * 128],
                                           B.ident_t[:]))
            fmslot = j % 4
            fmbuf = B.arC[:, fmslot * 128:(fmslot + 1) * 128]
            ctx.waitd(V, f"fm{fmslot}")
            ctx.wait(V, "t", tt)
            v_fm = ctx.v("tensor_copy", fmbuf, bank[:, slot * 128:(slot + 1) * 128])
            tr_done[jj % 2] = v_fm
            ctx.wait(G, "v", v_fm)
            ctx.dma(G, f"fm{fmslot}", B.ag_in[j * 128:(j + 1) * 128, i * 128:(i + 1) * 128], fmbuf)
        xn_rd_t[bsl] = ctx.cnt["t"]
    c_ln_v = ctx.cnt["v"]

    fmb = [B.arC[:, 512:512 + 2048], B.arC[:, 2560:2560 + 2048]]
    zt = B.arB[:, 6144:6208]
    ez = B.arB[:, 6208:6272]
    junk = B.arB[:, 6272:6336]
    zmax = B.arD[0:128, 44:45]
    sez = B.arD[0:128, 45:46]
    pz = B.arD[0:128, 46:47]
    lse = B.arD[0:128, 47:48]
    isz = B.arD[0:128, 48:49]
    for fs in range(4):
        ctx.waitd(Y, f"fm{fs}")
    eb_free = [0, 0]
    pb1_free = 0
    for i in range(NTT if _PH1 >= 3 else 0):
        eb = fmb[i % 2]
        ctx.wait(Y, "t", eb_free[i % 2])
        ctx.dma(Y, f"eb{i % 2}", eb.rearrange("p (c t) -> p c t", t=128),
                B.ag_in[0:D, i * 128:(i + 1) * 128].rearrange("(c p) t -> p c t", p=128))
        ctx.waitd(T, f"eb{i % 2}")
        ctx.wait(T, "v", pb1_free)
        tmk = 0
        for c in range(16):
            tmk = ctx.mark("t", T.matmul(B.pb1[:, 0:64], eb[:, c * 128:(c + 1) * 128],
                                         B.unc_t[:, c * 64:(c + 1) * 64],
                                         start=(c == 0), stop=(c == 15)))
        eb_free[i % 2] = tmk
        ctx.wait(V, "t", tmk)
        ctx.v("reduce_max", zmax, B.pb1[:, 0:64], axis=AX.X)
        v_z = ctx.v("tensor_scalar", zt, B.pb1[:, 0:64], zmax, None, ALU.subtract)
        pb1_free = v_z
        ctx.wait(S, "v", v_z)
        s_e = ctx.s("activation", ez, zt, AF.Exp, accum_out=sez)
        ctx.wait(V, "s", s_e)
        ctx.v("scalar_tensor_tensor", junk, ez, 1.0, zt, ALU.mult, ALU.mult,
              accum_out=pz)
        v_p = ctx.cnt["v"]
        ctx.wait(S, "v", v_p)
        s_l = ctx.s("activation", lse, sez, AF.Ln)
        ctx.wait(V, "s", s_l)
        ctx.v("reciprocal", isz, sez)
        ctx.v("tensor_tensor", pz, pz, isz, ALU.mult)
        ctx.v("tensor_sub", lse, lse, pz)
        ctx.v("tensor_scalar", B.stat_t[:, 4 * i + 3:4 * i + 4], lse,
              1.0 / LOGBINS, None, ALU.mult)
    if _PH1 < 4:
        return
    ctx.wait(T, "v", ctx.cnt["v"])
    entcols = B.stat_t[:].rearrange("p (i f) -> p i f", f=4)[:, :, 3]
    te = ctx.mark("t", T.transpose(B.pT2[0:NTT, 0:128], entcols, B.identf_t[:]))
    ctx.wait(V, "t", te)
    for fs in range(4):
        ctx.waitd(V, f"fm{fs}")
    erow = B.arC[0:NTT, 0:128]
    v_er = ctx.v("tensor_copy", erow, B.pT2[0:NTT, 0:128])
    ctx.wait(G, "v", v_er)
    ctx.dma(G, "agent", B.ag_in[D:D + 1, :].rearrange("r (i t) -> (r i) t", t=128), erow)


def _ph4_scan(ctx, B):
    T, V, S, G, Y = ctx.T, ctx.V, ctx.S, ctx.G, ctx.Y
    NWIN = B.NWIN
    TPC = B.TPC
    PB = [B.pb0, B.pb1, B.pb2, B.pb3, B.pb4]

    xnc = B.arC[:, 5760:13952]
    qkv = B.arA[:, 8192:11264]
    vhat = B.arA[:, 11264:12288]
    HB = [0, 12288]
    O_M8I, O_M8O, O_S8I, O_E8O, O_LAM, O_ETL = 0, 512, 1024, 1536, 2048, 2560
    O_D16I, O_D16O, O_D32I, O_D32O, O_D64I, O_D64O = 3072, 3328, 3840, 4096, 4352, 4608
    O_QM8, O_QS8, O_QTL, O_Q16, O_Q32, O_Q64 = 4864, 5376, 5888, 6400, 6656, 6912

    vtm = [B.arC[:, 0:128], B.arC[:, 128:256]]
    _CSCOFF = (512, 768, 1024, 1152, 1536, 1792, 2048)
    _CSCSTride = (128, 128, 256, 256, 128, 128, 128)
    def csc(ip, k):
        o = _CSCOFF[k] + ip * _CSCSTride[k]
        return B.arC[:, o:o + 128]
    ABblk = B.arC[:, 1024:1536]
    Ublk = B.arC[:, 1536:1792]
    ofm = [B.arC[:, 2304:2816], B.arC[:, 2816:3328]]
    selt = B.arC[:, 3328:3456]
    selt2 = B.arC[:, 3328:3584]
    ftm = B.arC[:, 3584:3840]
    gfm = B.arC[:, 3840:4096]
    hf1 = B.arC[:, 4096:4224]
    hf2 = B.arC[:, 4224:4352]
    ffm = B.arC[:, 4352:4608]
    sqt = [B.arC[:, 4608:5120], B.arC[:, 5120:5632]]
    entt = B.arC[:, 5632:5760]

    PA = [B.arB[:, i * 520:(i + 1) * 520] for i in range(4)]
    esc = [B.arB[:, 2080:2592], B.arB[:, 2592:3104]]
    convin = [B.arB[:, 3104:3620], B.arB[:, 3620:4136]]
    convacc = [B.arB[:, 4136:4648], B.arB[:, 4648:5160]]
    aspc = B.arB[:, 5160:5672]
    arow = B.arB[:, 5672:6184]
    omx = B.arB[:, 6184:6312]
    ot1 = B.arB[:, 6312:6440]
    ot2 = B.arB[:, 6440:6568]
    krow = B.arB[:, 6568:7592]
    ojnk = B.arD[:, 2354:2482]
    gtm = B.arD[:, 2048:2304]
    nbet = B.arD[:, 2304:2320]
    fsc = B.arD[:, 2320:2336]
    sqc = B.arD[0:128, 2336:2337]
    rcl = B.arD[0:128, 2337:2338]
    ampc = B.arD[:, 2338:2354]
    ebuf = [esc[0], esc[1], convacc[0], convacc[1]]

    for e in (T, V, S, G, Y):
        ctx.wait(e, "cc", B.c_ag)
    ctx.wait(V, "t", B.c_ffn_done[0])
    ctx.wait(S, "t", B.c_ffn_done[0])
    ctx.wait(Y, "t", B.c_ffn_done[0])
    ctx.wait(T, "s", B.c_ffn_done[2])
    ctx.wait(T, "v", B.c_ffn_done[1])
    ctx.waitd(T, "wts", B.c_wts)
    ctx.waitd(V, "wts", B.c_wts)
    ctx.waitd(T, "rep", B.c_rep)
    ctx.waitd(S, "rep", B.c_rep)
    for e in (V, S, Y):
        ctx.waitd(e, "gf0")
        ctx.waitd(e, "gf1")
    ctx.wait(Y, "v", ctx.cnt["v"])
    ctx.wait(Y, "s", ctx.cnt["s"])
    ctx.wait(G, "v", ctx.cnt["v"])
    ctx.v("memset", B.Sf_t[:], 0.0)
    ctx.v("memset", B.Sb_t[:], 0.0)
    ctx.v("memset", B.carry_t[:], 0.0)
    ctx.v("memset", ftm, 0.0)
    ctx.v("memset", gfm, 0.0)
    ctx.v("memset", aspc, 0.0)
    v_init = ctx.cnt["v"]

    t_prev_end = B.c_ffn_done[0]
    v_prev_end = v_init

    def xn_load(wn, tdep):
        sc = (wn * 512) // TPC
        scol = (wn * 512) % TPC
        ctx.wait(Y, "t", tdep)
        for qq in range(4):
            ctx.dma(Y, f"xw{qq}",
                    xnc[:, qq * 2048:(qq + 1) * 2048].rearrange("p (c t) -> p c t", t=512),
                    B.ag_out[sc * B.AGROW + qq * 512:sc * B.AGROW + (qq + 1) * 512,
                             scol:scol + 512].rearrange("(c p) t -> p c t", p=128))

    xn_load(0, B.c_ffn_done[0])

    for w in range(NWIN):
        wsl = w % 2
        src_core = (w * 512) // TPC
        src_col = (w * 512) % TPC
        ctx.wait(Y, "v", v_prev_end)
        ctx.dma(Y, "ew", B.arC[0:4, 5632:5760],
                B.ag_out[src_core * B.AGROW + D:src_core * B.AGROW + D + 1,
                         src_col:src_col + 512].rearrange("r (c p) -> (r c) p", p=128))
        ctx.wait(T, "v", v_prev_end)
        for qq in range(4):
            ctx.waitd(T, f"xw{qq}")
        s_conv = [0, 0]
        v_conv = [0, 0]
        for jt in range(6):
            j, h = jt // 2, jt % 2
            pb = PB[jt % 2]
            wt = [B.wq_t, B.wk_t, B.wv_t][j]
            ctx.wait(T, "v", v_conv[jt % 2])
            tm = 0
            for ic in range(16):
                tm = ctx.mark("t", T.matmul(pb[:, 0:512],
                                            wt[:, ic * 256 + h * 128:ic * 256 + (h + 1) * 128],
                                            xnc[:, ic * 512:(ic + 1) * 512],
                                            start=(ic == 0), stop=(ic == 15)))
            sl = jt % 2
            ctx.wait(V, "t", tm)
            ctx.wait(V, "s", s_conv[sl])
            ctx.v("tensor_copy", convin[sl][:, 3:515], pb[:, 0:512])
            ctx.v("tensor_copy", convin[sl][:, 0:3],
                  B.carry_t[:, (j * HPC + h) * 3:(j * HPC + h) * 3 + 3])
            wc = B.convw[:, (j * HPC + h) * 4:(j * HPC + h) * 4 + 4]
            ctx.v("tensor_scalar", convacc[sl], convin[sl][:, 0:512], wc[:, 0:1],
                  None, ALU.mult)
            for tau in range(1, 4):
                ctx.v("scalar_tensor_tensor", convacc[sl], convin[sl][:, tau:tau + 512],
                      wc[:, tau:tau + 1], convacc[sl], ALU.mult, ALU.add)
            v_conv[sl] = ctx.v("tensor_copy", B.carry_t[:, (j * HPC + h) * 3:(j * HPC + h) * 3 + 3],
                               convin[sl][:, 512:515])
            ctx.wait(S, "v", v_conv[sl])
            s_conv[sl] = ctx.s("activation", qkv[:, jt * 512:(jt + 1) * 512],
                               convacc[sl], AF.Silu)
        if w + 1 < NWIN:
            xn_load(w + 1, tm)
        ctx.wait(T, "s", max(s_conv))
        ctx.wait(V, "s", max(s_conv))
        ctx.waitd(T, "ew")
        tent = ctx.mark("t", T.transpose(B.pT1a[:, 896:900], B.arC[0:4, 5632:5760],
                                         B.ident_t[0:4, 0:4]))
        ctx.wait(V, "t", tent)
        ctx.v("tensor_copy", entt[:, 0:4], B.pT1a[:, 896:900])
        nps = PB[2]
        t_vh = [0, 0]
        for h in range(HPC):
            kraw = qkv[:, (2 + h) * 512:(3 + h) * 512]
            vraw = qkv[:, (4 + h) * 512:(5 + h) * 512]
            tm = ctx.mark("t", T.matmul(PB[3][:, 0:512], B.proxy_t[:], kraw,
                                        start=True, stop=True))
            ctx.wait(V, "t", tm)
            ctx.v("tensor_copy", vhat[:, h * 512:(h + 1) * 512], PB[3][:, 0:512])
            krow_h = B.arB[0:1, 6568 + h * 512:6568 + (h + 1) * 512]
            ctx.wait(S, "v", ctx.cnt["v"])
            s_k2 = ctx.s("activation", sqt[0], kraw, AF.Square)
            ctx.wait(T, "s", s_k2)
            tm = ctx.mark("t", T.matmul(PB[3][0:1, 0:512],
                                        B.onescb_t[:], sqt[0], start=True, stop=True))
            ctx.wait(S, "t", tm)
            ctx.wait(S, "v", ctx.cnt["v"])
            s_sq = ctx.s("activation", krow_h, PB[3][0:1, 0:512], AF.Sqrt)
            ctx.wait(V, "s", s_sq)
            ctx.v("tensor_scalar", krow_h, krow_h, EPS, None, ALU.add)
            v_kr = ctx.v("reciprocal", krow_h, krow_h)
            ctx.wait(T, "v", v_kr)
            tm = ctx.mark("t", T.matmul(PB[3][:, 0:512], B.ones1f_t[0:1, :],
                                        krow_h, start=True, stop=True))
            ctx.wait(V, "t", tm)
            ctx.v("tensor_tensor", kraw, kraw, PB[3][:, 0:512], ALU.mult)
            ctx.wait(S, "v", ctx.cnt["v"])
            s_v2 = ctx.s("activation", sqt[0], vraw, AF.Square)
            ctx.wait(T, "s", s_v2)
            tm0 = 0
            for c in range(4):
                tm0 = ctx.mark("t", T.matmul(nps[:, c * 8 + h * 4:c * 8 + h * 4 + 1],
                                             sqt[0][:, c * 128:(c + 1) * 128],
                                             B.onescb_t[:], start=True, stop=True))
            s_h2 = ctx.s("activation", sqt[1], vhat[:, h * 512:(h + 1) * 512], AF.Square)
            ctx.wait(T, "s", s_h2)
            for c in range(4):
                tm0 = ctx.mark("t", T.matmul(nps[:, c * 8 + h * 4 + 1:c * 8 + h * 4 + 2],
                                             sqt[1][:, c * 128:(c + 1) * 128],
                                             B.onescb_t[:], start=True, stop=True))
            ctx.wait(V, "t", tm0)
            v_vv = ctx.v("tensor_tensor", sqt[0], vraw, vhat[:, h * 512:(h + 1) * 512],
                         ALU.mult)
            ctx.wait(T, "v", v_vv)
            for c in range(4):
                tm0 = ctx.mark("t", T.matmul(nps[:, c * 8 + h * 4 + 2:c * 8 + h * 4 + 3],
                                             sqt[0][:, c * 128:(c + 1) * 128],
                                             B.onescb_t[:], start=True, stop=True))
            t_vh[h] = tm0
        ctx.wait(V, "t", max(t_vh))
        nsc = B.arD[:, 2482:2514]
        ctx.v("tensor_copy", nsc, nps[:, 0:32])
        v2v = nsc.rearrange("p (c h t) -> p (c h) t", h=2, t=4)[:, :, 0]
        vh2v = nsc.rearrange("p (c h t) -> p (c h) t", h=2, t=4)[:, :, 1]
        vvhv = nsc.rearrange("p (c h t) -> p (c h) t", h=2, t=4)[:, :, 2]
        f_out = lambda f: ftm[:].rearrange("p (i q) -> p i q", q=32)[:, :, f]
        t1 = fsc[:, 0:8]
        t2 = fsc[:, 8:16]
        ctx.v("tensor_copy",
              ftm[:].rearrange("p (c h q) -> p c h q", h=2, q=32)[:, :, :, 0],
              entt[:, 0:4].unsqueeze(2).to_broadcast([128, 4, 2]))
        ctx.v("tensor_add", t1, v2v, vh2v)
        ctx.v("scalar_tensor_tensor", t1, vvhv, -2.0, t1, ALU.mult, ALU.add)
        ctx.v("tensor_scalar", t1, t1, 0.0, None, ALU.max)
        v_e2 = ctx.cnt["v"]
        ctx.wait(S, "v", v_e2)
        s_en = ctx.s("activation", t1, t1, AF.Sqrt)
        s_vn = ctx.s("activation", t2, v2v, AF.Sqrt)
        ctx.wait(V, "s", s_vn)
        ctx.v("tensor_scalar", t2, t2, EPS, None, ALU.add)
        ctx.v("reciprocal", t2, t2)
        ctx.v("tensor_tensor", f_out(1), t1, t2, ALU.mult)
        ctx.wait(S, "v", ctx.cnt["v"])
        s_l1 = ctx.s("activation", f_out(3), t1, AF.Ln, bias=1.0)
        s_vn2 = ctx.s("activation", t1, v2v, AF.Sqrt)
        s_hn = ctx.s("activation", t2, vh2v, AF.Sqrt)
        ctx.wait(V, "s", s_hn)
        ctx.v("tensor_tensor", t1, t1, t2, ALU.mult)
        ctx.v("tensor_scalar", t1, t1, EPS, None, ALU.add)
        ctx.v("reciprocal", t1, t1)
        ctx.v("tensor_tensor", f_out(2), vvhv, t1, ALU.mult)
        v_fe = ctx.v("tensor_copy",
                     ftm[:].rearrange("p (c h q) -> p c h q", h=2, q=32)[:, :, :, 4:8],
                     B.hebc_t[:, 0:8].rearrange("p (h f) -> p h f", f=4)
                         .unsqueeze(1).to_broadcast([128, 4, 2, 4]))
        s_gate = 0
        for ti in range(2):
            ctx.wait(T, "v", v_fe)
            ctx.wait(T, "s", s_gate)
            tt1 = ctx.mark("t", T.transpose(B.pT1a[:, 0:128] if ti == 0 else B.pT1b[:, 0:128],
                                            ftm[:, ti * 128:(ti + 1) * 128], B.ident_t[:]))
            ctx.wait(V, "t", tt1)
            v_ff = ctx.v("tensor_copy", ffm[:, ti * 128:(ti + 1) * 128],
                         (B.pT1a if ti == 0 else B.pT1b)[:, 0:128])
            ctx.wait(T, "v", v_ff)
            tmm = ctx.mark("t", T.matmul(PB[3][:, 0:128], B.mlp1_t[:],
                                         ffm[:, ti * 128:(ti + 1) * 128],
                                         start=True, stop=True))
            ctx.wait(S, "t", tmm)
            ctx.wait(S, "v", ctx.cnt["v"])
            s_h1 = ctx.s("activation", hf1, PB[3][:, 0:128], AF.Silu, bias=B.b1rep_t[:, 0:1])
            ctx.wait(T, "s", s_h1)
            tmm = ctx.mark("t", T.matmul(PB[4][:, 0:128], B.mlp2_t[:], hf1,
                                         start=True, stop=True))
            ctx.wait(S, "t", tmm)
            s_h2b = ctx.s("activation", hf2, PB[4][:, 0:128], AF.Silu, bias=B.b2rep_t[:, 0:1])
            ctx.wait(T, "s", s_h2b)
            tmm = ctx.mark("t", T.matmul(PB[3][:, 128:256], B.gaug_t[:], hf2,
                                         start=True, stop=True))
            for q in range(4):
                ctx.wait(S, "t", tmm)
                s_gate = ctx.s("activation", gfm[:, ti * 128:(ti + 1) * 128][32 * q:32 * q + 5, :],
                               PB[3][32 * q:32 * q + 5, 128:256], AF.Sigmoid,
                               bias=B.baugc_t[32 * q:32 * q + 5, 0:1])
        for ti in range(2):
            ctx.wait(T, "s", s_gate)
            tg = ctx.mark("t", T.transpose(B.pT1a[:, 128:256] if ti == 0 else B.pT1b[:, 128:256],
                                           gfm[:, ti * 128:(ti + 1) * 128], B.ident_t[:]))
            ctx.wait(V, "t", tg)
            ctx.v("tensor_copy", gtm[:, ti * 128:(ti + 1) * 128],
                  (B.pT1a if ti == 0 else B.pT1b)[:, 128:256])
        ctx.v("tensor_scalar", nbet.rearrange("p (i q) -> p i q", q=2),
              gtm[:].rearrange("p (i q) -> p i q", q=32)[:, :, 0:2],
              -1.0, None, ALU.mult)
        ctx.v("tensor_copy", ampc.rearrange("p (i q) -> p i q", q=2),
              gtm[:].rearrange("p (i q) -> p i q", q=32)[:, :, 3:5])
        ctx.wait(T, "v", ctx.cnt["v"])
        ta = ctx.mark("t", T.matmul(PB[3][:, 0:16], B.trilIf_t[:], ampc,
                                    start=True, stop=True))
        ctx.wait(V, "t", ta)
        for hr in range(4):
            ctx.v("tensor_copy", aspc[:, hr * 128:(hr + 1) * 128].rearrange("p (c z) -> p c z", z=32)[:, :, 0:1],
                  PB[3][:, 0:16].rearrange("p (c q) -> p c q", q=4)[:, :, hr:hr + 1])
        for hr in range(4):
            ctx.wait(T, "v", ctx.cnt["v"])
            tt2 = ctx.mark("t", T.transpose(B.pT2[:, 0:128], aspc[:, hr * 128:(hr + 1) * 128],
                                            B.identf_t[:]))
            ctx.wait(V, "t", tt2)
            ctx.v("tensor_copy", arow[:, hr * 128:(hr + 1) * 128], B.pT2[:, 0:128])
        for hr in range(4):
            ctx.v("memset", PA[hr][:, 0:520].rearrange("p (c z) -> p c z", z=130)[:, :, 0:1], 0.0)
        for hr in range(4):
            for cc in range(4):
                ctx.wait(T, "v", ctx.cnt["v"])
                rb, rk = (64, 33) if cc == 3 else (32 * cc, 1)
                lsel = B.sel96f_t if cc == 3 else B.onesff_t
                tb = ctx.mark("t", T.matmul(
                    B.pT2[:, 128:256], lsel[rb:rb + rk, :],
                    arow[rb:rb + rk, hr * 128:(hr + 1) * 128],
                    start=True, stop=True))
                ctx.wait(V, "t", tb)
                ctx.v("tensor_copy", PA[hr][:, cc * 130 + 1:cc * 130 + 129], B.pT2[:, 128:256])
        v_pa = ctx.cnt["v"]
        _scan_pair_loop(ctx, B, locals())
        t_prev_end = ctx.cnt["t"]
        v_prev_end = ctx.cnt["v"]
    B.c_scan = (ctx.cnt["t"], ctx.cnt["v"], ctx.cnt["s"], ctx.cnt["g"])


def _scan_pair_loop(ctx, B, L):
    T, V, S, G, Y = ctx.T, ctx.V, ctx.S, ctx.G, ctx.Y
    qkv = L["qkv"]
    w, wsl = L["w"], L["wsl"]
    HB, PA = L["HB"], L["PA"]
    vtm, ofm, selt = L["vtm"], L["ofm"], L["selt"]
    selt2, ABblk, Ublk = L["selt2"], L["ABblk"], L["Ublk"]
    gtm, nbet = L["gtm"], L["nbet"]
    omx, ot1, ot2, ojnk = L["omx"], L["ot1"], L["ot2"], L["ojnk"]
    csc = L["csc"]
    sqc, rcl = L["sqc"], L["rcl"]
    ebuf = L["ebuf"]
    v_pa = L["v_pa"]
    PB = [B.pb0, B.pb1, B.pb2, B.pb3, B.pb4]
    TPC = B.TPC
    (O_M8I, O_M8O, O_S8I, O_E8O, O_LAM, O_ETL, O_D16I, O_D16O, O_D32I,
     O_D32O, O_D64I, O_D64O, O_QM8, O_QS8, O_QTL, O_Q16, O_Q32, O_Q64) = (
        L["O_M8I"], L["O_M8O"], L["O_S8I"], L["O_E8O"], L["O_LAM"], L["O_ETL"],
        L["O_D16I"], L["O_D16O"], L["O_D32I"], L["O_D32O"], L["O_D64I"],
        L["O_D64O"], L["O_QM8"], L["O_QS8"], L["O_QTL"], L["O_Q16"],
        L["O_Q32"], L["O_Q64"])
    LEVS = ((16, O_D16I, O_D16O, O_Q16), (32, O_D32I, O_D32O, O_Q32),
            (64, O_D64I, O_D64O, O_Q64))

    def slab(ip, off, width=512):
        return B.arA[:, HB[ip] + off:HB[ip] + off + width]

    est = {"i": 0, "marks": [0, 0, 0, 0]}

    def enext():
        i = est["i"]
        est["i"] = (i + 1) % 4
        if est["marks"][i]:
            ctx.wait(V, "s", est["marks"][i])
        return ebuf[i], i

    def vexp(dst, src, idx, scale=None):
        ctx.wait(S, "v")
        if scale is None:
            est["marks"][idx] = ctx.s("activation", dst, src, AF.Exp)
        else:
            est["marks"][idx] = ctx.s("activation", dst, src, AF.Exp, scale=scale)

    bst = {"n": 0, "free": [0, 0]}

    def tpose(dst, src, src_v=0):
        b = bst["n"] % 2
        bst["n"] += 1
        bk = [B.pT1a, B.pT1b][b]
        ctx.wait(T, "v", max(bst["free"][b], src_v))
        tm = ctx.mark("t", T.transpose(bk[:, 384:512], src, B.ident_t[:]))
        ctx.wait(V, "t", tm)
        vm = ctx.v("tensor_copy", dst, bk[:, 384:512])
        bst["free"][b] = vm
        return vm

    pf = [0, 0]
    gb = [0, 0]

    for head in range(HPC):
        gc = B.gcol_t[:, head:head + 1]
        ngc = B.ngcol_t[:, head:head + 1]
        for ip in range(2):
            hr = head * 2 + ip
            pa = PA[hr]
            paz = pa[:].rearrange("p (c z) -> p c z", z=130)
            Av = paz[:, :, 1:129]
            Av4 = Av.rearrange("p c (b q) -> p c b q", q=8)
            As8 = paz[:, :, 0:121:8].unsqueeze(3).to_broadcast([128, 4, 16, 8])
            Ae8 = paz[:, :, 8:129:8].unsqueeze(3).to_broadcast([128, 4, 16, 8])
            K = qkv[:, (2 + head) * 512:(3 + head) * 512]
            Q = qkv[:, (0 + head) * 512:(1 + head) * 512]
            ctx.wait(V, "t", ctx.cnt["t"])
            e0, i0 = enext()
            e0c = e0[:].rearrange("p (c t) -> p c t", t=128)
            e0v = e0[:].rearrange("p (c b q) -> p c b q", b=16, q=8)
            ctx.v("tensor_tensor", e0v, As8, Ae8, ALU.add)
            ctx.v("scalar_tensor_tensor", e0c, e0c, -0.5, Av, ALU.mult, ALU.add)
            eA, iA = enext()
            ctx.v("tensor_scalar", eA, e0, gc, CLAMP, ALU.mult, ALU.min)
            vexp(slab(ip, O_M8I), eA, iA)
            eB, iB = enext()
            ctx.v("tensor_scalar", eB, e0, ngc, CLAMP, ALU.mult, ALU.min)
            vexp(slab(ip, O_M8O), eB, iB)
            eC, iC = enext()
            eCv = eC[:].rearrange("p (c b q) -> p c b q", b=16, q=8)
            ctx.v("tensor_tensor", eCv, Av4, As8, ALU.subtract)
            vexp(slab(ip, O_S8I), eC, iC, scale=gc)
            eD, iD = enext()
            eDv = eD[:].rearrange("p (c b q) -> p c b q", b=16, q=8)
            ctx.v("tensor_tensor", eDv, Ae8, Av4, ALU.subtract)
            vexp(slab(ip, O_E8O), eD, iD, scale=gc)
            ctx.wait(S, "v", L["v_pa"])
            sm = ctx.s("activation", slab(ip, O_LAM).rearrange("p (c t) -> p c t", t=128),
                       Av, AF.Exp, scale=gc)
            est["marks"][est["i"]] = max(est["marks"][est["i"]], 0)
            A127 = paz[:, :, 128:129].to_broadcast([128, 4, 128])
            eF, iF = enext()
            eFc = eF[:].rearrange("p (c t) -> p c t", t=128)
            ctx.v("tensor_tensor", eFc, A127, Av, ALU.subtract)
            vexp(slab(ip, O_ETL), eF, iF, scale=gc)
            for (LV, oin, oout, oq) in LEVS:
                nb = 128 // (2 * LV)
                cwid = nb * LV
                eg, ig = enext()
                egv = eg[:].rearrange("p (c m q) -> p c m q", m=nb, q=2 * LV)
                Bnd = paz[:, :, LV:LV + 2 * LV * (nb - 1) + 1:2 * LV]\
                    .unsqueeze(3).to_broadcast([128, 4, nb, 2 * LV])
                AvL = Av.rearrange("p c (m q) -> p c m q", q=2 * LV)
                ctx.v("tensor_tensor", egv, AvL, Bnd, ALU.subtract)
                vexp(slab(ip, oin, 4 * cwid).rearrange("p (c m q) -> p c m q",
                                                       m=nb, q=LV),
                     egv[:, :, :, LV:2 * LV], ig, scale=gc)
                if LV == 16:
                    eh, ih = enext()
                    ctx.v("tensor_scalar", eh, eg, ngc, CLAMP, ALU.mult, ALU.min)
                    vexp(slab(ip, oout, 512), eh, ih)
                else:
                    vexp(slab(ip, oout, 4 * cwid).rearrange("p (c m q) -> p c m q",
                                                            m=nb, q=LV),
                         egv[:, :, :, 0:LV], ig, scale=ngc)
            ctx.wait(V, "s")
            ctx.v("scalar_tensor_tensor", slab(ip, O_QM8), Q, QS,
                  slab(ip, O_M8I), ALU.mult, ALU.mult)
            ctx.v("scalar_tensor_tensor", slab(ip, O_QS8), Q, QS,
                  slab(ip, O_QS8 - O_QS8 + O_S8I), ALU.mult, ALU.mult)
            ctx.v("scalar_tensor_tensor", slab(ip, O_QTL), Q, QS,
                  slab(ip, O_LAM), ALU.mult, ALU.mult)
            for (LV, oin, oout, oq) in LEVS:
                nb = 128 // (2 * LV)
                cwid = nb * LV
                Qh = Q[:].rearrange("p (c m q) -> p c m q", m=nb, q=2 * LV)[:, :, :, LV:2 * LV]
                ctx.v("scalar_tensor_tensor",
                      slab(ip, oq, 4 * cwid).rearrange("p (c m q) -> p c m q", m=nb, q=LV),
                      Qh, QS,
                      slab(ip, oin, 4 * cwid).rearrange("p (c m q) -> p c m q", m=nb, q=LV),
                      ALU.mult, ALU.mult)
            v_qp = ctx.cnt["v"]
            ctx.wait(G, "s", ctx.cnt["s"])
            ctx.wait(G, "v", v_qp)
            for off in (O_M8I, O_M8O, O_S8I, O_E8O, O_LAM, O_ETL):
                ctx.g("tensor_tensor", slab(ip, off), K, slab(ip, off), ALU.mult)
            for (LV, oin, oout, oq) in LEVS:
                nb = 128 // (2 * LV)
                cwid = nb * LV
                Kin = K[:].rearrange("p (c m q) -> p c m q", m=nb, q=2 * LV)[:, :, :, LV:2 * LV]
                Kout = K[:].rearrange("p (c m q) -> p c m q", m=nb, q=2 * LV)[:, :, :, 0:LV]
                ctx.g("tensor_tensor",
                      slab(ip, oin, 4 * cwid).rearrange("p (c m q) -> p c m q", m=nb, q=LV),
                      Kin,
                      slab(ip, oin, 4 * cwid).rearrange("p (c m q) -> p c m q", m=nb, q=LV),
                      ALU.mult)
                if LV == 16:
                    ctx.g("tensor_tensor", slab(ip, oout, 512), K,
                          slab(ip, oout, 512), ALU.mult)
                else:
                    ctx.g("tensor_tensor",
                          slab(ip, oout, 4 * cwid).rearrange("p (c m q) -> p c m q", m=nb, q=LV),
                          Kout,
                          slab(ip, oout, 4 * cwid).rearrange("p (c m q) -> p c m q", m=nb, q=LV),
                          ALU.mult)
            gb[ip] = ctx.cnt["g"]

        vb = ctx.cnt["v"]

        for cc in range(4):
            csl = cc % 2
            sl = slice(cc * 128, (cc + 1) * 128)
            v_vtm = tpose(vtm[csl], qkv[:, (4 + head) * 512 + cc * 128:
                                        (4 + head) * 512 + (cc + 1) * 128])
            tmk = {}
            for ip in range(2):
                pbp = PB[ip]
                ctx.wait(T, "v", max(pf[ip], vb))
                ctx.wait(T, "g", gb[ip])
                ctx.mark("t", T.matmul(pbp[:, 0:128], slab(ip, O_E8O)[:, sl],
                                       slab(ip, O_S8I)[:, sl], start=True, stop=True))
                ctx.mark("t", T.matmul(pbp[:, 128:256], slab(ip, O_E8O)[:, sl],
                                       slab(ip, O_QS8)[:, sl], start=True, stop=True))
                ctx.mark("t", T.matmul(pbp[:, 256:384], slab(ip, O_M8O)[:, sl],
                                       slab(ip, O_M8I)[:, sl], start=True, stop=True))
                ctx.mark("t", T.matmul(pbp[:, 384:512], slab(ip, O_M8O)[:, sl],
                                       slab(ip, O_QM8)[:, sl], start=True, stop=True))
                for (LV, oin, oout, oq) in LEVS:
                    nb = 128 // (2 * LV)
                    cwid = nb * LV
                    for m in range(nb):
                        j0 = 2 * LV * m
                        i0 = j0 + LV
                        cb = cc * cwid + m * LV
                        if LV == 16:
                            jb = 64 if m == 3 else j0
                            lt = slab(ip, oout, 512)[:, cc * 128 + jb:cc * 128 + j0 + LV]
                        else:
                            jb = j0
                            lt = slab(ip, oout, 4 * cwid)[:, cb:cb + LV]
                        ctx.mark("t", T.matmul(
                            pbp[jb:j0 + LV, i0:i0 + LV], lt,
                            slab(ip, oin, 4 * cwid)[:, cb:cb + LV],
                            start=True, stop=True))
                        tmk[ip] = ctx.mark("t", T.matmul(
                            pbp[jb:j0 + LV, 128 + i0:128 + i0 + LV], lt,
                            slab(ip, oq, 4 * cwid)[:, cb:cb + LV],
                            start=True, stop=True))
            for ip in range(2):
                hr = head * 2 + ip
                pbp = PB[ip]
                nb_col = nbet[:, cc * 4 + hr:cc * 4 + hr + 1]
                b_col = gtm[:, (cc * 2 + head) * 32 + ip:(cc * 2 + head) * 32 + ip + 1]
                ctx.wait(V, "t", tmk[ip])
                ctx.v("tensor_copy", selt2, pbp[:, 0:256])
                ctx.v("copy_predicated", selt2, B.bd8m_t[:], pbp[:, 256:512])
                ctx.v("scalar_tensor_tensor", csc(ip, 0), selt2[:, 0:128], nb_col,
                      B.trilS_t[:], ALU.mult, ALU.mult)
                pf[ip] = ctx.v("scalar_tensor_tensor", csc(ip, 1), selt2[:, 128:256],
                               b_col, B.trilI_t[:], ALU.mult, ALU.mult)
            tR = {}
            for ip in range(2):
                hr = head * 2 + ip
                ctx.wait(T, "v", pf[ip])
                tR[ip] = ctx.mark("t", T.matmul(
                    PB[3][:, 256 + ip * 128:384 + ip * 128],
                    slab(ip, O_LAM)[:, sl], B.Sb_t[:, hr * 128:(hr + 1) * 128],
                    start=True, stop=True))
            for ip in range(2):
                tpose(csc(ip, 2), csc(ip, 0))
                ctx.v("tensor_copy", csc(ip, 3), csc(ip, 0))
            ctx.wait(V, "t", max(tR.values()))
            ctx.v("tensor_tensor", Ublk.rearrange("p (i t) -> p i t", t=128),
                  vtm[csl].unsqueeze(1).to_broadcast([128, 2, 128]),
                  PB[3][:, 256:512].rearrange("p (i t) -> p i t", t=128),
                  ALU.subtract)
            for k in range(7):
                ctx.wait(T, "v", ctx.cnt["v"])
                tap = {}
                for ip in range(2):
                    tap[ip] = ctx.mark("t", T.matmul(
                        PB[3][:, ip * 128:(ip + 1) * 128], csc(ip, 3), csc(ip, 4),
                        start=True, stop=True))
                tsq = {}
                if k < 6:
                    for ip in range(2):
                        ctx.mark("t", T.matmul(
                            PB[2][:, ip * 256:ip * 256 + 128], csc(ip, 3), csc(ip, 2),
                            start=True, stop=True))
                        tsq[ip] = ctx.mark("t", T.matmul(
                            PB[2][:, ip * 256 + 128:ip * 256 + 256], csc(ip, 2),
                            csc(ip, 3), start=True, stop=True))
                ctx.wait(V, "t", max(tap.values()))
                ctx.v("tensor_tensor", Ublk, Ublk, PB[3][:, 0:256], ALU.add)
                if k < 6:
                    ctx.wait(V, "t", max(tsq.values()))
                    ctx.v("tensor_copy", ABblk, PB[2][:, 0:512])
            for ip in range(2):
                hr = head * 2 + ip
                b_col = gtm[:, (cc * 2 + head) * 32 + ip:(cc * 2 + head) * 32 + ip + 1]
                ctx.wait(T, "v", ctx.cnt["v"])
                ctx.mark("t", T.matmul(PB[4][:, ip * 128:(ip + 1) * 128],
                                       slab(ip, O_QTL)[:, sl],
                                       B.Sb_t[:, hr * 128:(hr + 1) * 128],
                                       start=True, stop=False))
                tgu = ctx.mark("t", T.matmul(PB[4][:, ip * 128:(ip + 1) * 128],
                                             csc(ip, 1), csc(ip, 4),
                                             start=False, stop=True))
                ctx.v("tensor_scalar", csc(ip, 5), csc(ip, 4), b_col, None, ALU.mult)
                tpose(csc(ip, 6), slab(ip, O_ETL)[:, sl])
                ctx.wait(T, "v", ctx.cnt["v"])
                tsu = ctx.mark("t", T.matmul(PB[4][:, 256 + ip * 128:384 + ip * 128],
                                             csc(ip, 6), csc(ip, 5),
                                             start=True, stop=True))
                ctx.wait(S, "v", v_pa)
                slc = ctx.s("activation", B.lamc_t[:, hr:hr + 1],
                            PA[hr][:, cc * 130 + 128:cc * 130 + 129], AF.Exp,
                            scale=gc)
                ctx.wait(V, "s", slc)
                ctx.wait(V, "t", tsu)
                ctx.v("scalar_tensor_tensor", B.Sf_t[:, hr * 128:(hr + 1) * 128],
                      B.Sf_t[:, hr * 128:(hr + 1) * 128], B.lamc_t[:, hr:hr + 1],
                      PB[4][:, 256 + ip * 128:384 + ip * 128], ALU.mult, ALU.add)
                ctx.v("tensor_copy", B.Sb_t[:, hr * 128:(hr + 1) * 128],
                      B.Sf_t[:, hr * 128:(hr + 1) * 128])
            lam_col = gtm[:, (cc * 2 + head) * 32 + 2:(cc * 2 + head) * 32 + 3]
            ctx.wait(V, "t", ctx.cnt["t"])
            ctx.v("tensor_scalar", ot1, PB[4][:, 0:128], lam_col, None, ALU.mult)
            ctx.v("tensor_copy", ot2, PB[4][:, 128:256])
            ctx.v("scalar_tensor_tensor", omx, ot2, lam_col, ot2,
                  ALU.mult, ALU.subtract)
            ctx.v("tensor_sub", omx, ot1, omx)
            ctx.wait(S, "v")
            ssq = ctx.s("activation", ojnk[:, 0:128], omx, AF.Square, accum_out=sqc)
            ctx.wait(V, "s", ssq)
            ctx.v("tensor_scalar", rcl, sqc, 1.0 / DV, NORM_EPS, ALU.mult, ALU.add)
            ctx.v("reciprocal", rcl, rcl)
            ctx.wait(S, "v")
            src2 = ctx.s("activation", rcl, rcl, AF.Sqrt)
            ctx.wait(V, "s", src2)
            v_on = ctx.v("tensor_scalar", selt, omx, rcl, None, ALU.mult)
            ctx.waitd(V, f"oa{head}")
            tpose(ofm[head][:, cc * 128:(cc + 1) * 128], selt, src_v=v_on)
        d = (w * 512) // TPC
        ctx.wait(G, "v", ctx.cnt["v"])
        ctx.dma(G, f"oa{head}",
                B.a2a_in[d * (HPC * DV) + head * 128:d * (HPC * DV) + (head + 1) * 128,
                         w * 512 - d * TPC:w * 512 - d * TPC + 512],
                ofm[head])


def _ph5_a2a(ctx, B):
    G = ctx.G
    for head in range(HPC):
        ctx.waitd(G, f"oa{head}")
    with allow_ici_all_to_all():
        B.c_a2a = ctx.mark("cc", G.collective_compute(
            "AllToAll", ALU.bypass, replica_groups=[list(range(NCORES))],
            ins=[B.a2a_in.ap().opt()], outs=[B.a2a_out.ap().opt()]))


def _ph6_epilogue(ctx, B):
    T, V, S, G, Y = ctx.T, ctx.V, ctx.S, ctx.G, ctx.Y
    NTT = B.NTT
    woc = B.arC[:, 0:8192]
    gfb = [B.arC[:, 8192:8704], B.arC[:, 8704:9216]]
    sgt = [B.arC[:, 9216:11264], B.arC[:, 11264:13312]]
    xb = [B.arB[:, 0:512], B.arB[:, 512:1024]]
    yt = B.arB[:, 1024:1536]
    yst = [B.arB[:, 1536:2048], B.arB[:, 2048:2560]]

    for e in (T, V, S, G, Y):
        ctx.wait(e, "cc", B.c_a2a)
    ctx.wait(Y, "t", B.c_scan[0])
    ctx.wait(Y, "v", B.c_scan[1])
    ctx.wait(Y, "s", B.c_scan[2])
    ctx.wait(T, "v", B.c_scan[1])
    ctx.wait(G, "t", B.c_scan[0])
    ctx.wait(G, "v", B.c_scan[1])
    for q in range(2):
        ctx.waitd(Y, f"sgs{q}")
        ctx.waitd(Y, f"gf{q}")

    v_of = 0
    t_half = 0
    pbfree = [0, 0]
    for h0 in range(0, NTT, 8):
        group = list(range(h0, min(h0 + 8, NTT)))
        for it in group:
            j = it - h0
            ofs = B.arA[:, j * 2048:(j + 1) * 2048]
            ctx.wait(Y, "t", t_half)
            ctx.wait(Y, "v", v_of)
            ctx.waitd(Y, f"ao{j % 2}")
            ctx.dma(Y, f"ao{j % 2}", ofs.rearrange("p (c t) -> p c t", t=128),
                    B.a2a_out[0:H * DV, it * 128:(it + 1) * 128]
                    .rearrange("(c p) t -> p c t", p=128))
            ctx.waitd(Y, f"so{j % 2}")
            ctx.dma(Y, f"so{j % 2}", sgt[j % 2].rearrange("p (c t) -> p c t", t=128),
                    B.sgate_d[0:D, it * 128:(it + 1) * 128]
                    .rearrange("(c p) t -> p c t", p=128))
            ctx.waitd(V, f"ao{j % 2}")
            ctx.waitd(V, f"so{j % 2}")
            ctx.wait(V, "t", t_half)
            v_of = ctx.v("scalar_tensor_tensor", ofs, ofs, B.onorm_t[:, 0:1],
                         sgt[j % 2], ALU.mult, ALU.mult)
        for oc in range(4):
            for q in range(4):
                ctx.waitd(G, f"wo{q % 2}")
                ctx.wait(G, "t", t_half)
                ctx.dma(G, f"wo{q % 2}",
                        woc[:, q * 2048:(q + 1) * 2048].rearrange("p (c o) -> p c o", o=512),
                        B.wo[q * 512:(q + 1) * 512, oc * 512:(oc + 1) * 512]
                        .rearrange("(c p) o -> p c o", p=128))
            for q in range(4):
                ctx.waitd(T, f"wo{q % 2}")
            ctx.wait(T, "v", v_of)
            for it in group:
                j = it - h0
                pb = [B.pb0, B.pb1][j % 2]
                ctx.wait(T, "v", pbfree[j % 2])
                tmk = 0
                for ch in range(16):
                    tmk = ctx.mark("t", T.matmul(
                        pb[:, 0:512], B.arA[:, (j * 16 + ch) * 128:(j * 16 + ch + 1) * 128],
                        woc[:, ch * 512:(ch + 1) * 512],
                        start=(ch == 0), stop=(ch == 15)))
                ctx.waitd(S, f"x{j % 2}e")
                ctx.waitd(S, f"gf{j % 2}e")
                ctx.wait(S, "v", pbfree[j % 2])
                ctx.dma(S, f"x{j % 2}e", xb[j % 2],
                        B.x_sl[it * 128:(it + 1) * 128, oc * 512:(oc + 1) * 512])
                ctx.dma(S, f"gf{j % 2}e", gfb[j % 2],
                        B.gffn_d[it * 128:(it + 1) * 128, oc * 512:(oc + 1) * 512])
                ctx.wait(V, "t", tmk)
                ctx.waitd(V, f"x{j % 2}e")
                ctx.waitd(V, f"gf{j % 2}e")
                ctx.waitd(V, f"ys{j % 2}")
                ctx.v("tensor_tensor", yt, pb[:, 0:512], gfb[j % 2], ALU.mult)
                pbfree[j % 2] = ctx.v("tensor_tensor", yst[j % 2], yt, xb[j % 2],
                                      ALU.add)
                ctx.wait(G, "v", pbfree[j % 2])
                ctx.dma(G, f"ys{j % 2}",
                        B.y_sl[it * 128:(it + 1) * 128, oc * 512:(oc + 1) * 512],
                        yst[j % 2])
        t_half = ctx.cnt["t"]
    ctx.waitd(G, "ys0")
    ctx.waitd(G, "ys1")
    ctx.waitd(Y, "ys0")
    ctx.waitd(Y, "ys1")


def _ph_stub_out(ctx, B):
    T, V, S, G, Y = ctx.T, ctx.V, ctx.S, ctx.G, ctx.Y
    for fs in range(4):
        ctx.waitd(G, f"fm{fs}")
    ctx.waitd(G, "agent")
    cc = ctx.mark("cc", G.collective_compute(
        "AllGather", ALU.bypass, replica_groups=[list(range(NCORES))],
        ins=[B.ag_in.ap().opt()], outs=[B.ag_out.ap().opt()]))
    ctx.wait(G, "cc", cc)
    ctx.wait(Y, "v", ctx.cnt["v"])
    ctx.wait(Y, "s", ctx.cnt["s"])
    for i in range(B.NTT):
        buf = B.arB[:, 0:2048]
        ctx.waitd(Y, "sty")
        ctx.dma(Y, "stx", buf, B.x_sl[i * 128:(i + 1) * 128, :])
        ctx.waitd(G, "stx")
        ctx.dma(G, "sty", B.y_sl[i * 128:(i + 1) * 128, :], buf)
    ctx.waitd(G, "sty")


_PROG_CACHE = {}


def _get_prog(T):
    if T not in _PROG_CACHE:
        _PROG_CACHE[T] = build_program(T)
    return _PROG_CACHE[T]


def _consts():
    i128b = np.eye(128, dtype=bfdt)
    i128f = np.eye(128, dtype=np.float32)
    jj, ii = np.meshgrid(np.arange(128), np.arange(128), indexing="ij")
    trilS = (jj < ii).astype(bfdt)
    trilI = (jj <= ii).astype(bfdt)
    trilIf = (jj <= ii).astype(np.float32)
    bd8m = np.tile(((jj // 8 == ii // 8) & (ii >= jj)).astype(np.uint8), (1, 2))
    rep32 = np.zeros((32, 128), dtype=np.float32)
    for k in range(4):
        rep32[np.arange(32), 32 * k + np.arange(32)] = 1
    return {
        "identc": i128b, "identf": i128f, "trilS": trilS, "trilI": trilI,
        "rep32": rep32, "ones1": np.ones((1, 128), dtype=bfdt),
        "ones1f": np.ones((1, 128), dtype=np.float32),
        "onescf": np.ones((128, 1), dtype=np.float32),
        "trilIf": trilIf, "bd8m": bd8m,
        "onescb": np.ones((128, 1), dtype=bfdt),
        "onesff": np.ones((128, 128), dtype=np.float32),
        "sel96f": (np.arange(128)[:, None] == 96).astype(np.float32) * np.ones((1, 128), np.float32),
    }


def _blockdiag(wm, n):
    r, c = wm.shape
    out = np.zeros((128, 128), dtype=wm.dtype)
    for q in range(n):
        out[32 * q:32 * q + r, 32 * q:32 * q + c] = wm
    return out


def _in_maps(g, T, i):
    TPC = T // NCORES
    hs = slice(i * HPC * DK, (i + 1) * HPC * DK)
    m = {
        "x_sl": np.ascontiguousarray(g["L0_attn_hidden_states"][0, i * TPC:(i + 1) * TPC]),
        "wq_s": np.ascontiguousarray(g["L0_attn_wq"][:, hs]),
        "wk_s": np.ascontiguousarray(g["L0_attn_wk"][:, hs]),
        "wv_s": np.ascontiguousarray(g["L0_attn_wv"][:, hs]),
        "cq_s": np.ascontiguousarray(g["L0_attn_conv_q_w"][hs]),
        "ck_s": np.ascontiguousarray(g["L0_attn_conv_k_w"][hs]),
        "cv_s": np.ascontiguousarray(g["L0_attn_conv_v_w"][hs]),
        "alog_s": np.ascontiguousarray(g["L0_attn_A_log"][None, i * HPC:(i + 1) * HPC]),
        "dtb_s": np.ascontiguousarray(g["L0_attn_dt_bias"][hs].reshape(HPC, 128).T),
        "proxy": g["L0_attn_proxy_w"],
        "unc": g["L0_attn_unc_w"],
        "he_s": np.ascontiguousarray(
            g["L0_attn_head_embed"][i * HPC:(i + 1) * HPC].reshape(1, HPC * 4)),
        "mlp1": _blockdiag(g["L0_attn_mlp_w1"], 4),
        "mlpb1": np.ascontiguousarray(g["L0_attn_mlp_b1"][:, None]),
        "mlp2": _blockdiag(g["L0_attn_mlp_w2"], 4),
        "mlpb2": np.ascontiguousarray(g["L0_attn_mlp_b2"][:, None]),
        "gw5": np.ascontiguousarray(np.stack(
            [g["L0_attn_bb_w"], g["L0_attn_bd_w"], g["L0_attn_lam_w"],
             g["L0_attn_ab_w"], g["L0_attn_ad_w"]], axis=1)),
        "gb5": np.ascontiguousarray(np.concatenate(
            [g["L0_attn_bb_b"], g["L0_attn_bd_b"], g["L0_attn_lam_b"],
             g["L0_attn_ab_b"], g["L0_attn_ad_b"]])[None, :]),
        "g1w": g["L0_attn_g1_w"],
        "g2w": g["L0_attn_g2_w"],
        "g2b": np.ascontiguousarray(g["L0_attn_g2_b"].reshape(-1, 128).T),
        "onorm": np.ascontiguousarray(g["L0_attn_onorm_w"][:, None]),
        "wo": g["L0_attn_wo"],
        "gateT": np.ascontiguousarray(g["L0_ffn_gate_w"].T),
        "downT": np.ascontiguousarray(g["L0_ffn_down_w"].T),
        "gnorm": np.ascontiguousarray(g["L0_ffn_gate_norm_w"][None, :]),
        "dnorm": np.ascontiguousarray(g["L0_ffn_down_norm_w"][None, :]),
    }
    m.update(_consts())
    return m


def kernel(**inputs):
    g = {k: np.asarray(v) for k, v in inputs.items()}
    T = g["L0_attn_hidden_states"].shape[1]
    nc = _get_prog(T)
    in_maps = [_in_maps(g, T, i) for i in range(NCORES)]
    res = run_bass_kernel_spmd(nc, in_maps, core_ids=list(range(NCORES)))
    out = np.concatenate([res.results[i]["y_sl"] for i in range(NCORES)], axis=0)
    return out[None].astype(np.float32)
